# version 10
# baseline (speedup 1.0000x reference)
import contextlib
import math

import numpy as np
import ml_dtypes

import concourse.bass as bass
import concourse.mybir as mybir
from concourse.bass import IndirectOffsetOnAxis
from concourse.bass_utils import run_bass_kernel_spmd

F32 = mybir.dt.float32
BF16 = mybir.dt.bfloat16
U32 = mybir.dt.uint32
AF = mybir.ActivationFunctionType
ALU = mybir.AluOpType
AX = mybir.AxisListType

NCORES = 8
B, T, D = 8, 2048, 1024
NH, HD = 16, 64
P = 128
DC = D // P

ENGINES = ("sync", "tensor", "vector", "scalar", "gpsimd")
DEBUG_ATTNB = False
DEBUG_MOE = False
PARANOID = True
PARANOID_ENGS = {"vector", "scalar", "gpsimd"}


class Buf:
    __slots__ = ("name", "writer", "readers")

    def __init__(self, name):
        self.name = name
        self.writer = None
        self.readers = {}


class Prog:

    def __init__(self, name="prog"):
        self.nc = bass.Bass("TRN2", target_bir_lowering=False, num_devices=8)
        self.name = name
        self.ops = {e: [] for e in ENGINES}
        self.tick = {e: 0 for e in ENGINES}
        self.pending = {e: False for e in ENGINES}
        for e in ENGINES:
            self.tick[e + "_dma"] = 0
        self.waited = {}
        self.sems = {}
        self.nbuf = 0

    def buf(self, name=None):
        self.nbuf += 1
        return Buf(name or f"b{self.nbuf}")

    def _force_tick(self, eng):
        if not self.pending[eng]:
            return
        self.tick[eng] += 1
        self.pending[eng] = False
        self.ops[eng].append(("inst", lambda s: s.nop(), eng, None))

    def _require(self, prod_key, value):
        if prod_key.endswith("_dma"):
            return
        if value > self.tick[prod_key]:
            self._force_tick(prod_key)
            assert value <= self.tick[prod_key], (prod_key, value, self.tick[prod_key])

    def _wait(self, eng, prod_key, value):
        if prod_key == eng:
            return
        if prod_key.endswith("_dma"):
            value = self.tick[prod_key]
        self._require(prod_key, value)
        key = (eng, prod_key)
        if self.waited.get(key, 0) >= value:
            return
        self.waited[key] = value
        self.ops[eng].append(("wait", prod_key, value))

    def emit(self, eng, f, reads=(), writes=(), tick=None, dma=False):
        if tick is None:
            tick = eng != "tensor"
        if (PARANOID or getattr(self, "paranoid_now", False)) and not dma \
                and eng in PARANOID_ENGS:
            tick = True
            if self.tick[eng] > 0:
                self.ops[eng].append(("wait", eng, self.tick[eng]))
        for b in reads:
            if b.writer is not None:
                self._wait(eng, *b.writer)
        for b in writes:
            if b.writer is not None:
                self._wait(eng, *b.writer)
            for pk, v in list(b.readers.items()):
                self._wait(eng, pk, v)
        if dma:
            prod = eng + "_dma"
            self.tick[prod] += 16
            myv = self.tick[prod]
            self.ops[eng].append(("dma", f, prod, myv))
            my_key = (prod, myv)
        elif tick:
            self.tick[eng] += 1
            self.pending[eng] = False
            myv = self.tick[eng]
            self.ops[eng].append(("inst", f, eng, None))
            my_key = (eng, myv)
        else:
            self.pending[eng] = True
            self.ops[eng].append(("inst", f, None, None))
            my_key = (eng, self.tick[eng] + 1)
        for b in reads:
            prev = b.readers.get(my_key[0], 0)
            b.readers[my_key[0]] = max(prev, my_key[1])
        for b in writes:
            b.writer = my_key
            b.readers = {}
        return my_key

    def barrier(self):
        for e in ENGINES:
            self._force_tick(e)
        for e in ENGINES:
            if e == "sync":
                continue
            if self.tick[e] > 0:
                self.ops["sync"].append(("wait", e, self.tick[e]))
                self.waited[("sync", e)] = self.tick[e]
            ed = e + "_dma"
            if self.tick[ed] > 0:
                self.ops["sync"].append(("wait", ed, self.tick[ed]))
                self.waited[("sync", ed)] = self.tick[ed]
        if self.tick["sync_dma"] > 0:
            self.ops["sync"].append(("wait", "sync_dma", self.tick["sync_dma"]))
            self.waited[("sync", "sync_dma")] = self.tick["sync_dma"]
        self.tick["sync"] += 1
        bar = self.tick["sync"]
        self.ops["sync"].append(("inst", lambda s: s.nop(), "sync", None))
        for e in ENGINES:
            if e == "sync":
                continue
            self.ops[e].append(("wait", "sync", bar))
            self.waited[(e, "sync")] = bar
        return bar

    def build(self, ctx):
        nc = self.nc
        for e in ENGINES:
            self.sems[e] = ctx.enter_context(nc.semaphore("sm_" + e))
            self.sems[e + "_dma"] = ctx.enter_context(nc.semaphore("smd_" + e))
        block = ctx.enter_context(nc.Block())
        prog = self

        def mk_section(eng):
            def section(seq):
                for op in prog.ops[eng]:
                    if op[0] == "wait":
                        seq.wait_ge(prog.sems[op[1]], op[2])
                    elif op[0] == "inst":
                        inst = op[1](seq)
                        if op[2] is not None:
                            inst.then_inc(prog.sems[op[2]], 1)
                    else:
                        op[1](seq).then_inc(prog.sems[op[2]], 16)
            return section

        sec = {e: mk_section(e) for e in ENGINES}

        @block.sync
        def _(sync):
            sec["sync"](sync)

        @block.tensor
        def _(tensor):
            sec["tensor"](tensor)

        @block.vector
        def _(vector):
            sec["vector"](vector)

        @block.scalar
        def _(scalar):
            sec["scalar"](scalar)

        @block.gpsimd
        def _(gpsimd):
            sec["gpsimd"](gpsimd)


class TB:
    __slots__ = ("t", "b")

    def __init__(self, t, b):
        self.t = t
        self.b = b


class Core:

    def __init__(self, p: Prog, ctx):
        self.p = p
        self.nc = p.nc
        self.ctx = ctx
        self.sb_bytes = 0
        self.psum_banks = 0

    prefix = ""

    def sbuf(self, name, shape, dtype):
        t = self.ctx.enter_context(self.nc.sbuf_tensor(self.prefix + name, shape, dtype))
        bpe = 2 if dtype in (BF16, mybir.dt.float16) else 4
        self.sb_bytes += int(np.prod(shape[1:])) * bpe
        return TB(t, self.p.buf(name))

    def psum(self, name, shape, dtype=F32):
        t = self.ctx.enter_context(self.nc.psum_tensor(self.prefix + name, shape, dtype))
        bpe = 2 if dtype == BF16 else 4
        self.psum_banks += math.ceil(int(np.prod(shape[1:])) * bpe / 2048)
        return TB(t, self.p.buf(name))

    def dram(self, name, shape, dtype, kind=None):
        if kind:
            t = self.nc.dram_tensor(name, shape, dtype, kind=kind)
        else:
            t = self.nc.dram_tensor(name, shape, dtype)
        return TB(t.ap(), self.p.buf(name))

    def dram_i(self, name, shape, dtype):
        t = self.nc.dram_tensor(self.prefix + name, shape, dtype)
        return TB(t.ap(), self.p.buf(self.prefix + name))



def emit_ln(c, sc, x_dram, xn, n_tok=T, eps=1e-5, gamma=None, beta=None,
            keep_rows=None):
    p = c.p
    xa, xa_b = sc["xchunk"].t, sc["xchunk"].b
    cast, sq = sc["cast"], sc["sq"]
    ones_bf, ones_f = sc["ones_bf"], sc["ones_f32"]
    stm_ps, sts_ps = sc["statm_ps"], sc["stats_ps"]
    bc_a, bc_b = sc["bc_a"], sc["bc_b"]
    rows, rtmp, tmp = sc["rows"], sc["rtmp"], sc["tmp32"]
    NU = n_tok // 512
    xbufs = sc["xchunk_bufs"]
    for u in range(NU):
        for ch in range(DC):
            par = ch % 2
            xs = xa[:, par * 512:(par + 1) * 512]
            xb = xbufs[par]
            p.emit("sync", lambda s, ch=ch, u=u, xs=xs: s.dma_start(
                out=xs, in_=x_dram.t[ch * P:(ch + 1) * P, u * 512:(u + 1) * 512]),
                reads=[x_dram.b], writes=[xb], dma=True)
            p.emit("vector", lambda s, xs=xs: s.tensor_copy(cast.t[:], xs),
                   reads=[xb], writes=[cast.b])
            p.emit("scalar", lambda s, xs=xs: s.activation(sq.t[:], xs, AF.Square),
                   reads=[xb], writes=[sq.b])
            p.emit("tensor", lambda s, ch=ch: s.matmul(
                stm_ps.t[0:1, 0:512], ones_bf.t[:], cast.t[:],
                start=(ch == 0), stop=(ch == DC - 1)),
                reads=[cast.b, ones_bf.b], writes=[stm_ps.b], tick=(ch == DC - 1))
            p.emit("tensor", lambda s, ch=ch: s.matmul(
                sts_ps.t[0:1, 0:512], ones_bf.t[:], sq.t[:],
                start=(ch == 0), stop=(ch == DC - 1)),
                reads=[sq.b, ones_bf.b], writes=[sts_ps.b], tick=(ch == DC - 1))
        sl = slice(0, 512)
        slb = slice(512, 1024)
        usl = slice(u * 512, (u + 1) * 512)
        p.emit("vector", lambda s, slb=slb: s.tensor_scalar_mul(
            rows.t[0:1, slb], stm_ps.t[0:1, 0:512], 1.0 / D),
            reads=[stm_ps.b], writes=[rows.b])
        p.emit("vector", lambda s, sl=sl: s.tensor_scalar_mul(
            rows.t[0:1, sl], sts_ps.t[0:1, 0:512], 1.0 / D),
            reads=[sts_ps.b], writes=[rows.b])
        p.emit("vector", lambda s, slb=slb: s.tensor_tensor(
            rtmp.t[0:1, 0:512], rows.t[0:1, slb], rows.t[0:1, slb], ALU.mult),
            reads=[rows.b], writes=[rtmp.b])
        p.emit("vector", lambda s, sl=sl: s.tensor_tensor(
            rows.t[0:1, sl], rows.t[0:1, sl], rtmp.t[0:1, 0:512], ALU.subtract),
            reads=[rows.b, rtmp.b], writes=[rows.b])
        p.emit("scalar", lambda s, sl=sl: s.activation(
            rows.t[0:1, sl], rows.t[0:1, sl], AF.Sqrt, bias=sc["eps"].t[0:1, 0:1]),
            reads=[rows.b, sc["eps"].b], writes=[rows.b])
        p.emit("vector", lambda s, sl=sl: s.reciprocal(rows.t[0:1, sl], rows.t[0:1, sl]),
               reads=[rows.b], writes=[rows.b])
        p.emit("vector", lambda s, sl=sl, slb=slb: s.tensor_tensor(
            rows.t[0:1, slb], rows.t[0:1, slb], rows.t[0:1, sl], ALU.mult),
            reads=[rows.b], writes=[rows.b])
        p.emit("vector", lambda s, slb=slb: s.tensor_scalar_mul(
            rows.t[0:1, slb], rows.t[0:1, slb], -1.0), reads=[rows.b], writes=[rows.b])
        if keep_rows is not None:
            p.emit("vector", lambda s, usl=usl: s.tensor_copy(
                keep_rows.t[0:1, usl], rows.t[0:1, 0:512]),
                reads=[rows.b], writes=[keep_rows.b])
            p.emit("vector", lambda s, usl=usl: s.tensor_copy(
                keep_rows.t[0:1, n_tok + usl.start:n_tok + usl.stop], rows.t[0:1, 512:1024]),
                reads=[rows.b], writes=[keep_rows.b])
        p.emit("tensor", lambda s, sl=sl: s.matmul(
            bc_a.t[:], ones_f.t[:], rows.t[0:1, sl], start=True, stop=True),
            reads=[rows.b, ones_f.b], writes=[bc_a.b])
        p.emit("tensor", lambda s, slb=slb: s.matmul(
            bc_b.t[:], ones_f.t[:], rows.t[0:1, slb], start=True, stop=True),
            reads=[rows.b, ones_f.b], writes=[bc_b.b], tick=True)
        for ch in range(DC):
            par = ch % 2
            xs = xa[:, par * 512:(par + 1) * 512]
            xb = xbufs[par]
            xo = xn.t[:, ch * n_tok + u * 512: ch * n_tok + u * 512 + 512]
            p.emit("sync", lambda s, ch=ch, u=u, xs=xs: s.dma_start(
                out=xs, in_=x_dram.t[ch * P:(ch + 1) * P, u * 512:(u + 1) * 512]),
                reads=[x_dram.b], writes=[xb], dma=True)
            p.emit("vector", lambda s, xs=xs: s.tensor_tensor(
                tmp.t[:, 0:512], xs, bc_a.t[:], ALU.mult),
                reads=[xb, bc_a.b], writes=[tmp.b])
            if gamma is None:
                p.emit("vector", lambda s, xo=xo: s.tensor_tensor(
                    xo, tmp.t[:, 0:512], bc_b.t[:], ALU.add),
                    reads=[tmp.b, bc_b.b], writes=[xn.b])
            else:
                p.emit("vector", lambda s: s.tensor_tensor(
                    tmp.t[:, 0:512], tmp.t[:, 0:512], bc_b.t[:], ALU.add),
                    reads=[tmp.b, bc_b.b], writes=[tmp.b])
                p.emit("vector", lambda s, xo=xo, ch=ch: s.tensor_scalar(
                    xo, tmp.t[:, 0:512], gamma.t[:, ch:ch + 1], beta.t[:, ch:ch + 1],
                    ALU.mult, ALU.add),
                    reads=[tmp.b, gamma.b, beta.b], writes=[xn.b])


def emit_ln_double(c, sc, x_dram, xn, gamma, beta, keep_rows, n_tok=T, eps=1e-5):
    p = c.p
    xa = sc["xchunk"].t
    xbufs = sc["xchunk_bufs"]
    cast, sq = sc["cast"], sc["sq"]
    ones_bf, ones_f = sc["ones_bf"], sc["ones_f32"]
    stm_ps, sts_ps = sc["statm_ps"], sc["stats_ps"]
    bc_a, bc_b = sc["bc_a"], sc["bc_b"]
    rows, rtmp, rtmp2, tmp = sc["rows"], sc["rtmp"], sc["rtmp2"], sc["tmp32"]
    NU = n_tok // 512
    for u in range(NU):
        for ch in range(DC):
            par = ch % 2
            xs = xa[:, par * 512:(par + 1) * 512]
            xb = xbufs[par]
            p.emit("sync", lambda s, ch=ch, u=u, xs=xs: s.dma_start(
                out=xs, in_=x_dram.t[ch * P:(ch + 1) * P, u * 512:(u + 1) * 512]),
                reads=[x_dram.b], writes=[xb], dma=True)
            p.emit("vector", lambda s, xs=xs: s.tensor_copy(cast.t[:], xs),
                   reads=[xb], writes=[cast.b])
            p.emit("scalar", lambda s, xs=xs: s.activation(sq.t[:], xs, AF.Square),
                   reads=[xb], writes=[sq.b])
            p.emit("tensor", lambda s, ch=ch: s.matmul(
                stm_ps.t[0:1, 0:512], ones_bf.t[:], cast.t[:],
                start=(ch == 0), stop=(ch == DC - 1)),
                reads=[cast.b, ones_bf.b], writes=[stm_ps.b], tick=(ch == DC - 1))
            p.emit("tensor", lambda s, ch=ch: s.matmul(
                sts_ps.t[0:1, 0:512], ones_bf.t[:], sq.t[:],
                start=(ch == 0), stop=(ch == DC - 1)),
                reads=[sq.b, ones_bf.b], writes=[sts_ps.b], tick=(ch == DC - 1))
        sl = slice(0, 512)
        slb = slice(512, 1024)
        usl = slice(u * 512, (u + 1) * 512)
        p.emit("vector", lambda s: s.tensor_scalar_mul(
            rows.t[0:1, slb], stm_ps.t[0:1, 0:512], 1.0 / D),
            reads=[stm_ps.b], writes=[rows.b])
        p.emit("vector", lambda s: s.tensor_scalar_mul(
            rows.t[0:1, sl], sts_ps.t[0:1, 0:512], 1.0 / D),
            reads=[sts_ps.b], writes=[rows.b])
        p.emit("vector", lambda s: s.tensor_tensor(
            rtmp.t[0:1, 0:512], rows.t[0:1, slb], rows.t[0:1, slb], ALU.mult),
            reads=[rows.b], writes=[rtmp.b])
        p.emit("vector", lambda s: s.tensor_tensor(
            rows.t[0:1, sl], rows.t[0:1, sl], rtmp.t[0:1, 0:512], ALU.subtract),
            reads=[rows.b, rtmp.b], writes=[rows.b])
        p.emit("vector", lambda s: s.tensor_copy(rtmp2.t[0:1, :], rows.t[0:1, sl]),
               reads=[rows.b], writes=[rtmp2.b])
        p.emit("scalar", lambda s: s.activation(
            rows.t[0:1, sl], rows.t[0:1, sl], AF.Sqrt, bias=sc["eps"].t[0:1, 0:1]),
            reads=[rows.b, sc["eps"].b], writes=[rows.b])
        p.emit("vector", lambda s: s.reciprocal(rows.t[0:1, sl], rows.t[0:1, sl]),
               reads=[rows.b], writes=[rows.b])
        p.emit("vector", lambda s: s.tensor_tensor(
            rows.t[0:1, slb], rows.t[0:1, slb], rows.t[0:1, sl], ALU.mult),
            reads=[rows.b], writes=[rows.b])
        p.emit("vector", lambda s: s.tensor_scalar_mul(
            rows.t[0:1, slb], rows.t[0:1, slb], -1.0), reads=[rows.b], writes=[rows.b])
        p.emit("vector", lambda s, usl=usl: s.tensor_copy(
            keep_rows.t[0:1, usl], rows.t[0:1, sl]),
            reads=[rows.b], writes=[keep_rows.b])
        p.emit("vector", lambda s, usl=usl: s.tensor_copy(
            keep_rows.t[0:1, n_tok + usl.start:n_tok + usl.stop], rows.t[0:1, slb]),
            reads=[rows.b], writes=[keep_rows.b])
        p.emit("vector", lambda s: s.tensor_tensor(
            rtmp.t[0:1, 0:512], rows.t[0:1, sl], rows.t[0:1, sl], ALU.mult),
            reads=[rows.b], writes=[rtmp.b])
        p.emit("vector", lambda s: s.tensor_tensor(
            rtmp2.t[0:1, :], rtmp2.t[0:1, :], rtmp.t[0:1, 0:512], ALU.mult),
            reads=[rtmp2.b, rtmp.b], writes=[rtmp2.b])
        p.emit("scalar", lambda s: s.activation(
            rtmp2.t[0:1, :], rtmp2.t[0:1, :], AF.Sqrt, bias=sc["eps"].t[0:1, 0:1]),
            reads=[rtmp2.b, sc["eps"].b], writes=[rtmp2.b])
        p.emit("vector", lambda s: s.reciprocal(rtmp2.t[0:1, :], rtmp2.t[0:1, :]),
               reads=[rtmp2.b], writes=[rtmp2.b])
        p.emit("vector", lambda s: s.tensor_tensor(
            rows.t[0:1, sl], rows.t[0:1, sl], rtmp2.t[0:1, :], ALU.mult),
            reads=[rows.b, rtmp2.b], writes=[rows.b])
        p.emit("vector", lambda s: s.tensor_tensor(
            rows.t[0:1, slb], rows.t[0:1, slb], rtmp2.t[0:1, :], ALU.mult),
            reads=[rows.b, rtmp2.b], writes=[rows.b])
        p.emit("tensor", lambda s: s.matmul(
            bc_a.t[:], ones_f.t[:], rows.t[0:1, sl], start=True, stop=True),
            reads=[rows.b, ones_f.b], writes=[bc_a.b])
        p.emit("tensor", lambda s: s.matmul(
            bc_b.t[:], ones_f.t[:], rows.t[0:1, slb], start=True, stop=True),
            reads=[rows.b, ones_f.b], writes=[bc_b.b], tick=True)
        for ch in range(DC):
            par = ch % 2
            xs = xa[:, par * 512:(par + 1) * 512]
            xb = xbufs[par]
            xo = xn.t[:, ch * n_tok + u * 512: ch * n_tok + u * 512 + 512]
            p.emit("sync", lambda s, ch=ch, u=u, xs=xs: s.dma_start(
                out=xs, in_=x_dram.t[ch * P:(ch + 1) * P, u * 512:(u + 1) * 512]),
                reads=[x_dram.b], writes=[xb], dma=True)
            p.emit("vector", lambda s, xs=xs, xb=xb: s.tensor_tensor(
                tmp.t[:, 0:512], xs, bc_a.t[:], ALU.mult),
                reads=[xb, bc_a.b], writes=[tmp.b])
            p.emit("vector", lambda s: s.tensor_tensor(
                tmp.t[:, 0:512], tmp.t[:, 0:512], bc_b.t[:], ALU.add),
                reads=[tmp.b, bc_b.b], writes=[tmp.b])
            p.emit("vector", lambda s, xo=xo, ch=ch: s.tensor_scalar(
                xo, tmp.t[:, 0:512], gamma.t[:, ch:ch + 1], beta.t[:, ch:ch + 1],
                ALU.mult, ALU.add),
                reads=[tmp.b, gamma.b, beta.b], writes=[xn.b])


def make_ln_scratch(c, n_tok=T, ps_share=None):
    sc = {}
    xch = c.sbuf("ln_xchunk", [P, 1024], F32)
    sc["xchunk"] = xch
    sc["xchunk_bufs"] = [c.p.buf("ln_xc0"), c.p.buf("ln_xc1")]
    sc["cast"] = c.sbuf("ln_cast", [P, 512], BF16)
    sc["sq"] = c.sbuf("ln_sq", [P, 512], BF16)
    sc["ones_bf"] = c.sbuf("ln_ones_bf", [P, 1], BF16)
    sc["ones_f32"] = c.sbuf("ln_ones_f", [1, P], F32)
    sc["rows"] = c.sbuf("ln_rows", [1, 1024], F32)
    sc["rtmp"] = c.sbuf("ln_rtmp", [1, 512], F32)
    sc["eps"] = c.sbuf("ln_eps", [1, 1], F32)
    sc["rtmp2"] = c.sbuf("ln_rtmp2", [1, 512], F32)
    sc["tmp32"] = c.sbuf("ln_tmp", [P, 512], F32)
    if ps_share is None:
        sc["statm_ps"] = c.psum("ln_statm_ps", [P, 512], F32)
        sc["stats_ps"] = c.psum("ln_stats_ps", [P, 512], F32)
        bc = c.psum("ln_bc_ps", [P, 1024], F32)
        sc["bc_a"] = TB(bc.t[:, 0:512], bc.b)
        sc["bc_b"] = TB(bc.t[:, 512:1024], bc.b)
    else:
        sc["statm_ps"] = ps_share["statm_ps"]
        sc["stats_ps"] = ps_share["stats_ps"]
        if "bcast_ps" in ps_share:
            bc = ps_share["bcast_ps"]
            sc["bc_a"] = TB(bc.t[:, 0:512], bc.b)
            sc["bc_b"] = TB(bc.t[:, 512:1024], bc.b)
        else:
            sc["bc_a"] = ps_share["bc_a"]
            sc["bc_b"] = ps_share["bc_b"]
    return sc


def init_ln_consts(c, sc):
    p = c.p
    p.emit("vector", lambda s: s.memset(sc["ones_bf"].t[:], 1.0),
           writes=[sc["ones_bf"].b])
    p.emit("vector", lambda s: s.memset(sc["ones_f32"].t[:], 1.0),
           writes=[sc["ones_f32"].b])
    p.emit("vector", lambda s: s.memset(sc["eps"].t[:], 1e-5),
           writes=[sc["eps"].b])



def to_fm(x):
    return np.ascontiguousarray(x.T)


def from_fm(y):
    return np.ascontiguousarray(y.T)


def decl_ffn0(c, npfx=""):
    return {
        "w1bf": c.dram(npfx + "w1bf", [32, P, DC * P], BF16),
        "w2bf": c.dram(npfx + "w2bf", [32, P, DC * P], BF16),
        "h1bf": c.dram(npfx + "h1bf", [32, P, 32 * P], BF16),
        "h2bf": c.dram(npfx + "h2bf", [32, P, 32 * P], BF16),
        "w3bf": c.dram(npfx + "w3bf", [DC, P, 32 * P], BF16),
        "w1_d": c.dram(npfx + "w1T", [D, 4096], F32, kind="ExternalInput"),
        "w2_d": c.dram(npfx + "w2T", [D, 4096], F32, kind="ExternalInput"),
        "h1_d": c.dram(npfx + "h1T", [4096, 4096], F32, kind="ExternalInput"),
        "h2_d": c.dram(npfx + "h2T", [4096, 4096], F32, kind="ExternalInput"),
        "w3_d": c.dram(npfx + "w3T", [4096, D], F32, kind="ExternalInput"),
        "s1_d": c.dram(npfx + "w1s", [P, 32], F32, kind="ExternalInput"),
        "s2_d": c.dram(npfx + "w2s", [P, 32], F32, kind="ExternalInput"),
        "sh1_d": c.dram(npfx + "h1s", [P, 32], F32, kind="ExternalInput"),
        "sh2_d": c.dram(npfx + "h2s", [P, 32], F32, kind="ExternalInput"),
        "s3_d": c.dram(npfx + "w3s", [P, DC], F32, kind="ExternalInput"),
    }


def emit_ffn0_stage(c, d, x_d, xo_d, pfx="f_", first_use=True):
    p = c.p
    w1_d = d["w1_d"]
    w2_d = d["w2_d"]
    h1_d = d["h1_d"]
    h2_d = d["h2_d"]
    w3_d = d["w3_d"]
    s1_d = d["s1_d"]
    s2_d = d["s2_d"]
    sh1_d = d["sh1_d"]
    sh2_d = d["sh2_d"]
    s3_d = d["s3_d"]
    w1bf, w2bf, h1bf, h2bf, w3bf = d["w1bf"], d["w2bf"], d["h1bf"], d["h2bf"], d["w3bf"]
    old_pfx, old_ctx = c.prefix, c.ctx
    c.prefix = pfx
    with contextlib.ExitStack() as sctx:
        c.ctx = sctx
        sc = make_ln_scratch(c)
        xn = c.sbuf("xn", [P, DC * T], BF16)
        g1 = c.sbuf("g1", [P, 32 * 1024], BF16)
        hh = c.sbuf("hh", [P, 32 * 1024], BF16)
        wab = [c.sbuf(f"wa{i}", [P, DC * P], BF16) for i in range(2)]
        wbb = [c.sbuf(f"wb{i}", [P, DC * P], BF16) for i in range(2)]
        whb = [c.sbuf(f"wh{i}", [P, 32 * P], BF16) for i in range(2)]
        scol = c.sbuf("scol", [P, 32 * 4 + DC], F32)
        t1 = c.sbuf("t1", [P, 512], F32)
        t2 = c.sbuf("t2", [P, 512], F32)
        xrt = [t1, t2]
        xout = c.sbuf("xout", [P, 512], F32)
        psA = c.psum("psA", [P, 512], F32)
        psB = c.psum("psB", [P, 512], F32)

        init_ln_consts(c, sc)
        for i, sd in enumerate([s1_d, s2_d, sh1_d, sh2_d]):
            p.emit("sync", lambda s, i=i, sd=sd: s.dma_start(
                out=scol.t[:, i * 32:(i + 1) * 32], in_=sd.t),
                reads=[sd.b], writes=[scol.b], dma=True)
        p.emit("sync", lambda s: s.dma_start(
            out=scol.t[:, 128:128 + DC], in_=s3_d.t),
            reads=[s3_d.b], writes=[scol.b], dma=True)

        emit_ln(c, sc, x_d, xn)

        def wload(wb, fp_d, bf_d, fo, nk, tt):
            if first_use and tt == 0:
                p.emit("gpsimd", lambda s, fo=fo, wb=wb, fp_d=fp_d, nk=nk: s.dma_start(
                    out=wb.t[:, 0:nk * P].rearrange("p (k f) -> p k f", f=P),
                    in_=fp_d.t.rearrange("(k p) f -> p k f", p=P)[:, :, fo * P:(fo + 1) * P]),
                    reads=[fp_d.b], writes=[wb.b], dma=True)
                p.emit("sync", lambda s, fo=fo, wb=wb, bf_d=bf_d, nk=nk: s.dma_start(
                    out=bf_d.t[fo], in_=wb.t[:, 0:nk * P]),
                    reads=[wb.b], writes=[bf_d.b], dma=True)
            else:
                p.emit("gpsimd", lambda s, fo=fo, wb=wb, bf_d=bf_d, nk=nk: s.dma_start(
                    out=wb.t[:, 0:nk * P], in_=bf_d.t[fo]),
                    reads=[bf_d.b], writes=[wb.b], dma=True)

        NT = 2
        for tt in range(NT):
            ts0 = tt * 1024
            for fo in range(32):
                par = fo % 2
                wa, wb_ = wab[par], wbb[par]
                wload(wa, w1_d, w1bf, fo, DC, tt)
                wload(wb_, w2_d, w2bf, fo, DC, tt)
                for tu in range(2):
                    toff = ts0 + tu * 512
                    for k in range(DC):
                        p.emit("tensor", lambda s, k=k, toff=toff, wa=wa: s.matmul(
                            psA.t[:], wa.t[:, k * P:(k + 1) * P],
                            xn.t[:, k * T + toff:k * T + toff + 512],
                            start=(k == 0), stop=(k == DC - 1)),
                            reads=[wa.b, xn.b], writes=[psA.b], tick=(k == DC - 1))
                        p.emit("tensor", lambda s, k=k, toff=toff, wb_=wb_: s.matmul(
                            psB.t[:], wb_.t[:, k * P:(k + 1) * P],
                            xn.t[:, k * T + toff:k * T + toff + 512],
                            start=(k == 0), stop=(k == DC - 1)),
                            reads=[wb_.b, xn.b], writes=[psB.b], tick=(k == DC - 1))
                    p.emit("scalar", lambda s, fo=fo: s.activation(
                        t1.t[:], psA.t[:], AF.Silu, scale=scol.t[:, fo:fo + 1]),
                        reads=[psA.b, scol.b], writes=[t1.b])
                    p.emit("vector", lambda s, fo=fo: s.tensor_scalar_mul(
                        t2.t[:], psB.t[:], scol.t[:, 32 + fo:32 + fo + 1]),
                        reads=[psB.b, scol.b], writes=[t2.b])
                    p.emit("vector", lambda s, fo=fo, tu=tu: s.tensor_tensor(
                        g1.t[:, fo * 1024 + tu * 512:fo * 1024 + tu * 512 + 512],
                        t1.t[:], t2.t[:], ALU.mult),
                        reads=[t1.b, t2.b], writes=[g1.b])
            for stage, (wd, bfd, src_t, dst_t, s_off) in enumerate([
                    (h1_d, h1bf, g1, hh, 64), (h2_d, h2bf, hh, g1, 96)]):
                for fo in range(32):
                    par = fo % 2
                    wh = whb[par]
                    wload(wh, wd, bfd, fo, 32, tt)
                    for tu in range(2):
                        toff_l = tu * 512
                        for k in range(32):
                            p.emit("tensor", lambda s, k=k, wh=wh, src_t=src_t, toff_l=toff_l: s.matmul(
                                psA.t[:], wh.t[:, k * P:(k + 1) * P],
                                src_t.t[:, k * 1024 + toff_l:k * 1024 + toff_l + 512],
                                start=(k == 0), stop=(k == 31)),
                                reads=[wh.b, src_t.b], writes=[psA.b], tick=(k == 31))
                        p.emit("scalar", lambda s, fo=fo, dst_t=dst_t, s_off=s_off, tu=tu: s.activation(
                            dst_t.t[:, fo * 1024 + tu * 512:fo * 1024 + tu * 512 + 512],
                            psA.t[:], AF.Gelu,
                            scale=scol.t[:, s_off + fo:s_off + fo + 1]),
                            reads=[psA.b, scol.b], writes=[dst_t.b])
            for fo in range(DC):
                par = fo % 2
                wh = whb[par]
                wload(wh, w3_d, w3bf, fo, 32, tt)
                for tu in range(2):
                    toff = ts0 + tu * 512
                    par2 = tu % 2
                    xs = xrt[par2].t[:, 0:512]
                    p.emit("sync", lambda s, fo=fo, toff=toff, xs=xs: s.dma_start(
                        out=xs, in_=x_d.t[fo * P:(fo + 1) * P, toff:toff + 512]),
                        reads=[x_d.b], writes=[xrt[par2].b], dma=True)
                    for k in range(32):
                        p.emit("tensor", lambda s, k=k, wh=wh, tu=tu: s.matmul(
                            psA.t[:], wh.t[:, k * P:(k + 1) * P],
                            g1.t[:, k * 1024 + tu * 512:k * 1024 + tu * 512 + 512],
                            start=(k == 0), stop=(k == 31)),
                            reads=[wh.b, g1.b], writes=[psA.b], tick=(k == 31))
                    p.emit("vector", lambda s, fo=fo, xs=xs, par2=par2: s.scalar_tensor_tensor(
                        xout.t[:], psA.t[:], scol.t[:, 128 + fo:128 + fo + 1], xs,
                        ALU.mult, ALU.add),
                        reads=[psA.b, scol.b, xrt[par2].b], writes=[xout.b])
                    p.emit("sync", lambda s, fo=fo, toff=toff: s.dma_start(
                        out=xo_d.t[fo * P:(fo + 1) * P, toff:toff + 512], in_=xout.t[:]),
                        reads=[xout.b], writes=[xo_d.b], dma=True)
        p.barrier()
    c.prefix, c.ctx = old_pfx, old_ctx

def build_ffn0_prog():
    p = Prog("ffn0")
    nc = p.nc
    ctx = contextlib.ExitStack()
    c = Core(p, ctx)
    x_d = c.dram("x_fm", [D, T], F32, kind="ExternalInput")
    xo_d = c.dram("xo_fm", [D, T], F32, kind="ExternalOutput")
    d = decl_ffn0(c)
    with ctx:
        emit_ffn0_stage(c, d, x_d, xo_d, pfx="")
        p.emit("sync", lambda s: s.nop(), reads=[xo_d.b])
        p.build(ctx)
    return nc

def ffn0_host_inputs(x_batch_fm, ins, li="L0"):
    g = lambda k: np.asarray(ins[k], np.float32)
    return dict(
        x_fm=np.ascontiguousarray(x_batch_fm),
        w1T=np.ascontiguousarray(g(f"{li}_ffn_w1_w").T),
        w2T=np.ascontiguousarray(g(f"{li}_ffn_w2_w").T),
        h1T=np.ascontiguousarray(g(f"{li}_ffn_h1_w").T),
        h2T=np.ascontiguousarray(g(f"{li}_ffn_h2_w").T),
        w3T=np.ascontiguousarray(g(f"{li}_ffn_w3_w").T),
        w1s=np.ascontiguousarray(g(f"{li}_ffn_w1_s").reshape(32, P).T),
        w2s=np.ascontiguousarray(g(f"{li}_ffn_w2_s").reshape(32, P).T),
        h1s=np.ascontiguousarray(g(f"{li}_ffn_h1_s").reshape(32, P).T),
        h2s=np.ascontiguousarray(g(f"{li}_ffn_h2_s").reshape(32, P).T),
        w3s=np.ascontiguousarray(g(f"{li}_ffn_w3_s").reshape(DC, P).T),
    )


def emit_wproj_fm(c, w_d, wbufs, xn, out_emit, n_fo=DC, n_k=DC, xn_stride=T,
                  psP=None, q="gpsimd"):
    p = c.p
    for fo in range(n_fo):
        wb = wbufs[fo % len(wbufs)]
        p.emit(q, lambda s, fo=fo, wb=wb: s.dma_start(
            out=wb.t[:].rearrange("p (k f) -> p k f", f=P),
            in_=w_d.t.rearrange("(k p) f -> p k f", p=P)[:, :, fo * P:(fo + 1) * P]),
            reads=[w_d.b], writes=[wb.b], dma=True)
        for u in range(T // 512):
            for k in range(n_k):
                p.emit("tensor", lambda s, fo=fo, u=u, k=k, wb=wb: s.matmul(
                    psP.t[:], wb.t[:, k * P:(k + 1) * P],
                    xn.t[:, k * xn_stride + u * 512: k * xn_stride + u * 512 + 512],
                    start=(k == 0), stop=(k == n_k - 1)),
                    reads=[wb.b, xn.b], writes=[psP.b], tick=(k == n_k - 1))
            out_emit(fo, u, psP)




def emit_attn_heads(c, sh, qf, kf, vt, causal, n_heads=NH, kaug_prebuilt=False,
                    kf_single=False):
    p = c.p
    qaug2, kaug2 = sh["qaug2"], sh["kaug2"]
    ptile, ptb2 = sh["ptile"], sh["ptb2"]
    mcols_2, rrow, bcs = sh["mcols_2"], sh["rrow"], sh["bcs"]
    mcols2 = sh["mcols2"]
    nid = sh["nid"]
    msk = sh.get("msk")
    ones_bf, ones_f = sh["ones_bf"], sh["ones_f"]
    psSA, psSB, psRow, psO, psD, psBC = (sh["psSA"], sh["psSB"], sh["psRow"],
                                         sh["psO"], sh["psD"], sh["psBC"])
    p1banks = sh["p1banks"]

    def pass1_aug(h):
        hp, hi = h // 2, h % 2
        base = hi * 64
        par = h % 2
        qaug, kaug, mcols = qaug2[par], kaug2[par % len(kaug2)], mcols_2[par]
        if kf_single:
            p.emit("sync", lambda s, base=base, hp=hp, qaug=qaug: s.dma_start(
                out=qaug.t[0:64, :], in_=qf.t[base:base + 64, hp * T:(hp + 1) * T]),
                reads=[qf.b], writes=[qaug.b], dma=True)
        for q_ in range(16):
            nu = (q_ // 4 + 1) if causal else 4
            for su in range(nu):
                ps_ = p1banks[(q_ * 4 + su) % len(p1banks)]
                if kf_single:
                    p.emit("tensor", lambda s, q_=q_, su=su, ps_=ps_, qaug=qaug: s.matmul(
                        ps_.t[:], qaug.t[0:64, q_ * P:(q_ + 1) * P],
                        kf.t[0:64, su * 512:(su + 1) * 512],
                        start=True, stop=True),
                        reads=[qaug.b, kf.b], writes=[ps_.b], tick=True)
                else:
                    p.emit("tensor", lambda s, q_=q_, su=su, base=base, hp=hp, ps_=ps_: s.matmul(
                        ps_.t[:], qf.t[base:base + 64, hp * T + q_ * P: hp * T + (q_ + 1) * P],
                        kf.t[base:base + 64, hp * T + su * 512: hp * T + (su + 1) * 512],
                        start=True, stop=True,
                        tile_position=None),
                        reads=[qf.b, kf.b], writes=[ps_.b], tick=True)
                p.emit("vector", lambda s, q_=q_, su=su, ps_=ps_: s.reduce_max(
                    mcols2.t[:, q_ * 4 + su:q_ * 4 + su + 1], ps_.t[:], axis=AX.X),
                    reads=[ps_.b], writes=[mcols2.b])
            p.emit("vector", lambda s, q_=q_, nu=nu, mcols=mcols: s.reduce_max(
                mcols.t[:, q_:q_ + 1], mcols2.t[:, q_ * 4:q_ * 4 + nu], axis=AX.X),
                reads=[mcols2.b], writes=[mcols.b])
        if not kf_single:
            p.emit("sync", lambda s, base=base, hp=hp, qaug=qaug: s.dma_start(
                out=qaug.t[0:64, :], in_=qf.t[base:base + 64, hp * T:(hp + 1) * T]),
                reads=[qf.b], writes=[qaug.b], dma=True)
        if not kaug_prebuilt:
            p.emit("sync", lambda s, base=base, hp=hp, kaug=kaug: s.dma_start(
                out=kaug.t[0:64, :], in_=kf.t[base:base + 64, hp * T:(hp + 1) * T]),
                reads=[kf.b], writes=[kaug.b], dma=True)
            p.emit("vector", lambda s, kaug=kaug: s.memset(kaug.t[64:65, :], 1.0),
                   writes=[kaug.b])
        for u in range(4):
            for qq in range(4):
                q_ = u * 4 + qq
                p.emit("tensor", lambda s, q_=q_, qq=qq, mcols=mcols: s.matmul(
                    psRow.t[0:1, qq * P:(qq + 1) * P], mcols.t[:, q_:q_ + 1],
                    nid.t[:], start=(qq == 0), stop=(qq == 3), skip_group_check=True),
                    reads=[mcols.b, nid.b], writes=[psRow.b], tick=(qq == 3))
            p.emit("vector", lambda s, u=u, qaug=qaug: s.tensor_copy(
                qaug.t[64:65, u * 512:(u + 1) * 512], psRow.t[0:1, 0:512]),
                reads=[psRow.b], writes=[qaug.b])

    def pass2(h):
        hp, hi = h // 2, h % 2
        base = hi * 64
        par = h % 2
        qaug = qaug2[par]
        kaug = kaug2[par] if not kaug_prebuilt else kaug2[0]
        for u in range(4):
            nst = (4 * u + 4) if causal else 16
            first_pv = True
            for st_i in range(nst):
                pvpar = st_i % 2
                pt = ptile.t[:, pvpar * 512:(pvpar + 1) * 512]
                ptb = ptb2[pvpar]
                psSC = psSA if pvpar == 0 else psSB
                p.emit("tensor", lambda s, st_i=st_i, u=u, psSC=psSC, kaug=kaug, qaug=qaug: s.matmul(
                    psSC.t[:], kaug.t[0:65, st_i * P:(st_i + 1) * P],
                    qaug.t[0:65, u * 512:(u + 1) * 512],
                    start=True, stop=True),
                    reads=[kaug.b, qaug.b], writes=[psSC.b], tick=True)
                diag = causal and st_i >= 4 * u
                if diag:
                    p.emit("scalar", lambda s, psSC=psSC: s.activation(
                        sh["t1"].t[:], psSC.t[:], AF.Exp),
                        reads=[psSC.b], writes=[sh["t1"].b])
                    p.emit("vector", lambda s, pt=pt, st_i=st_i, u=u: s.tensor_tensor(
                        pt, sh["t1"].t[:],
                        msk.t[:, (st_i - 4 * u) * 512:(st_i - 4 * u + 1) * 512],
                        ALU.mult),
                        reads=[sh["t1"].b, msk.b], writes=[ptb])
                else:
                    p.emit("scalar", lambda s, pt=pt, psSC=psSC: s.activation(
                        pt, psSC.t[:], AF.Exp),
                        reads=[psSC.b], writes=[ptb])
                p.emit("tensor", lambda s, pt=pt, st_i=st_i, h=h, first_pv=first_pv,
                       nst=nst: s.matmul(
                    psO.t[0:65, :], vt.t[:, st_i * 1040 + h * 65: st_i * 1040 + h * 65 + 65],
                    pt, start=first_pv, stop=(st_i == nst - 1)),
                    reads=[vt.b, ptb], writes=[psO.b], tick=(st_i == nst - 1))
                first_pv = False
            p.emit("vector", lambda s: s.reciprocal(rrow.t[0:1, :], psO.t[64:65, :]),
                   reads=[psO.b], writes=[rrow.b])
            p.emit("tensor", lambda s: s.matmul(
                psBC.t[0:64, :], ones_f.t[0:1, 0:64], rrow.t[0:1, :], start=True, stop=True),
                reads=[ones_f.b, rrow.b], writes=[psBC.b], tick=True)
            p.emit("vector", lambda s: s.tensor_copy(bcs.t[0:64, :], psBC.t[0:64, :]),
                   reads=[psBC.b], writes=[bcs.b])
            p.emit("vector", lambda s, base=base, hp=hp, u=u: s.tensor_tensor(
                qf.t[base:base + 64, hp * T + u * 512: hp * T + (u + 1) * 512],
                psO.t[0:64, :], bcs.t[0:64, :], ALU.mult),
                reads=[psO.b, bcs.b], writes=[qf.b])

    for h in range(n_heads + 1):
        if h >= 1:
            pass2(h - 1)
        if h < n_heads:
            pass1_aug(h)

def decl_attn_b(c, npfx=""):
    return {
        "wq_d": c.dram(npfx + "WqT", [D, D], F32, kind="ExternalInput"),
        "wqs_d": c.dram(npfx + "WqsT", [D, D], F32, kind="ExternalInput"),
        "wk_d": c.dram(npfx + "WkT", [D, D], F32, kind="ExternalInput"),
        "wks_d": c.dram(npfx + "WksT", [D, D], F32, kind="ExternalInput"),
        "wv_d": c.dram(npfx + "WvT", [D, D], F32, kind="ExternalInput"),
        "wo_d": c.dram(npfx + "WoT", [D, D], F32, kind="ExternalInput"),
        "cq_d": c.dram(npfx + "cos8q", [P, T], F32, kind="ExternalInput"),
        "sq_d": c.dram(npfx + "sin8qs", [P, T], F32, kind="ExternalInput"),
        "ck_d": c.dram(npfx + "cosk", [P, T], F32, kind="ExternalInput"),
        "sk_d": c.dram(npfx + "sinks", [P, T], F32, kind="ExternalInput"),
        "msk_d": c.dram(npfx + "masks", [P, 4 * 512], F32, kind="ExternalInput"),
        "nid_d": c.dram(npfx + "negid8", [P, P], F32, kind="ExternalInput"),
    }


def emit_attn_b_stage(c, d, x_d, xo_d, pfx="s3_"):
    p = c.p
    wq_d = d["wq_d"]
    wqs_d = d["wqs_d"]
    wk_d = d["wk_d"]
    wks_d = d["wks_d"]
    wv_d = d["wv_d"]
    wo_d = d["wo_d"]
    cq_d = d["cq_d"]
    sq_d = d["sq_d"]
    ck_d = d["ck_d"]
    sk_d = d["sk_d"]
    msk_d = d["msk_d"]
    nid_d = d["nid_d"]
    old_pfx, old_ctx = c.prefix, c.ctx
    c.prefix = pfx
    with contextlib.ExitStack() as sctx:
        c.ctx = sctx
        ps_a1 = c.psum("ps_a1", [P, 512], F32)
        ps_a2 = c.psum("ps_a2", [P, 512], F32)
        ps_row = c.psum("ps_row", [P, 512], F32)
        ps_bc = c.psum("ps_bc", [P, 512], F32)
        sc = make_ln_scratch(c, ps_share=dict(
            statm_ps=ps_row, stats_ps=ps_bc, bc_a=ps_a1, bc_b=ps_a2))
        xn = c.sbuf("xn", [P, DC * T], BF16)
        qf = c.sbuf("qf", [P, DC * T], BF16)
        kf = c.sbuf("kf", [P, DC * T], BF16)
        vt = c.sbuf("vt", [P, 16 * 1040], BF16)
        p.emit("vector", lambda s: s.memset(
            vt.t[:].rearrange("p (t h f) -> p t h f", h=16, f=65)[:, :, :, 64:65], 1.0),
            writes=[vt.b])
        qaug2 = [c.sbuf("qaugA", [P, T], BF16), c.sbuf("qaugB", [P, T], BF16)]
        kaug2 = [c.sbuf("kaugA", [P, T], BF16), c.sbuf("kaugB", [P, T], BF16)]
        ptile = c.sbuf("ptile", [P, 2 * 512], BF16)
        ptb2 = [p.buf("pt0"), p.buf("pt1")]
        mcols_2 = [c.sbuf("mcolsA", [P, 16], BF16), c.sbuf("mcolsB", [P, 16], BF16)]
        mcols2 = c.sbuf("mcols2", [P, 64], BF16)
        rrow = c.sbuf("rrow", [1, 512], F32)
        bcs = c.sbuf("bcs", [P, 512], F32)
        cq = c.sbuf("cq", [P, T], BF16)
        sq = c.sbuf("sq", [P, T], BF16)
        ck = c.sbuf("ck", [P, T], BF16)
        sk = c.sbuf("sk", [P, T], BF16)
        msk = c.sbuf("msk", [P, 4 * 512], BF16)
        nid = c.sbuf("nid", [P, P], BF16)
        wb2 = [c.sbuf(f"wpb{i}", [P, DC * P], BF16) for i in range(2)]
        wv_sb = c.sbuf("wv_sb", [P, DC * 512], BF16)
        t1 = c.sbuf("t1", [P, 512], F32)
        t2 = c.sbuf("t2", [P, 512], F32)
        xrt = [t1, t2]
        xout = c.sbuf("xout", [P, 512], F32)
        psP = c.psum("psP", [P, 512], F32)
        psQ = c.psum("psQ", [P, 512], F32)
        psSA = ps_a1
        psSB = ps_a2
        psRow = ps_row
        psO = c.psum("psO", [P, 512], F32)
        psD = c.psum("psD", [P, 512], F32)
        psBC = ps_bc

        init_ln_consts(c, sc)
        ones_bf = sc["ones_bf"]
        ones_f = sc["ones_f32"]

        for dst, src_d in [(cq, cq_d), (sq, sq_d), (ck, ck_d), (sk, sk_d),
                           (msk, msk_d), (nid, nid_d)]:
            p.emit("gpsimd", lambda s, dst=dst, src_d=src_d: s.dma_start(
                out=dst.t[:], in_=src_d.t), reads=[src_d.b], writes=[dst.b], dma=True)

        emit_ln(c, sc, x_d, xn)

        for (wd, wsd, ct, st_, dst) in [(wq_d, wqs_d, cq, sq, qf),
                                        (wk_d, wks_d, ck, sk, kf)]:
            for fo in range(DC):
                wbA, wbB = wb2[0], wb2[1]
                p.emit("gpsimd", lambda s, fo=fo, wbA=wbA, wd=wd: s.dma_start(
                    out=wbA.t[:].rearrange("p (k f) -> p k f", f=P),
                    in_=wd.t.rearrange("(k p) f -> p k f", p=P)[:, :, fo * P:(fo + 1) * P]),
                    reads=[wd.b], writes=[wbA.b], dma=True)
                p.emit("gpsimd", lambda s, fo=fo, wbB=wbB, wsd=wsd: s.dma_start(
                    out=wbB.t[:].rearrange("p (k f) -> p k f", f=P),
                    in_=wsd.t.rearrange("(k p) f -> p k f", p=P)[:, :, fo * P:(fo + 1) * P]),
                    reads=[wsd.b], writes=[wbB.b], dma=True)
                for u in range(4):
                    for k in range(DC):
                        p.emit("tensor", lambda s, fo=fo, u=u, k=k, wbA=wbA: s.matmul(
                            psQ.t[:], wbA.t[:, k * P:(k + 1) * P],
                            xn.t[:, k * T + u * 512: k * T + u * 512 + 512],
                            start=(k == 0), stop=(k == DC - 1)),
                            reads=[wbA.b, xn.b], writes=[psQ.b], tick=(k == DC - 1))
                        p.emit("tensor", lambda s, fo=fo, u=u, k=k, wbB=wbB: s.matmul(
                            psP.t[:], wbB.t[:, k * P:(k + 1) * P],
                            xn.t[:, k * T + u * 512: k * T + u * 512 + 512],
                            start=(k == 0), stop=(k == DC - 1)),
                            reads=[wbB.b, xn.b], writes=[psP.b], tick=(k == DC - 1))
                    us = slice(u * 512, (u + 1) * 512)
                    p.emit("vector", lambda s, us=us, ct=ct: s.tensor_tensor(
                        t1.t[:], psQ.t[:], ct.t[:, us], ALU.mult),
                        reads=[psQ.b, ct.b], writes=[t1.b])
                    p.emit("vector", lambda s, us=us, st_=st_: s.tensor_tensor(
                        t2.t[:], psP.t[:], st_.t[:, us], ALU.mult),
                        reads=[psP.b, st_.b], writes=[t2.b])
                    p.emit("vector", lambda s, fo=fo, us=us, dst=dst: s.tensor_tensor(
                        dst.t[:, fo * T + us.start: fo * T + us.stop],
                        t1.t[:], t2.t[:], ALU.add),
                        reads=[t1.b, t2.b], writes=[dst.b])
        for half in range(2):
            p.emit("gpsimd", lambda s, half=half: s.dma_start(
                out=wv_sb.t[:].rearrange("p (k f) -> p k f", f=512),
                in_=wv_d.t.rearrange("(k p) f -> p k f", p=P)[:, :, half * 512:(half + 1) * 512]),
                reads=[wv_d.b], writes=[wv_sb.b], dma=True)
            for tile in range(16):
                for k in range(DC):
                    p.emit("tensor", lambda s, tile=tile, k=k: s.matmul(
                        psP.t[:], xn.t[:, k * T + tile * P: k * T + (tile + 1) * P],
                        wv_sb.t[:, k * 512:(k + 1) * 512],
                        start=(k == 0), stop=(k == DC - 1)),
                        reads=[xn.b, wv_sb.b], writes=[psP.b], tick=(k == DC - 1))
                p.emit("vector", lambda s, tile=tile, half=half: s.tensor_copy(
                    vt.t[:].rearrange("p (t h f) -> p t h f", h=16, f=65)
                    [:, tile, half * 8:(half + 1) * 8, 0:64],
                    psP.t[:].rearrange("p (h f) -> p h f", f=64)),
                    reads=[psP.b], writes=[vt.b])

        sh = dict(qaug2=qaug2, kaug2=kaug2, ptile=ptile, ptb2=ptb2,
                  mcols_2=mcols_2, mcols2=mcols2, rrow=rrow, bcs=bcs, msk=msk, nid=nid,
                  ones_bf=ones_bf, ones_f=ones_f, t1=t1,
                  psSA=psSA, psSB=psSB, psRow=psRow, psO=psO, psD=psD, psBC=psBC,
                  p1banks=[psP, psQ, psD])
        emit_attn_heads(c, sh, qf, kf, vt, causal=True)

        for fo in range(DC):
            wb = wb2[fo % 2]
            p.emit("gpsimd", lambda s, fo=fo, wb=wb: s.dma_start(
                out=wb.t[:].rearrange("p (k f) -> p k f", f=P),
                in_=wo_d.t.rearrange("(k p) f -> p k f", p=P)[:, :, fo * P:(fo + 1) * P]),
                reads=[wo_d.b], writes=[wb.b], dma=True)
            for u in range(4):
                par = u % 2
                xs = xrt[par].t[:, 0:512]
                p.emit("sync", lambda s, fo=fo, u=u, xs=xs: s.dma_start(
                    out=xs, in_=x_d.t[fo * P:(fo + 1) * P, u * 512:(u + 1) * 512]),
                    reads=[x_d.b], writes=[xrt[par].b], dma=True)
                for k in range(DC):
                    p.emit("tensor", lambda s, fo=fo, u=u, k=k, wb=wb: s.matmul(
                        psP.t[:], wb.t[:, k * P:(k + 1) * P],
                        qf.t[:, k * T + u * 512: k * T + u * 512 + 512],
                        start=(k == 0), stop=(k == DC - 1)),
                        reads=[wb.b, qf.b], writes=[psP.b], tick=(k == DC - 1))
                p.emit("vector", lambda s, xs=xs, par=par: s.tensor_tensor(
                    xout.t[:], psP.t[:], xs, ALU.add),
                    reads=[psP.b, xrt[par].b], writes=[xout.b])
                p.emit("sync", lambda s, fo=fo, u=u: s.dma_start(
                    out=xo_d.t[fo * P:(fo + 1) * P, u * 512:(u + 1) * 512], in_=xout.t[:]),
                    reads=[xout.b], writes=[xo_d.b], dma=True)
        p.emit("sync", lambda s: s.nop(), reads=[xo_d.b])
        p.barrier()
    c.prefix, c.ctx = old_pfx, old_ctx


def build_attn_b_prog():
    p = Prog("attnb")
    nc = p.nc
    ctx = contextlib.ExitStack()
    c = Core(p, ctx)
    x_d = c.dram("x_fm", [D, T], F32, kind="ExternalInput")
    xo_d = c.dram("xo_fm", [D, T], F32, kind="ExternalOutput")
    d = decl_attn_b(c)
    with ctx:
        emit_attn_b_stage(c, d, x_d, xo_d, pfx="")
        p.emit("sync", lambda s: s.nop(), reads=[xo_d.b])
        p.build(ctx)
    return nc

def rope_tables_b():
    dk = HD
    theta = (1.0 / 10000.0 ** (np.arange(0, dk, 2, dtype=np.float32) / dk)).astype(np.float32)
    mth = np.outer(np.arange(T, dtype=np.float32), theta).astype(np.float32)
    ang = np.concatenate([mth, mth], axis=1)
    cos = np.cos(ang, dtype=np.float32)
    sin = np.sin(ang, dtype=np.float32)
    sgn = np.concatenate([-np.ones(32, np.float32), np.ones(32, np.float32)])
    sin_s = sin * sgn[None, :]
    cosf = np.ascontiguousarray(np.tile(cos.T, (2, 1)))
    sinf = np.ascontiguousarray(np.tile(sin_s.T, (2, 1)))
    return cosf, sinf


def attn_b_host_inputs(x_batch_fm, ins):
    g = lambda k: np.asarray(ins[k], np.float32)
    Wq, Wk, Wv, Wo = g("L0_attn_b_W_Q"), g("L0_attn_b_W_K"), g("L0_attn_b_W_V"), g("L0_attn_b_W_O")
    perm = np.arange(D).reshape(NH, HD)
    perm = np.concatenate([perm[:, 32:], perm[:, :32]], axis=1).reshape(-1)
    cosf, sinf = rope_tables_b()
    masks = np.zeros((P, 4 * 512), np.float32)
    for j in range(4):
        s_glob = np.arange(P)[:, None] + j * P + 0
        t_glob = np.arange(512)[None, :]
        masks[:, j * 512:(j + 1) * 512] = (np.arange(P)[:, None] + (3 - j) * -0 + j * 128
                                           <= t_glob)
    for j in range(4):
        masks[:, j * 512:(j + 1) * 512] = ((j * 128 + np.arange(P)[:, None]) <= np.arange(512)[None, :])
    return dict(
        x_fm=np.ascontiguousarray(x_batch_fm),
        WqT=np.ascontiguousarray(Wq.T), WqsT=np.ascontiguousarray(Wq.T[:, perm]),
        WkT=np.ascontiguousarray(Wk.T), WksT=np.ascontiguousarray(Wk.T[:, perm]),
        WvT=np.ascontiguousarray(Wv.T), WoT=np.ascontiguousarray(Wo.T),
        cos8q=np.ascontiguousarray(cosf / 8.0), sin8qs=np.ascontiguousarray(sinf / 8.0),
        cosk=cosf, sinks=sinf,
        masks=masks, negid8=np.ascontiguousarray(-np.eye(P, dtype=np.float32)),
    )


def decl_attn_a(c, npfx=""):
    return {
        "wq_d": c.dram(npfx + "WqT", [D, D], F32, kind="ExternalInput"),
        "wqs_d": c.dram(npfx + "WqsT", [D, D], F32, kind="ExternalInput"),
        "wk_d": c.dram(npfx + "WkT", [D, D], F32, kind="ExternalInput"),
        "wks_d": c.dram(npfx + "WksT", [D, D], F32, kind="ExternalInput"),
        "wv_d": c.dram(npfx + "WvT", [D, D], F32, kind="ExternalInput"),
        "wo_d": c.dram(npfx + "WoT", [D, D], F32, kind="ExternalInput"),
        "ck_d": c.dram(npfx + "cosk", [P, T], F32, kind="ExternalInput"),
        "sk_d": c.dram(npfx + "sinks", [P, T], F32, kind="ExternalInput"),
        "nid_d": c.dram(npfx + "negid", [P, P], F32, kind="ExternalInput"),
        "lnw_d": c.dram(npfx + "lnw", [P, DC], F32, kind="ExternalInput"),
        "lnb_d": c.dram(npfx + "lnb", [P, DC], F32, kind="ExternalInput"),
        "bq_d": c.dram(npfx + "bq", [P, DC], F32, kind="ExternalInput"),
        "bqs_d": c.dram(npfx + "bqs", [P, DC], F32, kind="ExternalInput"),
        "bk_d": c.dram(npfx + "bk", [P, DC], F32, kind="ExternalInput"),
        "bks_d": c.dram(npfx + "bks", [P, DC], F32, kind="ExternalInput"),
        "bv_d": c.dram(npfx + "bv_row", [1, D], F32, kind="ExternalInput"),
        "bo_d": c.dram(npfx + "bo", [P, DC], F32, kind="ExternalInput"),
    }


def emit_attn_a_stage(c, d, x_d, xo_d, pfx="s5_"):
    p = c.p
    wq_d = d["wq_d"]
    wqs_d = d["wqs_d"]
    wk_d = d["wk_d"]
    wks_d = d["wks_d"]
    wv_d = d["wv_d"]
    wo_d = d["wo_d"]
    ck_d = d["ck_d"]
    sk_d = d["sk_d"]
    nid_d = d["nid_d"]
    lnw_d = d["lnw_d"]
    lnb_d = d["lnb_d"]
    bq_d = d["bq_d"]
    bqs_d = d["bqs_d"]
    bk_d = d["bk_d"]
    bks_d = d["bks_d"]
    bv_d = d["bv_d"]
    bo_d = d["bo_d"]
    old_pfx, old_ctx = c.prefix, c.ctx
    c.prefix = pfx
    with contextlib.ExitStack() as sctx:
        c.ctx = sctx
        ps_a1 = c.psum("ps_a1", [P, 512], F32)
        ps_a2 = c.psum("ps_a2", [P, 512], F32)
        ps_row = c.psum("ps_row", [P, 512], F32)
        ps_bc = c.psum("ps_bc", [P, 512], F32)
        sc = make_ln_scratch(c, ps_share=dict(
            statm_ps=ps_row, stats_ps=ps_bc, bc_a=ps_a1, bc_b=ps_a2))
        xn = c.sbuf("xn", [P, DC * T], BF16)
        abrows = c.sbuf("abrows", [1, 2 * T], F32)
        qf = c.sbuf("qf", [P, DC * T], BF16)
        kf = c.sbuf("kf", [P, DC * T], BF16)
        vt = c.sbuf("vt", [P, 16 * 1040], BF16)
        p.emit("vector", lambda s: s.memset(
            vt.t[:].rearrange("p (t h f) -> p t h f", h=16, f=65)[:, :, :, 64:65], 1.0),
            writes=[vt.b])
        qaug2 = [c.sbuf("qaugA", [P, T], BF16), c.sbuf("qaugB", [P, T], BF16)]
        kaug2 = [c.sbuf("kaugA", [P, T], BF16), c.sbuf("kaugB", [P, T], BF16)]
        ptile = c.sbuf("ptile", [P, 2 * 512], BF16)
        ptb2 = [p.buf("pt0"), p.buf("pt1")]
        mcols_2 = [c.sbuf("mcolsA", [P, 16], BF16), c.sbuf("mcolsB", [P, 16], BF16)]
        mcols2 = c.sbuf("mcols2", [P, 64], BF16)
        rrow = c.sbuf("rrow", [1, 512], F32)
        bcs = c.sbuf("bcs", [P, 512], F32)
        ck = c.sbuf("ck", [P, T], BF16)
        sk = c.sbuf("sk", [P, T], BF16)
        nid = c.sbuf("nid", [P, P], BF16)
        cols = c.sbuf("cols", [P, 9 * DC], F32)
        bvbc = c.sbuf("bvbc", [P, D], BF16)
        wb2 = [c.sbuf(f"wpb{i}", [P, DC * P], BF16) for i in range(2)]
        wv_sb = c.sbuf("wv_sb", [P, DC * 256], BF16)
        t1 = c.sbuf("t1", [P, 512], F32)
        t2 = c.sbuf("t2", [P, 512], F32)
        xrt = [t1, t2]
        xout = c.sbuf("xout", [P, 512], F32)
        psP = c.psum("psP", [P, 512], F32)
        psQ = c.psum("psQ", [P, 512], F32)
        psSA = ps_a1
        psSB = ps_a2
        psRow = ps_row
        psO = c.psum("psO", [P, 512], F32)
        psD = c.psum("psD", [P, 512], F32)
        psBC = ps_bc

        init_ln_consts(c, sc)
        ones_bf = sc["ones_bf"]
        ones_f = sc["ones_f32"]

        for dst, src_d in [(ck, ck_d), (sk, sk_d), (nid, nid_d)]:
            p.emit("gpsimd", lambda s, dst=dst, src_d=src_d: s.dma_start(
                out=dst.t[:], in_=src_d.t), reads=[src_d.b], writes=[dst.b], dma=True)
        for i, src_d in enumerate([lnw_d, lnb_d, bq_d, bqs_d, bk_d, bks_d, bo_d]):
            p.emit("sync", lambda s, i=i, src_d=src_d: s.dma_start(
                out=cols.t[:, i * DC:(i + 1) * DC], in_=src_d.t),
                reads=[src_d.b], writes=[cols.b], dma=True)
        LNW, LNB, BQ, BQS, BK, BKS, BO = range(7)
        colsl = lambda i, ch: cols.t[:, i * DC + ch:i * DC + ch + 1]
        p.emit("gpsimd", lambda s: s.dma_start(
            out=bvbc.t[:], in_=bv_d.t.partition_broadcast(P).squeeze(1)),
            reads=[bv_d.b], writes=[bvbc.b], dma=True)

        emit_ln_double(c, sc, x_d, xn, gamma=TB(cols.t[:, LNW * DC:(LNW + 1) * DC], cols.b),
                       beta=TB(cols.t[:, LNB * DC:(LNB + 1) * DC], cols.b),
                       keep_rows=abrows)

        for (wd, wsd, ct, st_, dst, bi, bsi, qscale) in [
                (wq_d, wqs_d, ck, sk, qf, BQ, BQS, True),
                (wk_d, wks_d, ck, sk, kf, BK, BKS, False)]:
            for fo in range(DC):
                wbA, wbB = wb2[0], wb2[1]
                p.emit("gpsimd", lambda s, fo=fo, wbA=wbA, wd=wd: s.dma_start(
                    out=wbA.t[:].rearrange("p (k f) -> p k f", f=P),
                    in_=wd.t.rearrange("(k p) f -> p k f", p=P)[:, :, fo * P:(fo + 1) * P]),
                    reads=[wd.b], writes=[wbA.b], dma=True)
                p.emit("gpsimd", lambda s, fo=fo, wbB=wbB, wsd=wsd: s.dma_start(
                    out=wbB.t[:].rearrange("p (k f) -> p k f", f=P),
                    in_=wsd.t.rearrange("(k p) f -> p k f", p=P)[:, :, fo * P:(fo + 1) * P]),
                    reads=[wsd.b], writes=[wbB.b], dma=True)
                for u in range(4):
                    for k in range(DC):
                        p.emit("tensor", lambda s, fo=fo, u=u, k=k, wbA=wbA: s.matmul(
                            psQ.t[:], wbA.t[:, k * P:(k + 1) * P],
                            xn.t[:, k * T + u * 512: k * T + u * 512 + 512],
                            start=(k == 0), stop=(k == DC - 1)),
                            reads=[wbA.b, xn.b], writes=[psQ.b], tick=(k == DC - 1))
                        p.emit("tensor", lambda s, fo=fo, u=u, k=k, wbB=wbB: s.matmul(
                            psP.t[:], wbB.t[:, k * P:(k + 1) * P],
                            xn.t[:, k * T + u * 512: k * T + u * 512 + 512],
                            start=(k == 0), stop=(k == DC - 1)),
                            reads=[wbB.b, xn.b], writes=[psP.b], tick=(k == DC - 1))
                    us = slice(u * 512, (u + 1) * 512)
                    if qscale:
                        p.emit("vector", lambda s, fo=fo, bi=bi: s.tensor_scalar(
                            t1.t[:], psQ.t[:], colsl(bi, fo), 0.125, ALU.add, ALU.mult),
                            reads=[psQ.b, cols.b], writes=[t1.b])
                        p.emit("vector", lambda s, fo=fo, bsi=bsi: s.tensor_scalar(
                            t2.t[:], psP.t[:], colsl(bsi, fo), 0.125, ALU.add, ALU.mult),
                            reads=[psP.b, cols.b], writes=[t2.b])
                    else:
                        p.emit("vector", lambda s, fo=fo, bi=bi: s.tensor_scalar_add(
                            t1.t[:], psQ.t[:], colsl(bi, fo)),
                            reads=[psQ.b, cols.b], writes=[t1.b])
                        p.emit("vector", lambda s, fo=fo, bsi=bsi: s.tensor_scalar_add(
                            t2.t[:], psP.t[:], colsl(bsi, fo)),
                            reads=[psP.b, cols.b], writes=[t2.b])
                    p.emit("vector", lambda s, us=us, ct=ct: s.tensor_tensor(
                        t1.t[:], t1.t[:], ct.t[:, us], ALU.mult),
                        reads=[t1.b, ct.b], writes=[t1.b])
                    p.emit("vector", lambda s, us=us, st_=st_: s.tensor_tensor(
                        t2.t[:], t2.t[:], st_.t[:, us], ALU.mult),
                        reads=[t2.b, st_.b], writes=[t2.b])
                    p.emit("vector", lambda s, fo=fo, us=us, dst=dst: s.tensor_tensor(
                        dst.t[:, fo * T + us.start: fo * T + us.stop],
                        t1.t[:], t2.t[:], ALU.add),
                        reads=[t1.b, t2.b], writes=[dst.b])
        for quar in range(4):
            p.emit("gpsimd", lambda s, quar=quar: s.dma_start(
                out=wv_sb.t[:].rearrange("p (k f) -> p k f", f=256),
                in_=wv_d.t.rearrange("(k p) f -> p k f", p=P)[:, :, quar * 256:(quar + 1) * 256]),
                reads=[wv_d.b], writes=[wv_sb.b], dma=True)
            for tile in range(16):
                for k in range(DC):
                    p.emit("tensor", lambda s, tile=tile, k=k: s.matmul(
                        psP.t[:, 0:256], xn.t[:, k * T + tile * P: k * T + (tile + 1) * P],
                        wv_sb.t[:, k * 256:(k + 1) * 256],
                        start=(k == 0), stop=(k == DC - 1)),
                        reads=[xn.b, wv_sb.b], writes=[psP.b], tick=(k == DC - 1))
                p.emit("vector", lambda s, tile=tile, quar=quar: s.tensor_tensor(
                    vt.t[:].rearrange("p (t h f) -> p t h f", h=16, f=65)
                    [:, tile, quar * 4:(quar + 1) * 4, 0:64],
                    psP.t[:, 0:256].rearrange("p (h f) -> p h f", f=64),
                    bvbc.t[:, quar * 256:(quar + 1) * 256].rearrange("p (h f) -> p h f", f=64),
                    ALU.add),
                    reads=[psP.b, bvbc.b], writes=[vt.b])

        sh = dict(qaug2=qaug2, kaug2=kaug2, ptile=ptile, ptb2=ptb2,
                  mcols_2=mcols_2, mcols2=mcols2, rrow=rrow, bcs=bcs, nid=nid,
                  ones_bf=ones_bf, ones_f=ones_f, t1=t1,
                  psSA=psSA, psSB=psSB, psRow=psRow, psO=psO, psD=psD, psBC=psBC,
                  p1banks=[psP, psQ, psD])
        emit_attn_heads(c, sh, qf, kf, vt, causal=False)

        for u in range(4):
            p.emit("vector", lambda s, u=u: s.tensor_scalar_add(
                rrow.t[0:1, :], abrows.t[0:1, u * 512:(u + 1) * 512], 1.0),
                reads=[abrows.b], writes=[rrow.b])
            p.emit("tensor", lambda s, u=u: s.matmul(
                ps_a1.t[:], ones_f.t[:], rrow.t[0:1, :], start=True, stop=True),
                reads=[rrow.b, ones_f.b], writes=[ps_a1.b])
            p.emit("tensor", lambda s, u=u: s.matmul(
                ps_a2.t[:], ones_f.t[:],
                abrows.t[0:1, T + u * 512: T + (u + 1) * 512], start=True, stop=True),
                reads=[abrows.b, ones_f.b], writes=[ps_a2.b], tick=True)

            for fo in range(DC):
                wb = wb2[fo % 2]
                p.emit("gpsimd", lambda s, fo=fo, wb=wb: s.dma_start(
                    out=wb.t[:].rearrange("p (k f) -> p k f", f=P),
                    in_=wo_d.t.rearrange("(k p) f -> p k f", p=P)[:, :, fo * P:(fo + 1) * P]),
                    reads=[wo_d.b], writes=[wb.b], dma=True)
                par = fo % 2
                xs = xrt[par].t[:, 0:512]
                p.emit("sync", lambda s, fo=fo, u=u, xs=xs: s.dma_start(
                    out=xs, in_=x_d.t[fo * P:(fo + 1) * P, u * 512:(u + 1) * 512]),
                    reads=[x_d.b], writes=[xrt[par].b], dma=True)
                for k in range(DC):
                    p.emit("tensor", lambda s, fo=fo, u=u, k=k, wb=wb: s.matmul(
                        psP.t[:], wb.t[:, k * P:(k + 1) * P],
                        qf.t[:, k * T + u * 512: k * T + u * 512 + 512],
                        start=(k == 0), stop=(k == DC - 1)),
                        reads=[wb.b, qf.b], writes=[psP.b], tick=(k == DC - 1))
                p.emit("vector", lambda s, xs=xs, par=par: s.tensor_tensor(
                    xout.t[:], xs, ps_a1.t[:], ALU.mult),
                    reads=[xrt[par].b, ps_a1.b], writes=[xout.b])
                p.emit("vector", lambda s: s.tensor_tensor(
                    xout.t[:], xout.t[:], ps_a2.t[:], ALU.add),
                    reads=[xout.b, ps_a2.b], writes=[xout.b])
                p.emit("vector", lambda s, fo=fo: s.tensor_scalar_add(
                    t1.t[:], psP.t[:], colsl(BO, fo)),
                    reads=[psP.b, cols.b], writes=[t1.b])
                p.emit("vector", lambda s: s.tensor_tensor(
                    xout.t[:], xout.t[:], t1.t[:], ALU.add),
                    reads=[xout.b, t1.b], writes=[xout.b])
                p.emit("sync", lambda s, fo=fo, u=u: s.dma_start(
                    out=xo_d.t[fo * P:(fo + 1) * P, u * 512:(u + 1) * 512], in_=xout.t[:]),
                    reads=[xout.b], writes=[xo_d.b], dma=True)
        p.emit("sync", lambda s: s.nop(), reads=[xo_d.b])
        p.barrier()
    c.prefix, c.ctx = old_pfx, old_ctx


def build_attn_a_prog():
    p = Prog("attna")
    nc = p.nc
    ctx = contextlib.ExitStack()
    c = Core(p, ctx)
    x_d = c.dram("x_fm", [D, T], F32, kind="ExternalInput")
    xo_d = c.dram("xo_fm", [D, T], F32, kind="ExternalOutput")
    d = decl_attn_a(c)
    with ctx:
        emit_attn_a_stage(c, d, x_d, xo_d, pfx="")
        p.emit("sync", lambda s: s.nop(), reads=[xo_d.b])
        p.build(ctx)
    return nc

def rope_tables_a():
    hd = HD
    inv = (1.0 / 10000.0 ** (np.arange(0, hd, 2, dtype=np.float32) / hd)).astype(np.float32)
    fr = np.outer(np.arange(T, dtype=np.float32), inv).astype(np.float32)
    cos = np.cos(fr, dtype=np.float32)
    sin = np.sin(fr, dtype=np.float32)
    cos_il = np.repeat(cos, 2, axis=1)
    sin_il = np.repeat(sin, 2, axis=1)
    sgn = np.tile(np.array([-1.0, 1.0], np.float32), 32)
    sin_s = sin_il * sgn[None, :]
    cosf = np.ascontiguousarray(np.tile(cos_il.T, (2, 1)))
    sinf = np.ascontiguousarray(np.tile(sin_s.T, (2, 1)))
    return cosf, sinf


def attn_a_host_inputs(x_batch_fm, ins):
    g = lambda k: np.asarray(ins[k], np.float32)
    Wq, Wk, Wv, Wo = g("L1_attn_a_Wq"), g("L1_attn_a_Wk"), g("L1_attn_a_Wv"), g("L1_attn_a_Wo")
    bq, bk, bv, bo = g("L1_attn_a_bq"), g("L1_attn_a_bk"), g("L1_attn_a_bv"), g("L1_attn_a_bo")
    lnw, lnb = g("L1_attn_a_ln_w"), g("L1_attn_a_ln_b")
    perm = np.arange(D).reshape(NH * 32, 2)[:, ::-1].reshape(-1)
    cosf, sinf = rope_tables_a()
    colify = lambda v: np.ascontiguousarray(v.reshape(DC, P).T)
    return dict(
        x_fm=np.ascontiguousarray(x_batch_fm),
        WqT=np.ascontiguousarray(Wq.T), WqsT=np.ascontiguousarray(Wq.T[:, perm]),
        WkT=np.ascontiguousarray(Wk.T), WksT=np.ascontiguousarray(Wk.T[:, perm]),
        WvT=np.ascontiguousarray(Wv.T), WoT=np.ascontiguousarray(Wo.T),
        cosk=cosf, sinks=sinf,
        negid=np.ascontiguousarray(-np.eye(P, dtype=np.float32)),
        lnw=colify(lnw), lnb=colify(lnb),
        bq=colify(bq), bqs=colify(bq[perm]), bk=colify(bk), bks=colify(bk[perm]),
        bv_row=np.ascontiguousarray(bv.reshape(1, D)), bo=colify(bo),
    )


def decl_attn_b1(c, npfx=""):
    QL, KV = 512, 64
    return {
        "wqd_d": c.dram(npfx + "wqdT", [D, QL], F32, kind="ExternalInput"),
        "wqk_d": c.dram(npfx + "wqkT", [QL, D], F32, kind="ExternalInput"),
        "wkv_d": c.dram(npfx + "wkvT", [D, KV], F32, kind="ExternalInput"),
        "wvu_d": c.dram(npfx + "wvuT", [KV, D], F32, kind="ExternalInput"),
        "wpr_d": c.dram(npfx + "wprT", [D, D], F32, kind="ExternalInput"),
        "bqd_d": c.dram(npfx + "bqd", [P, 4], F32, kind="ExternalInput"),
        "bqk_d": c.dram(npfx + "bqk", [P, DC], F32, kind="ExternalInput"),
        "bkv_d": c.dram(npfx + "bkv", [64, 1], F32, kind="ExternalInput"),
        "bvu_d": c.dram(npfx + "bvu_row", [1, D], F32, kind="ExternalInput"),
        "bpr_d": c.dram(npfx + "bpr", [P, DC], F32, kind="ExternalInput"),
        "nid_d": c.dram(npfx + "negid", [P, P], F32, kind="ExternalInput"),
    }


def emit_attn_b1_stage(c, d, x_d, xo_d, pfx="s7_"):
    p = c.p
    QL, KV = 512, 64
    wqd_d = d["wqd_d"]
    wqk_d = d["wqk_d"]
    wkv_d = d["wkv_d"]
    wvu_d = d["wvu_d"]
    wpr_d = d["wpr_d"]
    bqd_d = d["bqd_d"]
    bqk_d = d["bqk_d"]
    bkv_d = d["bkv_d"]
    bvu_d = d["bvu_d"]
    bpr_d = d["bpr_d"]
    nid_d = d["nid_d"]
    old_pfx, old_ctx = c.prefix, c.ctx
    c.prefix = pfx
    with contextlib.ExitStack() as sctx:
        c.ctx = sctx
        ps_a1 = c.psum("ps_a1", [P, 512], F32)
        ps_a2 = c.psum("ps_a2", [P, 512], F32)
        ps_row = c.psum("ps_row", [P, 512], F32)
        ps_bc = c.psum("ps_bc", [P, 512], F32)
        sc = make_ln_scratch(c, ps_share=dict(
            statm_ps=ps_row, stats_ps=ps_bc, bc_a=ps_a1, bc_b=ps_a2))
        xn = c.sbuf("xn", [P, DC * T], BF16)
        qf = c.sbuf("qf", [P, DC * T], BF16)
        cqf = c.sbuf("cqf", [P, 4 * T], BF16)
        vt = c.sbuf("vt", [P, 16 * 1040], BF16)
        p.emit("vector", lambda s: s.memset(
            vt.t[:].rearrange("p (t h f) -> p t h f", h=16, f=65)[:, :, :, 64:65], 1.0),
            writes=[vt.b])
        qaug2 = [c.sbuf("qaugA", [P, T], BF16), c.sbuf("qaugB", [P, T], BF16)]
        kaug2 = [c.sbuf("kaugA", [P, T], BF16)]
        ptile = c.sbuf("ptile", [P, 2 * 512], BF16)
        ptb2 = [p.buf("pt0"), p.buf("pt1")]
        mcols_2 = [c.sbuf("mcolsA", [P, 16], BF16), c.sbuf("mcolsB", [P, 16], BF16)]
        mcols2 = c.sbuf("mcols2", [P, 64], BF16)
        rrow = c.sbuf("rrow", [1, 512], F32)
        bcs = c.sbuf("bcs", [P, 512], F32)
        nid = c.sbuf("nid", [P, P], BF16)
        cols = c.sbuf("cols", [P, 4 + DC + DC], F32)
        bkvc = c.sbuf("bkvc", [64, 1], F32)
        bvbc = c.sbuf("bvbc", [P, D], BF16)
        wb2 = [c.sbuf(f"wpb{i}", [P, DC * P], BF16) for i in range(2)]
        wqk_sb = c.sbuf("wqk_sb", [P, 4 * P], BF16)
        wkv_sb = c.sbuf("wkv_sb", [P, DC * KV], BF16)
        wvu_sb = c.sbuf("wvu_sb", [64, D], BF16)
        t1 = c.sbuf("t1", [P, 512], F32)
        t2x = c.sbuf("t2x", [P, 512], F32)
        t3x = c.sbuf("t3x", [P, 512], F32)
        xrt = [t2x, t3x]
        xout = c.sbuf("xout", [P, 512], F32)
        psP = c.psum("psP", [P, 512], F32)
        psQ = c.psum("psQ", [P, 512], F32)
        psSA = ps_a1
        psSB = ps_a2
        psRow = ps_row
        psO = c.psum("psO", [P, 512], F32)
        psD = c.psum("psD", [P, 512], F32)
        psBC = ps_bc

        init_ln_consts(c, sc)
        ones_bf = sc["ones_bf"]
        ones_f = sc["ones_f32"]

        p.emit("gpsimd", lambda s: s.dma_start(out=nid.t[:], in_=nid_d.t),
               reads=[nid_d.b], writes=[nid.b], dma=True)
        for i, (src_d, w) in enumerate([(bqd_d, 4), (bqk_d, DC), (bpr_d, DC)]):
            off = [0, 4, 4 + DC][i]
            p.emit("sync", lambda s, src_d=src_d, off=off, w=w: s.dma_start(
                out=cols.t[:, off:off + w], in_=src_d.t),
                reads=[src_d.b], writes=[cols.b], dma=True)
        p.emit("sync", lambda s: s.dma_start(out=bkvc.t[:], in_=bkv_d.t),
               reads=[bkv_d.b], writes=[bkvc.b], dma=True)
        p.emit("gpsimd", lambda s: s.dma_start(
            out=bvbc.t[:], in_=bvu_d.t.partition_broadcast(P).squeeze(1)),
            reads=[bvu_d.b], writes=[bvbc.b], dma=True)
        p.emit("gpsimd", lambda s: s.dma_start(
            out=wkv_sb.t[:].rearrange("p (k f) -> p k f", f=KV),
            in_=wkv_d.t.rearrange("(k p) f -> p k f", p=P)),
            reads=[wkv_d.b], writes=[wkv_sb.b], dma=True)
        p.emit("gpsimd", lambda s: s.dma_start(
            out=wvu_sb.t[0:64, :], in_=wvu_d.t),
            reads=[wvu_d.b], writes=[wvu_sb.b], dma=True)

        emit_ln(c, sc, x_d, xn)

        for fo in range(4):
            wb = wb2[fo % 2]
            p.emit("gpsimd", lambda s, fo=fo, wb=wb: s.dma_start(
                out=wb.t[:].rearrange("p (k f) -> p k f", f=P),
                in_=wqd_d.t.rearrange("(k p) f -> p k f", p=P)[:, :, fo * P:(fo + 1) * P]),
                reads=[wqd_d.b], writes=[wb.b], dma=True)
            for u in range(4):
                for k in range(DC):
                    p.emit("tensor", lambda s, fo=fo, u=u, k=k, wb=wb: s.matmul(
                        psP.t[:], wb.t[:, k * P:(k + 1) * P],
                        xn.t[:, k * T + u * 512: k * T + u * 512 + 512],
                        start=(k == 0), stop=(k == DC - 1)),
                        reads=[wb.b, xn.b], writes=[psP.b], tick=(k == DC - 1))
                p.emit("vector", lambda s, fo=fo, u=u: s.tensor_scalar_add(
                    cqf.t[:, fo * T + u * 512: fo * T + u * 512 + 512],
                    psP.t[:], cols.t[:, fo:fo + 1]),
                    reads=[psP.b, cols.b], writes=[cqf.b])
        for u in range(4):
            for k in range(DC):
                p.emit("tensor", lambda s, u=u, k=k: s.matmul(
                    psQ.t[0:64, :], wkv_sb.t[:, k * KV:(k + 1) * KV],
                    xn.t[:, k * T + u * 512: k * T + u * 512 + 512],
                    start=(k == 0), stop=(k == DC - 1)),
                    reads=[wkv_sb.b, xn.b], writes=[psQ.b], tick=(k == DC - 1))
            p.emit("vector", lambda s, u=u: s.tensor_scalar_add(
                kaug2[0].t[0:64, u * 512:(u + 1) * 512], psQ.t[0:64, :], bkvc.t[0:64, 0:1]),
                reads=[psQ.b, bkvc.b], writes=[kaug2[0].b])
        p.emit("vector", lambda s: s.memset(kaug2[0].t[64:65, :], 1.0), writes=[kaug2[0].b])
        for fo in range(DC):
            p.emit("gpsimd", lambda s, fo=fo: s.dma_start(
                out=wqk_sb.t[:].rearrange("p (k f) -> p k f", f=P),
                in_=wqk_d.t.rearrange("(k p) f -> p k f", p=P)[:, :, fo * P:(fo + 1) * P]),
                reads=[wqk_d.b], writes=[wqk_sb.b], dma=True)
            for u in range(4):
                for k in range(4):
                    p.emit("tensor", lambda s, fo=fo, u=u, k=k: s.matmul(
                        psP.t[:], wqk_sb.t[:, k * P:(k + 1) * P],
                        cqf.t[:, k * T + u * 512: k * T + u * 512 + 512],
                        start=(k == 0), stop=(k == 3)),
                        reads=[wqk_sb.b, cqf.b], writes=[psP.b], tick=(k == 3))
                p.emit("vector", lambda s, fo=fo, u=u: s.tensor_scalar(
                    qf.t[:, fo * T + u * 512: fo * T + u * 512 + 512],
                    psP.t[:], cols.t[:, 4 + fo:4 + fo + 1], 0.125, ALU.add, ALU.mult),
                    reads=[psP.b, cols.b], writes=[qf.b])
        for tile in range(16):
            for half in range(2):
                p.emit("tensor", lambda s, tile=tile, half=half: s.matmul(
                    psP.t[:], kaug2[0].t[0:64, tile * P:(tile + 1) * P],
                    wvu_sb.t[0:64, half * 512:(half + 1) * 512],
                    start=True, stop=True),
                    reads=[kaug2[0].b, wvu_sb.b], writes=[psP.b], tick=True)
                p.emit("vector", lambda s, tile=tile, half=half: s.tensor_tensor(
                    vt.t[:].rearrange("p (t h f) -> p t h f", h=16, f=65)
                    [:, tile, half * 8:(half + 1) * 8, 0:64],
                    psP.t[:].rearrange("p (h f) -> p h f", f=64),
                    bvbc.t[:, half * 512:(half + 1) * 512].rearrange("p (h f) -> p h f", f=64),
                    ALU.add),
                    reads=[psP.b, bvbc.b], writes=[vt.b])

        sh = dict(qaug2=qaug2, kaug2=kaug2, ptile=ptile, ptb2=ptb2,
                  mcols_2=mcols_2, mcols2=mcols2, rrow=rrow, bcs=bcs, nid=nid,
                  ones_bf=ones_bf, ones_f=ones_f, t1=t1,
                  psSA=psSA, psSB=psSB, psRow=psRow, psO=psO, psD=psD, psBC=psBC,
                  p1banks=[psP, psQ, psD])
        emit_attn_heads(c, sh, qf, kaug2[0], vt, causal=False,
                        kaug_prebuilt=True, kf_single=True)

        for fo in range(DC):
            wb = wb2[fo % 2]
            p.emit("gpsimd", lambda s, fo=fo, wb=wb: s.dma_start(
                out=wb.t[:].rearrange("p (k f) -> p k f", f=P),
                in_=wpr_d.t.rearrange("(k p) f -> p k f", p=P)[:, :, fo * P:(fo + 1) * P]),
                reads=[wpr_d.b], writes=[wb.b], dma=True)
            for u in range(4):
                par = u % 2
                xs = xrt[par].t[:, 0:512]
                p.emit("sync", lambda s, fo=fo, u=u, xs=xs: s.dma_start(
                    out=xs, in_=x_d.t[fo * P:(fo + 1) * P, u * 512:(u + 1) * 512]),
                    reads=[x_d.b], writes=[xrt[par].b], dma=True)
                for k in range(DC):
                    p.emit("tensor", lambda s, fo=fo, u=u, k=k, wb=wb: s.matmul(
                        psP.t[:], wb.t[:, k * P:(k + 1) * P],
                        qf.t[:, k * T + u * 512: k * T + u * 512 + 512],
                        start=(k == 0), stop=(k == DC - 1)),
                        reads=[wb.b, qf.b], writes=[psP.b], tick=(k == DC - 1))
                p.emit("vector", lambda s, fo=fo: s.tensor_scalar_add(
                    t1.t[:], psP.t[:], cols.t[:, 4 + DC + fo:4 + DC + fo + 1]),
                    reads=[psP.b, cols.b], writes=[t1.b])
                p.emit("vector", lambda s, xs=xs, par=par: s.tensor_tensor(
                    xout.t[:], t1.t[:], xs, ALU.add),
                    reads=[t1.b, xrt[par].b], writes=[xout.b])
                p.emit("sync", lambda s, fo=fo, u=u: s.dma_start(
                    out=xo_d.t[fo * P:(fo + 1) * P, u * 512:(u + 1) * 512], in_=xout.t[:]),
                    reads=[xout.b], writes=[xo_d.b], dma=True)
        p.emit("sync", lambda s: s.nop(), reads=[xo_d.b])
        p.barrier()
    c.prefix, c.ctx = old_pfx, old_ctx


def build_attn_b1_prog():
    p = Prog("attnb1")
    nc = p.nc
    ctx = contextlib.ExitStack()
    c = Core(p, ctx)
    x_d = c.dram("x_fm", [D, T], F32, kind="ExternalInput")
    xo_d = c.dram("xo_fm", [D, T], F32, kind="ExternalOutput")
    d = decl_attn_b1(c)
    with ctx:
        emit_attn_b1_stage(c, d, x_d, xo_d, pfx="")
        p.emit("sync", lambda s: s.nop(), reads=[xo_d.b])
        p.build(ctx)
    return nc

def attn_b1_host_inputs(x_batch_fm, ins):
    g = lambda k: np.asarray(ins[k], np.float32)
    colify = lambda v, n: np.ascontiguousarray(v.reshape(n, P).T)
    return dict(
        x_fm=np.ascontiguousarray(x_batch_fm),
        wqdT=np.ascontiguousarray(g("L1_attn_b_wq_d").T),
        wqkT=np.ascontiguousarray(g("L1_attn_b_w_qk").T),
        wkvT=np.ascontiguousarray(g("L1_attn_b_wkv_d").T),
        wvuT=np.ascontiguousarray(g("L1_attn_b_wv_u").T),
        wprT=np.ascontiguousarray(g("L1_attn_b_w_proj").T),
        bqd=colify(g("L1_attn_b_bq_d"), 4),
        bqk=colify(g("L1_attn_b_b_qk"), DC),
        bkv=np.ascontiguousarray(g("L1_attn_b_bkv_d").reshape(64, 1)),
        bvu_row=np.ascontiguousarray(g("L1_attn_b_bv_u").reshape(1, D)),
        bpr=colify(g("L1_attn_b_b_proj"), DC),
        negid=np.ascontiguousarray(-np.eye(P, dtype=np.float32)),
    )


def decl_cosformer(c, npfx=""):
    return {
        "wq_d": c.dram(npfx + "Wq", [D, D], F32, kind="ExternalInput"),
        "wk_d": c.dram(npfx + "Wk", [D, D], F32, kind="ExternalInput"),
        "wv_d": c.dram(npfx + "Wv", [D, D], F32, kind="ExternalInput"),
        "wo_d": c.dram(npfx + "Wo", [D, D], F32, kind="ExternalInput"),
        "bq_d": c.dram(npfx + "bq", [P, DC], F32, kind="ExternalInput"),
        "bk_d": c.dram(npfx + "bk", [P, DC], F32, kind="ExternalInput"),
        "bv_d": c.dram(npfx + "bv_row", [1, D], F32, kind="ExternalInput"),
        "bo_d": c.dram(npfx + "bo", [P, DC], F32, kind="ExternalInput"),
        "cosb_d": c.dram(npfx + "cosb", [P, T], F32, kind="ExternalInput"),
        "sinb_d": c.dram(npfx + "sinb", [P, T], F32, kind="ExternalInput"),
        "tri_d": c.dram(npfx + "tri", [P, P], F32, kind="ExternalInput"),
        "id64_d": c.dram(npfx + "id64", [P, 64], F32, kind="ExternalInput"),
    }


def emit_cosformer_stage(c, d, x_d, xo_d, pfx="s1_"):
    p = c.p
    wq_d = d["wq_d"]
    wk_d = d["wk_d"]
    wv_d = d["wv_d"]
    wo_d = d["wo_d"]
    bq_d = d["bq_d"]
    bk_d = d["bk_d"]
    bv_d = d["bv_d"]
    bo_d = d["bo_d"]
    cosb_d = d["cosb_d"]
    sinb_d = d["sinb_d"]
    tri_d = d["tri_d"]
    id64_d = d["id64_d"]
    old_pfx, old_ctx = c.prefix, c.ctx
    c.prefix = pfx
    with contextlib.ExitStack() as sctx:
        c.ctx = sctx
        ps_a1 = c.psum("ps_a1", [P, 512], F32)
        ps_a2 = c.psum("ps_a2", [P, 512], F32)
        ps_row = c.psum("ps_row", [P, 512], F32)
        ps_bc = c.psum("ps_bc", [P, 512], F32)
        sc = make_ln_scratch(c, ps_share=dict(
            statm_ps=ps_row, stats_ps=ps_bc, bc_a=ps_a1, bc_b=ps_a2))
        xn = c.sbuf("xn", [P, DC * T], BF16)
        qf = c.sbuf("qf", [P, DC * T], BF16)
        kf = c.sbuf("kf", [P, DC * T], BF16)
        vt = c.sbuf("vt", [P, 16 * D], BF16)
        qc0 = c.sbuf("qc0", [64, T], BF16)
        qs0 = c.sbuf("qs0", [64, T], BF16)
        kc0 = c.sbuf("kc0", [64, T], BF16)
        ks0 = c.sbuf("ks0", [64, T], BF16)
        h0 = c.sbuf("h0", [64, T], BF16)
        at_bf = c.sbuf("at_bf", [P, P], BF16)
        kct = c.sbuf("kct", [P, 2 * 64], BF16)
        scos = c.sbuf("scos", [64, 130], BF16)
        cosb = c.sbuf("cosb_sb", [P, T], BF16)
        sinb = c.sbuf("sinb_sb", [P, T], BF16)
        tri = c.sbuf("tri_sb", [P, P], BF16)
        id64 = c.sbuf("id64_sb", [P, 64], BF16)
        cols = c.sbuf("cols", [P, 3 * DC], F32)
        bvbc = c.sbuf("bvbc", [P, D], BF16)
        wb2 = [c.sbuf(f"wpb{i}", [P, DC * P], BF16) for i in range(2)]
        wv_sb = c.sbuf("wv_sb", [P, DC * 512], BF16)
        rrow = c.sbuf("rrow", [1, P], F32)
        bcs = c.sbuf("bcs", [64, P], F32)
        t1 = c.sbuf("t1", [P, 512], F32)
        t2x = c.sbuf("t2x", [P, 512], F32)
        t3x = c.sbuf("t3x", [P, 512], F32)
        xrt = [t2x, t3x]
        xout = c.sbuf("xout", [P, 512], F32)
        psP = ps_row
        psS = ps_a1
        psT = TB(ps_a2.t[:].bitcast(BF16), ps_a2.b)
        psO = c.psum("psO", [P, 512], F32)
        psD = c.psum("psD", [P, 512], F32)
        psSt = c.psum("psSt", [64, 130], F32)
        psBC = ps_bc

        init_ln_consts(c, sc)
        ones_bf = sc["ones_bf"]
        ones_f = sc["ones_f32"]

        for dst, src_d in [(cosb, cosb_d), (sinb, sinb_d), (tri, tri_d), (id64, id64_d)]:
            p.emit("gpsimd", lambda s, dst=dst, src_d=src_d: s.dma_start(
                out=dst.t[:], in_=src_d.t), reads=[src_d.b], writes=[dst.b], dma=True)
        for i, src_d in enumerate([bq_d, bk_d, bo_d]):
            p.emit("sync", lambda s, i=i, src_d=src_d: s.dma_start(
                out=cols.t[:, i * DC:(i + 1) * DC], in_=src_d.t),
                reads=[src_d.b], writes=[cols.b], dma=True)
        p.emit("gpsimd", lambda s: s.dma_start(
            out=bvbc.t[:], in_=bv_d.t.partition_broadcast(P).squeeze(1)),
            reads=[bv_d.b], writes=[bvbc.b], dma=True)

        emit_ln(c, sc, x_d, xn)

        for (wd, dst, bi) in [(wq_d, qf, 0), (wk_d, kf, 1)]:
            for fo in range(DC):
                wb = wb2[fo % 2]
                p.emit("gpsimd", lambda s, fo=fo, wb=wb, wd=wd: s.dma_start(
                    out=wb.t[:].rearrange("p (k f) -> p k f", f=P),
                    in_=wd.t.rearrange("(k p) f -> p k f", p=P)[:, :, fo * P:(fo + 1) * P]),
                    reads=[wd.b], writes=[wb.b], dma=True)
                for u in range(4):
                    for k in range(DC):
                        p.emit("tensor", lambda s, fo=fo, u=u, k=k, wb=wb: s.matmul(
                            psP.t[:], wb.t[:, k * P:(k + 1) * P],
                            xn.t[:, k * T + u * 512: k * T + u * 512 + 512],
                            start=(k == 0), stop=(k == DC - 1)),
                            reads=[wb.b, xn.b], writes=[psP.b], tick=(k == DC - 1))
                    p.emit("vector", lambda s, fo=fo, u=u, dst=dst, bi=bi: s.tensor_scalar(
                        dst.t[:, fo * T + u * 512: fo * T + u * 512 + 512],
                        psP.t[:], cols.t[:, bi * DC + fo:bi * DC + fo + 1], 0.0,
                        ALU.add, ALU.max),
                        reads=[psP.b, cols.b], writes=[dst.b])
        for half in range(2):
            p.emit("gpsimd", lambda s, half=half: s.dma_start(
                out=wv_sb.t[:].rearrange("p (k f) -> p k f", f=512),
                in_=wv_d.t.rearrange("(k p) f -> p k f", p=P)[:, :, half * 512:(half + 1) * 512]),
                reads=[wv_d.b], writes=[wv_sb.b], dma=True)
            for tile in range(16):
                for k in range(DC):
                    p.emit("tensor", lambda s, tile=tile, k=k: s.matmul(
                        psP.t[:], xn.t[:, k * T + tile * P: k * T + (tile + 1) * P],
                        wv_sb.t[:, k * 512:(k + 1) * 512],
                        start=(k == 0), stop=(k == DC - 1)),
                        reads=[xn.b, wv_sb.b], writes=[psP.b], tick=(k == DC - 1))
                p.emit("vector", lambda s, tile=tile, half=half: s.tensor_tensor(
                    vt.t[:, tile * D + half * 512: tile * D + half * 512 + 512],
                    psP.t[:], bvbc.t[:, half * 512:(half + 1) * 512], ALU.add),
                    reads=[psP.b, bvbc.b], writes=[vt.b])

        for h in range(NH):
            hp, hi = h // 2, h % 2
            base = hi * 64
            for (srct, cdst, sdst) in [(qf, qc0, qs0), (kf, kc0, ks0)]:
                p.emit("sync", lambda s, base=base, hp=hp, srct=srct: s.dma_start(
                    out=h0.t[0:64, :], in_=srct.t[base:base + 64, hp * T:(hp + 1) * T]),
                    reads=[srct.b], writes=[h0.b], dma=True)
                p.emit("vector", lambda s, cdst=cdst: s.tensor_tensor(
                    cdst.t[0:64, :], h0.t[0:64, :], cosb.t[0:64, :], ALU.mult),
                    reads=[h0.b, cosb.b], writes=[cdst.b])
                p.emit("vector", lambda s, sdst=sdst: s.tensor_tensor(
                    sdst.t[0:64, :], h0.t[0:64, :], sinb.t[0:64, :], ALU.mult),
                    reads=[h0.b, sinb.b], writes=[sdst.b])
            p.emit("vector", lambda s: s.memset(scos.t[:], 0.0), writes=[scos.b])
            for ci in range(16):
                cs = slice(ci * P, (ci + 1) * P)
                p.emit("tensor", lambda s, cs=cs: s.matmul(
                    psS.t[:, 0:P], kc0.t[0:64, cs], qc0.t[0:64, cs],
                    start=True, stop=False),
                    reads=[kc0.b, qc0.b], writes=[psS.b])
                p.emit("tensor", lambda s, cs=cs: s.matmul(
                    psS.t[:, 0:P], ks0.t[0:64, cs], qs0.t[0:64, cs],
                    start=False, stop=True),
                    reads=[ks0.b, qs0.b], writes=[psS.b], tick=True)
                p.emit("vector", lambda s: s.tensor_tensor(
                    at_bf.t[:], psS.t[:, 0:P], tri.t[:], ALU.mult),
                    reads=[psS.b, tri.b], writes=[at_bf.b])
                p.emit("tensor", lambda s, ci=ci, h=h: s.matmul(
                    psO.t[0:64, 0:P], vt.t[:, ci * D + h * 64: ci * D + h * 64 + 64],
                    at_bf.t[:], start=True, stop=False),
                    reads=[vt.b, at_bf.b], writes=[psO.b])
                p.emit("tensor", lambda s, cs=cs: s.matmul(
                    psO.t[0:64, 0:P], scos.t[0:64, 0:64], qc0.t[0:64, cs],
                    start=False, stop=False),
                    reads=[scos.b, qc0.b], writes=[psO.b])
                p.emit("tensor", lambda s, cs=cs: s.matmul(
                    psO.t[0:64, 0:P], scos.t[0:64, 65:129], qs0.t[0:64, cs],
                    start=False, stop=True),
                    reads=[scos.b, qs0.b], writes=[psO.b], tick=True)
                p.emit("tensor", lambda s: s.matmul(
                    psD.t[64:65, 0:P], ones_bf.t[:], at_bf.t[:],
                    start=True, stop=False, tile_position=(0, 64)),
                    reads=[ones_bf.b, at_bf.b], writes=[psD.b])
                p.emit("tensor", lambda s, cs=cs: s.matmul(
                    psD.t[64:65, 0:P], scos.t[0:64, 64:65], qc0.t[0:64, cs],
                    start=False, stop=False, tile_position=(0, 64)),
                    reads=[scos.b, qc0.b], writes=[psD.b])
                p.emit("tensor", lambda s, cs=cs: s.matmul(
                    psD.t[64:65, 0:P], scos.t[0:64, 129:130], qs0.t[0:64, cs],
                    start=False, stop=True, tile_position=(0, 64)),
                    reads=[scos.b, qs0.b], writes=[psD.b], tick=True)
                p.emit("vector", lambda s: s.tensor_scalar_max(
                    rrow.t[0:1, :], psD.t[64:65, 0:P], 1e-6),
                    reads=[psD.b], writes=[rrow.b])
                p.emit("vector", lambda s: s.reciprocal(rrow.t[0:1, :], rrow.t[0:1, :]),
                       reads=[rrow.b], writes=[rrow.b])
                p.emit("tensor", lambda s: s.matmul(
                    psBC.t[0:64, 0:P], ones_f.t[0:1, 0:64], rrow.t[0:1, :],
                    start=True, stop=True),
                    reads=[ones_f.b, rrow.b], writes=[psBC.b], tick=True)
                p.emit("vector", lambda s: s.tensor_copy(bcs.t[0:64, :], psBC.t[0:64, 0:P]),
                       reads=[psBC.b], writes=[bcs.b])
                p.emit("vector", lambda s, base=base, hp=hp, cs=cs: s.tensor_tensor(
                    qf.t[base:base + 64, hp * T + cs.start: hp * T + cs.stop],
                    psO.t[0:64, 0:P], bcs.t[0:64, :], ALU.mult),
                    reads=[psO.b, bcs.b], writes=[qf.b])
                p.emit("tensor", lambda s, cs=cs: s.transpose(
                    psT.t[:, 0:64], kc0.t[0:64, cs], id64.t[0:64, :]),
                    reads=[kc0.b, id64.b], writes=[psT.b], tick=True)
                p.emit("vector", lambda s: s.tensor_copy(kct.t[:, 0:64], psT.t[:, 0:64]),
                       reads=[psT.b], writes=[kct.b])
                p.emit("tensor", lambda s, cs=cs: s.transpose(
                    psT.t[:, 0:64], ks0.t[0:64, cs], id64.t[0:64, :]),
                    reads=[ks0.b, id64.b], writes=[psT.b], tick=True)
                p.emit("vector", lambda s: s.tensor_copy(kct.t[:, 64:128], psT.t[:, 0:64]),
                       reads=[psT.b], writes=[kct.b])
                st0 = (ci == 0)
                p.emit("tensor", lambda s, ci=ci, h=h, st0=st0: s.matmul(
                    psSt.t[:, 0:64], kct.t[:, 0:64],
                    vt.t[:, ci * D + h * 64: ci * D + h * 64 + 64],
                    start=st0, stop=False, skip_group_check=True),
                    reads=[kct.b, vt.b], writes=[psSt.b])
                p.emit("tensor", lambda s, st0=st0: s.matmul(
                    psSt.t[:, 64:65], kct.t[:, 0:64], ones_bf.t[:],
                    start=False, stop=False, skip_group_check=True),
                    reads=[kct.b, ones_bf.b], writes=[psSt.b])
                p.emit("tensor", lambda s, ci=ci, h=h: s.matmul(
                    psSt.t[:, 65:129], kct.t[:, 64:128],
                    vt.t[:, ci * D + h * 64: ci * D + h * 64 + 64],
                    start=False, stop=False, skip_group_check=True),
                    reads=[kct.b, vt.b], writes=[psSt.b])
                p.emit("tensor", lambda s, ci=ci: s.matmul(
                    psSt.t[:, 129:130], kct.t[:, 64:128], ones_bf.t[:],
                    start=False, stop=(ci == 15), skip_group_check=True),
                    reads=[kct.b, ones_bf.b], writes=[psSt.b], tick=True)
                if ci < 15:
                    p.emit("vector", lambda s: s.tensor_copy(scos.t[:], psSt.t[:]),
                           reads=[psSt.b], writes=[scos.b])

        for fo in range(DC):
            wb = wb2[fo % 2]
            p.emit("gpsimd", lambda s, fo=fo, wb=wb: s.dma_start(
                out=wb.t[:].rearrange("p (k f) -> p k f", f=P),
                in_=wo_d.t.rearrange("(k p) f -> p k f", p=P)[:, :, fo * P:(fo + 1) * P]),
                reads=[wo_d.b], writes=[wb.b], dma=True)
            for u in range(4):
                par = u % 2
                xs = xrt[par].t[:, 0:512]
                p.emit("sync", lambda s, fo=fo, u=u, xs=xs: s.dma_start(
                    out=xs, in_=x_d.t[fo * P:(fo + 1) * P, u * 512:(u + 1) * 512]),
                    reads=[x_d.b], writes=[xrt[par].b], dma=True)
                for k in range(DC):
                    p.emit("tensor", lambda s, fo=fo, u=u, k=k, wb=wb: s.matmul(
                        psP.t[:], wb.t[:, k * P:(k + 1) * P],
                        qf.t[:, k * T + u * 512: k * T + u * 512 + 512],
                        start=(k == 0), stop=(k == DC - 1)),
                        reads=[wb.b, qf.b], writes=[psP.b], tick=(k == DC - 1))
                p.emit("vector", lambda s, fo=fo: s.tensor_scalar_add(
                    t1.t[:], psP.t[:], cols.t[:, 2 * DC + fo:2 * DC + fo + 1]),
                    reads=[psP.b, cols.b], writes=[t1.b])
                p.emit("vector", lambda s, xs=xs, par=par: s.tensor_tensor(
                    xout.t[:], t1.t[:], xs, ALU.add),
                    reads=[t1.b, xrt[par].b], writes=[xout.b])
                p.emit("sync", lambda s, fo=fo, u=u: s.dma_start(
                    out=xo_d.t[fo * P:(fo + 1) * P, u * 512:(u + 1) * 512], in_=xout.t[:]),
                    reads=[xout.b], writes=[xo_d.b], dma=True)
        p.emit("sync", lambda s: s.nop(), reads=[xo_d.b])
        p.barrier()
    c.prefix, c.ctx = old_pfx, old_ctx


def build_cosformer_prog():
    p = Prog("cosf")
    nc = p.nc
    ctx = contextlib.ExitStack()
    c = Core(p, ctx)
    x_d = c.dram("x_fm", [D, T], F32, kind="ExternalInput")
    xo_d = c.dram("xo_fm", [D, T], F32, kind="ExternalOutput")
    d = decl_cosformer(c)
    with ctx:
        emit_cosformer_stage(c, d, x_d, xo_d, pfx="")
        p.emit("sync", lambda s: s.nop(), reads=[xo_d.b])
        p.build(ctx)
    return nc

def cosformer_host_inputs(x_batch_fm, ins):
    g = lambda k: np.asarray(ins[k], np.float32)
    colify = lambda v: np.ascontiguousarray(v.reshape(DC, P).T)
    ang = (np.pi / (2.0 * T) * np.arange(T, dtype=np.float32)).astype(np.float32)
    cosr = np.cos(ang, dtype=np.float32)
    sinr = np.sin(ang, dtype=np.float32)
    tri = (np.arange(P)[:, None] <= np.arange(P)[None, :]).astype(np.float32)
    id64 = np.tile(np.eye(64, dtype=np.float32), (2, 1))
    return dict(
        x_fm=np.ascontiguousarray(x_batch_fm),
        Wq=np.ascontiguousarray(g("L0_attn_a_Wq")), Wk=np.ascontiguousarray(g("L0_attn_a_Wk")),
        Wv=np.ascontiguousarray(g("L0_attn_a_Wv")), Wo=np.ascontiguousarray(g("L0_attn_a_Wo")),
        bq=colify(g("L0_attn_a_bq")), bk=colify(g("L0_attn_a_bk")),
        bv_row=np.ascontiguousarray(g("L0_attn_a_bv").reshape(1, D)),
        bo=colify(g("L0_attn_a_bo")),
        cosb=np.ascontiguousarray(np.tile(cosr, (P, 1))),
        sinb=np.ascontiguousarray(np.tile(sinr, (P, 1))),
        tri=tri, id64=np.ascontiguousarray(id64),
    )


def decl_moe_prep(c, npfx=""):
    return {
        "pw_d": c.dram(npfx + "pwT", [D, 8], F32, kind="ExternalInput"),
        "pb_d": c.dram(npfx + "pb_row", [1, 8], F32, kind="ExternalInput"),
        "z_d": c.dram(npfx + "z_my", [T, 8], F32, kind="ExternalOutput"),
    }


def emit_moe_prep_stage(c, d, x_d, pfx="zp_"):
    p = c.p
    pw_d = d["pw_d"]
    pb_d = d["pb_d"]
    z_d = d["z_d"]
    old_pfx, old_ctx = c.prefix, c.ctx
    c.prefix = pfx
    with contextlib.ExitStack() as sctx:
        c.ctx = sctx
        sc = make_ln_scratch(c)
        xn = c.sbuf("xn", [P, DC * T], BF16)
        pw_sb = c.sbuf("pw_sb", [P, DC * 8], BF16)
        pbbc = c.sbuf("pbbc", [P, 8], F32)
        zt = c.sbuf("zt", [P, 8], F32)
        psZ = c.psum("psZ", [P, 8], F32)
        init_ln_consts(c, sc)
        p.emit("gpsimd", lambda s: s.dma_start(
            out=pw_sb.t[:].rearrange("p (k f) -> p k f", f=8),
            in_=pw_d.t.rearrange("(k p) f -> p k f", p=P)),
            reads=[pw_d.b], writes=[pw_sb.b], dma=True)
        p.emit("sync", lambda s: s.dma_start(
            out=pbbc.t[:], in_=pb_d.t.partition_broadcast(P).squeeze(1)),
            reads=[pb_d.b], writes=[pbbc.b], dma=True)
        emit_ln(c, sc, x_d, xn)
        for tile in range(16):
            for k in range(DC):
                p.emit("tensor", lambda s, tile=tile, k=k: s.matmul(
                    psZ.t[:], xn.t[:, k * T + tile * P: k * T + (tile + 1) * P],
                    pw_sb.t[:, k * 8:(k + 1) * 8],
                    start=(k == 0), stop=(k == DC - 1)),
                    reads=[xn.b, pw_sb.b], writes=[psZ.b], tick=(k == DC - 1))
            p.emit("vector", lambda s, tile=tile: s.tensor_tensor(
                zt.t[:], psZ.t[:], pbbc.t[:], ALU.add),
                reads=[psZ.b, pbbc.b], writes=[zt.b])
            p.emit("sync", lambda s, tile=tile: s.dma_start(
                out=z_d.t[tile * P:(tile + 1) * P, :], in_=zt.t[:]),
                reads=[zt.b], writes=[z_d.b], dma=True)
        p.emit("sync", lambda s: s.nop(), reads=[z_d.b])
        p.barrier()
    c.prefix, c.ctx = old_pfx, old_ctx


def build_moe_prep_prog():
    p = Prog("moeprep")
    nc = p.nc
    ctx = contextlib.ExitStack()
    c = Core(p, ctx)
    x_d = c.dram("x_fm", [D, T], F32, kind="ExternalInput")
    d = decl_moe_prep(c)
    with ctx:
        emit_moe_prep_stage(c, d, x_d, pfx="")
        p.emit("sync", lambda s: s.nop(), reads=[d["z_d"].b])
        p.build(ctx)
    return nc

def emit_moe_routing(c, z_sb, keepw, sc, consts):
    p = c.p
    E = 8
    CAP = 5120.0
    u1, u2, u3 = consts["u1"], consts["u2"], consts["u3"]
    rowT = consts["rowT"]
    psA = consts["psA"]
    psB = consts["psB"]
    ebias_bc, erow_bc = consts["ebias_bc"], consts["erow_bc"]
    ltmask, ones_col = consts["ltmask"], consts["ones_col"]
    idf = consts["idf"]
    p.emit("scalar", lambda s: s.activation(u1.t[:], z_sb.t[:], AF.Sigmoid),
           reads=[z_sb.b], writes=[u1.b])
    p.emit("vector", lambda s: s.tensor_tensor(u1.t[:], u1.t[:], ebias_bc.t[:], ALU.add),
           reads=[u1.b, ebias_bc.b], writes=[u1.b])
    p.emit("vector", lambda s: s.tensor_scalar_max(u1.t[:], u1.t[:], 1e-12),
           reads=[u1.b], writes=[u1.b])
    p.emit("vector", lambda s: s.tensor_copy(u2.t[:], erow_bc.t[:]),
           reads=[erow_bc.b], writes=[u2.b])
    for a in range(E):
        for b_ in range(a + 1, E):
            p.emit("vector", lambda s, a=a, b_=b_: s.tensor_tensor(
                u3.t[:, 0:1024:8], u1.t[:, b_:1024:8], u1.t[:, a:1024:8], ALU.is_gt),
                reads=[u1.b], writes=[u3.b])
            p.emit("vector", lambda s, a=a: s.tensor_tensor(
                u2.t[:, a:1024:8], u2.t[:, a:1024:8], u3.t[:, 0:1024:8], ALU.add),
                reads=[u2.b, u3.b], writes=[u2.b])
            p.emit("vector", lambda s, b_=b_: s.tensor_tensor(
                u2.t[:, b_:1024:8], u2.t[:, b_:1024:8], u3.t[:, 0:1024:8], ALU.subtract),
                reads=[u2.b, u3.b], writes=[u2.b])
    p.emit("vector", lambda s: s.tensor_single_scalar(
        u2.t[:], u2.t[:], 1.5, ALU.is_le), reads=[u2.b], writes=[u2.b])
    p.emit("vector", lambda s: s.tensor_tensor(u3.t[:], u2.t[:], u1.t[:], ALU.mult),
           reads=[u2.b, u1.b], writes=[u3.b])
    p.emit("vector", lambda s: s.tensor_tensor(
        psB.t[:, 0:P], u3.t[:, 0:1024:8], u3.t[:, 1:1024:8], ALU.add),
        reads=[u3.b], writes=[psB.b])
    for e in range(2, E):
        p.emit("vector", lambda s, e=e: s.tensor_tensor(
            psB.t[:, 0:P], psB.t[:, 0:P], u3.t[:, e:1024:8], ALU.add),
            reads=[psB.b, u3.b], writes=[psB.b])
    p.emit("vector", lambda s: s.tensor_scalar_max(psB.t[:, 0:P], psB.t[:, 0:P], 1e-12),
           reads=[psB.b], writes=[psB.b])
    p.emit("vector", lambda s: s.reciprocal(psB.t[:, P:2 * P], psB.t[:, 0:P]),
           reads=[psB.b], writes=[psB.b])
    for e in range(E):
        p.emit("vector", lambda s, e=e: s.tensor_tensor(
            u3.t[:, e:1024:8], u3.t[:, e:1024:8], psB.t[:, P:2 * P], ALU.mult),
            reads=[u3.b, psB.b], writes=[u3.b])
    p.emit("vector", lambda s: s.tensor_tensor(u1.t[:], u3.t[:], u2.t[:], ALU.add),
           reads=[u3.b, u2.b], writes=[u1.b])
    p.emit("vector", lambda s: s.tensor_scalar_add(u1.t[:], u1.t[:], -1.0),
           reads=[u1.b], writes=[u1.b])
    cand_sb = consts["cand_sb"]
    quota_sb = consts["quota_sb"]
    off_sb = consts["off_sb"]

    def count_ge(cand_cols_off):
        for e in range(E):
            p.emit("vector", lambda s, e=e: s.tensor_scalar(
                u3.t[:, e:1024:8], u1.t[:, e:1024:8],
                cand_sb.t[:, e:e + 1], None, ALU.is_ge),
                reads=[u1.b, cand_sb.b], writes=[u3.b])
        for e in range(E):
            p.emit("vector", lambda s, e=e: s.tensor_reduce(
                u2.t[:, e:e + 1], u3.t[:, e:1024:8], AX.X, ALU.add),
                reads=[u3.b], writes=[u2.b])
        p.emit("tensor", lambda s: s.matmul(
            psB.t[0:1, 256:264], ones_col.t[:], u2.t[:, 0:8], start=True, stop=True),
            reads=[ones_col.b, u2.b], writes=[psB.b], tick=True)
        p.emit("vector", lambda s: s.tensor_copy(rowT.t[0:1, 32:40], psB.t[0:1, 256:264]),
               reads=[psB.b], writes=[rowT.b])

    def bcast_cand():
        p.emit("tensor", lambda s: s.matmul(
            psA.t[:, 0:8], consts["ones_f"].t[:], rowT.t[0:1, 16:24],
            start=True, stop=True),
            reads=[rowT.b, consts["ones_f"].b], writes=[psA.b], tick=True)
        p.emit("vector", lambda s: s.tensor_copy(cand_sb.t[:], psA.t[:, 0:8]),
               reads=[psA.b], writes=[cand_sb.b])

    dbg = consts.get("dbg")
    rnd_i = [0]

    def dbg_snap():
        if dbg is None:
            return
        bisbuf, _ = dbg
        i = rnd_i[0]
        rnd_i[0] += 1
        p.emit("vector", lambda s, i=i: s.tensor_copy(
            bisbuf.t[0:1, i * 48:(i + 1) * 48], rowT.t[0:1, 0:48]),
            reads=[rowT.b], writes=[bisbuf.b])

    p.emit("vector", lambda s: s.memset(rowT.t[0:1, 0:8], 2.0), writes=[rowT.b])
    for bbit in [32, 16, 8, 4, 2, 1]:
        p.emit("vector", lambda s, f=float(2.0 ** (-bbit)): s.tensor_scalar_mul(
            rowT.t[0:1, 16:24], rowT.t[0:1, 0:8], f), reads=[rowT.b], writes=[rowT.b])
        bcast_cand()
        count_ge(512)
        p.emit("vector", lambda s: s.tensor_single_scalar(
            rowT.t[0:1, 24:32], rowT.t[0:1, 32:40], CAP, ALU.is_lt),
            reads=[rowT.b], writes=[rowT.b])
        p.emit("vector", lambda s: s.tensor_tensor(
            rowT.t[0:1, 8:16], rowT.t[0:1, 16:24], rowT.t[0:1, 0:8], ALU.subtract),
            reads=[rowT.b], writes=[rowT.b])
        p.emit("vector", lambda s: s.tensor_tensor(
            rowT.t[0:1, 8:16], rowT.t[0:1, 8:16], rowT.t[0:1, 24:32], ALU.mult),
            reads=[rowT.b], writes=[rowT.b])
        p.emit("vector", lambda s: s.tensor_tensor(
            rowT.t[0:1, 0:8], rowT.t[0:1, 0:8], rowT.t[0:1, 8:16], ALU.add),
            reads=[rowT.b], writes=[rowT.b])
        dbg_snap()
    p.emit("vector", lambda s: s.tensor_scalar_mul(
        rowT.t[0:1, 0:8], rowT.t[0:1, 0:8], 0.5), reads=[rowT.b], writes=[rowT.b])
    p.emit("vector", lambda s: s.tensor_scalar_mul(
        rowT.t[0:1, 8:16], rowT.t[0:1, 0:8], 0.5), reads=[rowT.b], writes=[rowT.b])
    for kbit in range(23):
        p.emit("vector", lambda s: s.tensor_tensor(
            rowT.t[0:1, 16:24], rowT.t[0:1, 0:8], rowT.t[0:1, 8:16], ALU.add),
            reads=[rowT.b], writes=[rowT.b])
        bcast_cand()
        count_ge(512)
        p.emit("vector", lambda s: s.tensor_single_scalar(
            rowT.t[0:1, 24:32], rowT.t[0:1, 32:40], CAP, ALU.is_ge),
            reads=[rowT.b], writes=[rowT.b])
        p.emit("vector", lambda s: s.tensor_tensor(
            rowT.t[0:1, 48:56], rowT.t[0:1, 16:24], rowT.t[0:1, 0:8], ALU.subtract),
            reads=[rowT.b], writes=[rowT.b])
        p.emit("vector", lambda s: s.tensor_tensor(
            rowT.t[0:1, 48:56], rowT.t[0:1, 48:56], rowT.t[0:1, 24:32], ALU.mult),
            reads=[rowT.b], writes=[rowT.b])
        p.emit("vector", lambda s: s.tensor_tensor(
            rowT.t[0:1, 0:8], rowT.t[0:1, 0:8], rowT.t[0:1, 48:56], ALU.add),
            reads=[rowT.b], writes=[rowT.b])
        p.emit("vector", lambda s: s.tensor_scalar_mul(
            rowT.t[0:1, 8:16], rowT.t[0:1, 8:16], 0.5), reads=[rowT.b], writes=[rowT.b])
        dbg_snap()
    p.emit("vector", lambda s: s.tensor_copy(rowT.t[0:1, 16:24], rowT.t[0:1, 0:8]),
           reads=[rowT.b], writes=[rowT.b])
    bcast_cand()
    for e in range(E):
        p.emit("vector", lambda s, e=e: s.tensor_scalar(
            u2.t[:, e:1024:8], u1.t[:, e:1024:8], cand_sb.t[:, e:e + 1],
            None, ALU.is_gt),
            reads=[u1.b, cand_sb.b], writes=[u2.b])
        p.emit("vector", lambda s, e=e: s.tensor_scalar(
            u3.t[:, e:1024:8], u1.t[:, e:1024:8], cand_sb.t[:, e:e + 1],
            None, ALU.is_equal),
            reads=[u1.b, cand_sb.b], writes=[u3.b])
    for e in range(E):
        p.emit("vector", lambda s, e=e: s.tensor_reduce(
            consts["tot"].t[:, e:e + 1], u2.t[:, e:1024:8], AX.X, ALU.add),
            reads=[u2.b], writes=[consts["tot"].b])
    p.emit("tensor", lambda s: s.matmul(
        psB.t[0:1, 256:264], ones_col.t[:], consts["tot"].t[:, 0:8], start=True, stop=True),
        reads=[ones_col.b, consts["tot"].b], writes=[psB.b], tick=True)
    p.emit("vector", lambda s: s.tensor_scalar(
        rowT.t[0:1, 40:48], psB.t[0:1, 256:264], -1.0, CAP, ALU.mult, ALU.add),
        reads=[psB.b], writes=[rowT.b])
    p.emit("tensor", lambda s: s.matmul(
        psA.t[:, 8:16], consts["ones_f"].t[:], rowT.t[0:1, 40:48],
        start=True, stop=True),
        reads=[rowT.b, consts["ones_f"].b], writes=[psA.b], tick=True)
    p.emit("vector", lambda s: s.tensor_copy(quota_sb.t[:], psA.t[:, 8:16]),
           reads=[psA.b], writes=[quota_sb.b])
    for e in range(E):
        p.emit("vector", lambda s, e=e: s.tensor_tensor_scan(
            consts["pfx"].t[:, e:1024:8], u3.t[:, e:1024:8], consts["zeros"].t[:, 0:P],
            0.0, ALU.add, ALU.add),
            reads=[u3.b, consts["zeros"].b], writes=[consts["pfx"].b])
    p.emit("vector", lambda s: s.tensor_copy(
        consts["tot"].t[:, 0:8], consts["pfx"].t[:, 127 * 8:128 * 8]),
        reads=[consts["pfx"].b], writes=[consts["tot"].b])
    p.emit("tensor", lambda s: s.matmul(
        psB.t[:, 384:392], ltmask.t[:], consts["tot"].t[:, 0:8], start=True, stop=True),
        reads=[ltmask.b, consts["tot"].b], writes=[psB.b], tick=True)
    p.emit("vector", lambda s: s.tensor_copy(off_sb.t[:], psB.t[:, 384:392]),
           reads=[psB.b], writes=[off_sb.b])
    p.emit("vector", lambda s: s.tensor_tensor(
        consts["pfx"].t[:], consts["pfx"].t[:], u3.t[:], ALU.subtract),
        reads=[consts["pfx"].b, u3.b], writes=[consts["pfx"].b])
    for e in range(E):
        p.emit("vector", lambda s, e=e: s.tensor_scalar(
            consts["pfx"].t[:, e:1024:8], consts["pfx"].t[:, e:1024:8],
            off_sb.t[:, e:e + 1], None, ALU.add),
            reads=[consts["pfx"].b, off_sb.b], writes=[consts["pfx"].b])
        p.emit("vector", lambda s, e=e: s.tensor_scalar(
            consts["pfx"].t[:, e:1024:8], consts["pfx"].t[:, e:1024:8],
            quota_sb.t[:, e:e + 1], None, ALU.is_lt),
            reads=[consts["pfx"].b, quota_sb.b], writes=[consts["pfx"].b])
    p.emit("vector", lambda s: s.tensor_tensor(
        u3.t[:], u3.t[:], consts["pfx"].t[:], ALU.mult),
        reads=[u3.b, consts["pfx"].b], writes=[u3.b])
    p.emit("vector", lambda s: s.tensor_tensor(u2.t[:], u2.t[:], u3.t[:], ALU.add),
           reads=[u2.b, u3.b], writes=[u2.b])
    p.emit("vector", lambda s: s.tensor_tensor(keepw.t[:], u1.t[:], u2.t[:], ALU.mult),
           reads=[u1.b, u2.b], writes=[keepw.b])

def decl_moe(c, npfx=""):
    EH = 2048
    return {
        "fc1bf": c.dram(npfx + "fc1bf", [8, 16, P, DC * P], BF16),
        "fc2bf": c.dram(npfx + "fc2bf", [8, DC, P, 16 * P], BF16),
        "sh1bf": c.dram(npfx + "sh1bf", [32, P, DC * P], BF16),
        "sh2bf": c.dram(npfx + "sh2bf", [DC, P, 32 * P], BF16),
        "z_d": c.dram(npfx + "z_rot", [P, P * 8], F32, kind="ExternalInput"),
        "eb_d": c.dram(npfx + "ebias_row", [1, 1024], F32, kind="ExternalInput"),
        "er_d": c.dram(npfx + "erow_row", [1, 1024], F32, kind="ExternalInput"),
        "lt_d": c.dram(npfx + "ltmask", [P, P], F32, kind="ExternalInput"),
        "fc1_d": c.dram(npfx + "fc1T", [8, D, EH], F32, kind="ExternalInput"),
        "fc1b_d": c.dram(npfx + "fc1b", [P, 8 * 16], F32, kind="ExternalInput"),
        "fc2_d": c.dram(npfx + "fc2T", [8, EH, D], F32, kind="ExternalInput"),
        "fc2b_d": c.dram(npfx + "fc2b_rows", [1, 8 * D], F32, kind="ExternalInput"),
        "sh1_d": c.dram(npfx + "sh1T", [D, 4096], F32, kind="ExternalInput"),
        "sh1b_d": c.dram(npfx + "sh1b", [P, 32], F32, kind="ExternalInput"),
        "sh2_d": c.dram(npfx + "sh2T", [4096, D], F32, kind="ExternalInput"),
        "sh2b_d": c.dram(npfx + "sh2b", [P, DC], F32, kind="ExternalInput"),
        "idf_d": c.dram(npfx + "idf", [P, P], F32, kind="ExternalInput"),
        "kw_i": c.dram(npfx + "kw_scratch", [16, P, 8], F32),
    }


def emit_moe_stage(c, d, x_d, xo_d, pfx="m_"):
    p = c.p
    EH = 2048
    z_d = d["z_d"]
    eb_d = d["eb_d"]
    er_d = d["er_d"]
    lt_d = d["lt_d"]
    fc1_d = d["fc1_d"]
    fc1b_d = d["fc1b_d"]
    fc2_d = d["fc2_d"]
    fc2b_d = d["fc2b_d"]
    sh1_d = d["sh1_d"]
    sh1b_d = d["sh1b_d"]
    sh2_d = d["sh2_d"]
    sh2b_d = d["sh2b_d"]
    idf_d = d["idf_d"]
    kw_i = d["kw_i"]
    fc1bf, fc2bf = d["fc1bf"], d["fc2bf"]
    sh1bf, sh2bf = d["sh1bf"], d["sh2bf"]
    old_pfx, old_ctx = c.prefix, c.ctx
    c.prefix = pfx
    with contextlib.ExitStack() as sctx:
        c.ctx = sctx
        ps_ab = c.psum("ps_ab", [P, 1024], F32)
        ps_row = c.psum("ps_row", [P, 512], F32)
        ps_bc = c.psum("ps_bc", [P, 512], F32)
        psRtA = c.psum("psRtA", [P, 512], F32)
        psRtB = c.psum("psRtB", [P, 512], F32)
        sc = make_ln_scratch(c, ps_share=dict(
            statm_ps=ps_row, stats_ps=ps_bc,
            bc_a=TB(ps_ab.t[:, 0:512], ps_ab.b), bc_b=TB(ps_ab.t[:, 512:1024], ps_ab.b)))
        xn = c.sbuf("xn", [P, DC * T], BF16)
        z_sb = c.sbuf("z_sb", [P, P * 8], F32)
        u1 = c.sbuf("u1", [P, 1024], F32)
        u2 = c.sbuf("u2", [P, 1024], F32)
        u3 = c.sbuf("u3", [P, 1024], F32)
        pfx = c.sbuf("pfx", [P, 1024], F32)
        keepw = c.sbuf("keepw", [P, 1024], F32)
        tot = c.sbuf("tot", [P, 8], F32)
        rowT = c.sbuf("rowT", [1, 64], F32)
        zeros = c.sbuf("zeros", [P, P], F32)
        ebias_bc = c.sbuf("ebias_bc", [P, 1024], F32)
        erow_bc = c.sbuf("erow_bc", [P, 1024], F32)
        ltm = c.sbuf("ltm", [P, P], F32)
        idf = c.sbuf("idf_sb", [P, P], F32)
        ones_col = c.sbuf("ones_col", [P, 1], F32)
        w_tm = c.sbuf("w_tm", [P, 16 * 8], F32)
        cand_sb = c.sbuf("cand_sb", [P, 8], F32)
        quota_sb = c.sbuf("quota_sb", [P, 8], F32)
        off_sb = c.sbuf("off_sb", [P, 8], F32)
        gbuf = c.sbuf("gbuf", [P, 32 * 512], BF16)
        y_fm = c.sbuf("y_fm", [P, DC * 512], F32)
        b2row = c.sbuf("b2row", [1, 8 * D], BF16)
        cols = c.sbuf("colsb", [P, 32 + DC + 8 * 16], F32)
        wb2 = [c.sbuf(f"wmb{i}", [P, 32 * P], BF16) for i in range(2)]
        t1 = c.sbuf("t1", [P, 512], F32)
        wrow = c.sbuf("wrow", [1, 512], F32)
        wrow_bf = c.sbuf("wrow_bf", [1, 512], BF16)
        xrt = [c.sbuf("xra", [P, 512], F32), c.sbuf("xrbb", [P, 512], F32)]
        xout = c.sbuf("xout", [P, 512], F32)
        psF = c.psum("psF", [P, 512], F32)
        psW = ps_bc
        psH = ps_row

        init_ln_consts(c, sc)
        ones_f = sc["ones_f32"]

        p.emit("sync", lambda s: s.dma_start(out=z_sb.t[:], in_=z_d.t),
               reads=[z_d.b], writes=[z_sb.b], dma=True)
        for dst, src_d in [(ltm, lt_d), (idf, idf_d)]:
            p.emit("sync", lambda s, dst=dst, src_d=src_d: s.dma_start(
                out=dst.t[:], in_=src_d.t), reads=[src_d.b], writes=[dst.b], dma=True)
        for dst, src_d in [(ebias_bc, eb_d), (erow_bc, er_d)]:
            p.emit("sync", lambda s, dst=dst, src_d=src_d: s.dma_start(
                out=dst.t[:], in_=src_d.t.partition_broadcast(P).squeeze(1)),
                reads=[src_d.b], writes=[dst.b], dma=True)
        p.emit("gpsimd", lambda s: s.dma_start(out=b2row.t[:], in_=fc2b_d.t),
               reads=[fc2b_d.b], writes=[b2row.b], dma=True)
        p.emit("sync", lambda s: s.dma_start(out=cols.t[:, 0:32], in_=sh1b_d.t),
               reads=[sh1b_d.b], writes=[cols.b], dma=True)
        p.emit("sync", lambda s: s.dma_start(out=cols.t[:, 32:32 + DC], in_=sh2b_d.t),
               reads=[sh2b_d.b], writes=[cols.b], dma=True)
        p.emit("sync", lambda s: s.dma_start(out=cols.t[:, 40:40 + 128], in_=fc1b_d.t),
               reads=[fc1b_d.b], writes=[cols.b], dma=True)
        p.emit("vector", lambda s: s.memset(zeros.t[:], 0.0), writes=[zeros.b])
        p.emit("vector", lambda s: s.memset(ones_col.t[:], 1.0), writes=[ones_col.b])

        emit_ln(c, sc, x_d, xn)

        consts = dict(dbg=None, u1=u1, u2=u2, u3=u3, rowT=rowT, psA=psRtA, psB=psRtB,
                      ebias_bc=ebias_bc, erow_bc=erow_bc, ltmask=ltm,
                      ones_col=ones_col, idf=idf, pfx=pfx, tot=tot, zeros=zeros,
                      ones_f=ones_f, cand_sb=cand_sb, quota_sb=quota_sb,
                      off_sb=off_sb)
        p.paranoid_now = True
        emit_moe_routing(c, TB(z_sb.t[:], z_sb.b), keepw, sc, consts)
        p.paranoid_now = False

        p.emit("sync", lambda s: s.dma_start(
            out=kw_i.t.rearrange("tile r e -> tile (r e)"), in_=keepw.t[0:16, :]),
            reads=[keepw.b], writes=[kw_i.b], dma=True)
        p.emit("sync", lambda s: s.dma_start(
            out=w_tm.t[:].rearrange("r (tile e) -> r tile e", e=8),
            in_=kw_i.t.rearrange("tile r e -> r tile e")),
            reads=[kw_i.b], writes=[w_tm.b], dma=True)

        for u in range(4):
            for fo in range(32):
                wb = wb2[fo % 2]
                if u == 0:
                    p.emit("gpsimd", lambda s, fo=fo, wb=wb: s.dma_start(
                        out=wb.t[:, 0:DC * P].rearrange("p (k f) -> p k f", f=P),
                        in_=sh1_d.t.rearrange("(k p) f -> p k f", p=P)[:, :, fo * P:(fo + 1) * P]),
                        reads=[sh1_d.b], writes=[wb.b], dma=True)
                    p.emit("sync", lambda s, fo=fo, wb=wb: s.dma_start(
                        out=sh1bf.t[fo], in_=wb.t[:, 0:DC * P]),
                        reads=[wb.b], writes=[sh1bf.b], dma=True)
                else:
                    p.emit("gpsimd", lambda s, fo=fo, wb=wb: s.dma_start(
                        out=wb.t[:, 0:DC * P], in_=sh1bf.t[fo]),
                        reads=[sh1bf.b], writes=[wb.b], dma=True)
                for k in range(DC):
                    p.emit("tensor", lambda s, fo=fo, u=u, k=k, wb=wb: s.matmul(
                        psH.t[:], wb.t[:, k * P:(k + 1) * P],
                        xn.t[:, k * T + u * 512: k * T + u * 512 + 512],
                        start=(k == 0), stop=(k == DC - 1)),
                        reads=[wb.b, xn.b], writes=[psH.b], tick=(k == DC - 1))
                p.emit("scalar", lambda s, fo=fo: s.activation(
                    gbuf.t[:, fo * 512:(fo + 1) * 512], psH.t[:], AF.Gelu,
                    bias=cols.t[:, fo:fo + 1]),
                    reads=[psH.b, cols.b], writes=[gbuf.b])
            for fo in range(DC):
                wb = wb2[fo % 2]
                if u == 0:
                    p.emit("gpsimd", lambda s, fo=fo, wb=wb: s.dma_start(
                        out=wb.t[:].rearrange("p (k f) -> p k f", f=P),
                        in_=sh2_d.t.rearrange("(k p) f -> p k f", p=P)[:, :, fo * P:(fo + 1) * P]),
                        reads=[sh2_d.b], writes=[wb.b], dma=True)
                    p.emit("sync", lambda s, fo=fo, wb=wb: s.dma_start(
                        out=sh2bf.t[fo], in_=wb.t[:]),
                        reads=[wb.b], writes=[sh2bf.b], dma=True)
                else:
                    p.emit("gpsimd", lambda s, fo=fo, wb=wb: s.dma_start(
                        out=wb.t[:], in_=sh2bf.t[fo]),
                        reads=[sh2bf.b], writes=[wb.b], dma=True)
                for k in range(32):
                    p.emit("tensor", lambda s, fo=fo, k=k, wb=wb: s.matmul(
                        psF.t[:], wb.t[:, k * P:(k + 1) * P],
                        gbuf.t[:, k * 512:(k + 1) * 512],
                        start=(k == 0), stop=(k == 31)),
                        reads=[wb.b, gbuf.b], writes=[psF.b], tick=(k == 31))
                p.emit("vector", lambda s, fo=fo: s.tensor_scalar_add(
                    y_fm.t[:, fo * 512:(fo + 1) * 512], psF.t[:],
                    cols.t[:, 32 + fo:32 + fo + 1]),
                    reads=[psF.b, cols.b], writes=[y_fm.b])
            for e in range(8):
                for qq in range(4):
                    tile = u * 4 + qq
                    p.emit("tensor", lambda s, tile=tile, qq=qq, e=e: s.matmul(
                        psW.t[0:1, qq * P:(qq + 1) * P],
                        w_tm.t[:, tile * 8 + e:tile * 8 + e + 1], idf.t[:],
                        start=(qq == 0), stop=(qq == 3), skip_group_check=True),
                        reads=[w_tm.b, idf.b], writes=[psW.b], tick=(qq == 3))
                p.emit("vector", lambda s: s.tensor_copy(wrow.t[0:1, :], psW.t[0:1, 0:512]),
                       reads=[psW.b], writes=[wrow.b])
                p.emit("vector", lambda s: s.tensor_copy(wrow_bf.t[0:1, :], wrow.t[0:1, :]),
                       reads=[wrow.b], writes=[wrow_bf.b])
                p.emit("tensor", lambda s: s.matmul(
                    psW.t[:, 0:512], ones_f.t[:], wrow.t[0:1, :], start=True, stop=True),
                    reads=[ones_f.b, wrow.b], writes=[psW.b], tick=True)
                for fo in range(16):
                    wb = wb2[fo % 2]
                    if u == 0:
                        p.emit("gpsimd", lambda s, fo=fo, wb=wb, e=e: s.dma_start(
                            out=wb.t[:, 0:DC * P].rearrange("p (k f) -> p k f", f=P),
                            in_=fc1_d.t[e].rearrange("(k p) f -> p k f", p=P)[:, :, fo * P:(fo + 1) * P]),
                            reads=[fc1_d.b], writes=[wb.b], dma=True)
                        p.emit("sync", lambda s, fo=fo, wb=wb, e=e: s.dma_start(
                            out=fc1bf.t[e, fo], in_=wb.t[:, 0:DC * P]),
                            reads=[wb.b], writes=[fc1bf.b], dma=True)
                    else:
                        p.emit("gpsimd", lambda s, fo=fo, wb=wb, e=e: s.dma_start(
                            out=wb.t[:, 0:DC * P], in_=fc1bf.t[e, fo]),
                            reads=[fc1bf.b], writes=[wb.b], dma=True)
                    for k in range(DC):
                        p.emit("tensor", lambda s, fo=fo, u=u, k=k, wb=wb: s.matmul(
                            psH.t[:], wb.t[:, k * P:(k + 1) * P],
                            xn.t[:, k * T + u * 512: k * T + u * 512 + 512],
                            start=(k == 0), stop=(k == DC - 1)),
                            reads=[wb.b, xn.b], writes=[psH.b], tick=(k == DC - 1))
                    p.emit("scalar", lambda s, fo=fo, e=e: s.activation(
                        t1.t[:], psH.t[:], AF.Gelu,
                        bias=cols.t[:, 40 + e * 16 + fo:40 + e * 16 + fo + 1]),
                        reads=[psH.b, cols.b], writes=[t1.b])
                    p.emit("vector", lambda s, fo=fo: s.tensor_tensor(
                        gbuf.t[:, fo * 512:(fo + 1) * 512], t1.t[:], psW.t[:, 0:512],
                        ALU.mult),
                        reads=[t1.b, psW.b], writes=[gbuf.b])
                for fo in range(DC):
                    wb = wb2[fo % 2]
                    if u == 0:
                        p.emit("gpsimd", lambda s, fo=fo, wb=wb, e=e: s.dma_start(
                            out=wb.t[:, 0:16 * P].rearrange("p (k f) -> p k f", f=P),
                            in_=fc2_d.t[e].rearrange("(k p) f -> p k f", p=P)[:, :, fo * P:(fo + 1) * P]),
                            reads=[fc2_d.b], writes=[wb.b], dma=True)
                        p.emit("sync", lambda s, fo=fo, wb=wb, e=e: s.dma_start(
                            out=fc2bf.t[e, fo], in_=wb.t[:, 0:16 * P]),
                            reads=[wb.b], writes=[fc2bf.b], dma=True)
                    else:
                        p.emit("gpsimd", lambda s, fo=fo, wb=wb, e=e: s.dma_start(
                            out=wb.t[:, 0:16 * P], in_=fc2bf.t[e, fo]),
                            reads=[fc2bf.b], writes=[wb.b], dma=True)
                    for k in range(16):
                        p.emit("tensor", lambda s, fo=fo, k=k, wb=wb: s.matmul(
                            psF.t[:], wb.t[:, k * P:(k + 1) * P],
                            gbuf.t[:, k * 512:(k + 1) * 512],
                            start=(k == 0), stop=False),
                            reads=[wb.b, gbuf.b], writes=[psF.b])
                    p.emit("tensor", lambda s, fo=fo, e=e: s.matmul(
                        psF.t[:], b2row.t[0:1, e * D + fo * P: e * D + (fo + 1) * P],
                        wrow_bf.t[0:1, :],
                        start=False, stop=True),
                        reads=[b2row.b, wrow_bf.b], writes=[psF.b], tick=True)
                    p.emit("vector", lambda s, fo=fo: s.tensor_tensor(
                        y_fm.t[:, fo * 512:(fo + 1) * 512],
                        y_fm.t[:, fo * 512:(fo + 1) * 512], psF.t[:], ALU.add),
                        reads=[y_fm.b, psF.b], writes=[y_fm.b])
            for fo in range(DC):
                par = fo % 2
                xs = xrt[par].t[:, 0:512]
                p.emit("sync", lambda s, fo=fo, u=u, xs=xs: s.dma_start(
                    out=xs, in_=x_d.t[fo * P:(fo + 1) * P, u * 512:(u + 1) * 512]),
                    reads=[x_d.b], writes=[xrt[par].b], dma=True)
                p.emit("vector", lambda s, fo=fo, xs=xs, par=par: s.tensor_tensor(
                    xout.t[:], y_fm.t[:, fo * 512:(fo + 1) * 512], xs, ALU.add),
                    reads=[y_fm.b, xrt[par].b], writes=[xout.b])
                p.emit("sync", lambda s, fo=fo, u=u: s.dma_start(
                    out=xo_d.t[fo * P:(fo + 1) * P, u * 512:(u + 1) * 512], in_=xout.t[:]),
                    reads=[xout.b], writes=[xo_d.b], dma=True)
        p.emit("sync", lambda s: s.nop(), reads=[xo_d.b])
        p.barrier()
    c.prefix, c.ctx = old_pfx, old_ctx


def build_moe_prog():
    p = Prog("moe")
    nc = p.nc
    ctx = contextlib.ExitStack()
    c = Core(p, ctx)
    x_d = c.dram("x_fm", [D, T], F32, kind="ExternalInput")
    xo_d = c.dram("xo_fm", [D, T], F32, kind="ExternalOutput")
    d = decl_moe(c)
    with ctx:
        emit_moe_stage(c, d, x_d, xo_d, pfx="")
        p.emit("sync", lambda s: s.nop(), reads=[xo_d.b])
        p.build(ctx)
    return nc

def moe_prep_host_inputs(x_batch_fm, ins):
    g = lambda k: np.asarray(ins[k], np.float32)
    return dict(
        x_fm=np.ascontiguousarray(x_batch_fm),
        pwT=np.ascontiguousarray(g("L1_ffn_proj_w").T),
        pb_row=np.ascontiguousarray(g("L1_ffn_proj_b").reshape(1, 8)),
    )


def moe_host_inputs(x_batch_fm, z_full, core_id, ins):
    g = lambda k: np.asarray(ins[k], np.float32)
    zr = np.roll(z_full.reshape(P, P, 8), -16 * core_id, axis=0)
    p_true = (np.arange(P) + 16 * core_id) % P
    ltm = (p_true[:, None] < p_true[None, :]).astype(np.float32)
    fc1 = g("L1_ffn_fc1_w")
    fc2 = g("L1_ffn_fc2_w")
    fc1b = g("L1_ffn_fc1_b")
    fc2b = g("L1_ffn_fc2_b")
    return dict(
        x_fm=np.ascontiguousarray(x_batch_fm),
        z_rot=np.ascontiguousarray(zr.reshape(P, P * 8)),
        ebias_row=np.ascontiguousarray(np.tile(g("L1_ffn_expert_bias"), P).reshape(1, 1024)),
        erow_row=np.ascontiguousarray(np.tile(np.arange(8, dtype=np.float32), P).reshape(1, 1024)),
        ltmask=ltm,
        fc1T=np.ascontiguousarray(fc1.transpose(0, 2, 1)),
        fc1b=np.ascontiguousarray(fc1b.reshape(8, 16, P).transpose(2, 0, 1).reshape(P, 128)),
        fc2T=np.ascontiguousarray(fc2.transpose(0, 2, 1)),
        fc2b_rows=np.ascontiguousarray(fc2b.reshape(1, 8 * D)),
        sh1T=np.ascontiguousarray(g("L1_ffn_sh1_w").T),
        sh1b=np.ascontiguousarray(g("L1_ffn_sh1_b").reshape(32, P).T),
        sh2T=np.ascontiguousarray(g("L1_ffn_sh2_w").T),
        sh2b=np.ascontiguousarray(g("L1_ffn_sh2_b").reshape(DC, P).T),
        idf=np.eye(P, dtype=np.float32),
    )


_PROG_CACHE = {}


def _prog(name, builder):
    if name not in _PROG_CACHE:
        _PROG_CACHE[name] = builder()
    return _PROG_CACHE[name]


def kernel_multilaunch(**inputs):
    cores = list(range(NCORES))
    x = np.asarray(inputs["L0_attn_a_x"], np.float32)
    x_fm = [to_fm(x[b]) for b in range(NCORES)]

    def run(name, builder, host_fn):
        nc = _prog(name, builder)
        in_maps = [host_fn(b) for b in range(NCORES)]
        res = run_bass_kernel_spmd(nc, in_maps, core_ids=cores)
        return ([np.asarray(res.results[b]["xo_fm"]) for b in range(NCORES)],)

    x_fm = run("cosf", build_cosformer_prog,
               lambda b: cosformer_host_inputs(x_fm[b], inputs))[0]
    x_fm = run("ffn0", build_ffn0_prog,
               lambda b: ffn0_host_inputs(x_fm[b], inputs, "L0"))[0]
    x_fm = run("attnb", build_attn_b_prog,
               lambda b: attn_b_host_inputs(x_fm[b], inputs))[0]
    x_fm = run("ffn0", build_ffn0_prog,
               lambda b: ffn0_host_inputs(x_fm[b], inputs, "L0"))[0]
    x_fm = run("attna", build_attn_a_prog,
               lambda b: attn_a_host_inputs(x_fm[b], inputs))[0]

    def moe_pass(x_fm):
        nc1 = _prog("moeprep", build_moe_prep_prog)
        in1 = [moe_prep_host_inputs(x_fm[b], inputs) for b in range(NCORES)]
        r1 = run_bass_kernel_spmd(nc1, in1, core_ids=cores)
        z_full = np.concatenate([np.asarray(r1.results[b]["z_my"]) for b in range(NCORES)], axis=0)
        nc2 = _prog("moe", build_moe_prog)
        in2 = [moe_host_inputs(x_fm[b], z_full, b, inputs) for b in range(NCORES)]
        r2 = run_bass_kernel_spmd(nc2, in2, core_ids=cores)
        return [np.asarray(r2.results[b]["xo_fm"]) for b in range(NCORES)]

    x_fm = moe_pass(x_fm)
    x_fm = run("attnb1", build_attn_b1_prog,
               lambda b: attn_b1_host_inputs(x_fm[b], inputs))[0]
    x_fm = moe_pass(x_fm)

    return np.stack([from_fm(f) for f in x_fm], axis=0)


def build_L1_prog():
    p = Prog("L1")
    nc = p.nc
    ctx = contextlib.ExitStack()
    c = Core(p, ctx)
    x_d = c.dram("x_fm", [D, T], F32, kind="ExternalInput")
    xo_d = c.dram("xo_fm", [D, T], F32, kind="ExternalOutput")
    d1 = decl_cosformer(c, "s1_")
    d2 = decl_ffn0(c, "ff_")
    d3 = decl_attn_b(c, "s3_")
    d5 = decl_attn_a(c, "s5_")
    dz = decl_moe_prep(c, "zp_")
    xa = c.dram("xa_i", [D, T], F32)
    xb = c.dram("xb_i", [D, T], F32)
    with ctx:
        emit_cosformer_stage(c, d1, x_d, xa, pfx="s1_")
        emit_ffn0_stage(c, d2, xa, xb, pfx="s2_", first_use=True)
        emit_attn_b_stage(c, d3, xb, xa, pfx="s3_")
        emit_ffn0_stage(c, d2, xa, xb, pfx="s4_", first_use=False)
        emit_attn_a_stage(c, d5, xb, xo_d, pfx="s5_")
        emit_moe_prep_stage(c, dz, xo_d, pfx="zp_")
        p.emit("sync", lambda s: s.nop(), reads=[xo_d.b, dz["z_d"].b])
        p.build(ctx)
    return nc


def build_L2_prog():
    p = Prog("L2")
    nc = p.nc
    ctx = contextlib.ExitStack()
    c = Core(p, ctx)
    x_d = c.dram("x_fm", [D, T], F32, kind="ExternalInput")
    xo_d = c.dram("xo_fm", [D, T], F32, kind="ExternalOutput")
    d6 = decl_moe(c, "m6_")
    d7 = decl_attn_b1(c, "s7_")
    dz = decl_moe_prep(c, "zp_")
    xa = c.dram("xa_i", [D, T], F32)
    with ctx:
        emit_moe_stage(c, d6, x_d, xa, pfx="s6_")
        emit_attn_b1_stage(c, d7, xa, xo_d, pfx="s7_")
        emit_moe_prep_stage(c, dz, xo_d, pfx="zp_")
        p.emit("sync", lambda s: s.nop(), reads=[xo_d.b, dz["z_d"].b])
        p.build(ctx)
    return nc


def build_L3_prog():
    p = Prog("L3")
    nc = p.nc
    ctx = contextlib.ExitStack()
    c = Core(p, ctx)
    x_d = c.dram("x_fm", [D, T], F32, kind="ExternalInput")
    xo_d = c.dram("xo_fm", [D, T], F32, kind="ExternalOutput")
    d8 = decl_moe(c, "m8_")
    with ctx:
        emit_moe_stage(c, d8, x_d, xo_d, pfx="s8_")
        p.emit("sync", lambda s: s.nop(), reads=[xo_d.b])
        p.build(ctx)
    return nc


def _pref(dic, npfx):
    return {npfx + k: v for k, v in dic.items() if k != "x_fm"}


def kernel(**inputs):
    cores = list(range(NCORES))
    x = np.asarray(inputs["L0_attn_a_x"], np.float32)
    x_fm = [to_fm(x[b]) for b in range(NCORES)]

    nc1 = _prog("L1", build_L1_prog)
    in1 = []
    for b in range(NCORES):
        m = dict(x_fm=np.ascontiguousarray(x_fm[b]))
        m.update(_pref(cosformer_host_inputs(x_fm[b], inputs), "s1_"))
        m.update(_pref(ffn0_host_inputs(x_fm[b], inputs, "L0"), "ff_"))
        m.update(_pref(attn_b_host_inputs(x_fm[b], inputs), "s3_"))
        m.update(_pref(attn_a_host_inputs(x_fm[b], inputs), "s5_"))
        m.update(_pref(moe_prep_host_inputs(x_fm[b], inputs), "zp_"))
        in1.append(m)
    r1 = run_bass_kernel_spmd(nc1, in1, core_ids=cores)
    x5 = [np.asarray(r1.results[b]["xo_fm"]) for b in range(NCORES)]
    z6 = np.concatenate([np.asarray(r1.results[b]["zp_z_my"]) for b in range(NCORES)], 0)

    nc2 = _prog("L2", build_L2_prog)
    in2 = []
    for b in range(NCORES):
        m = dict(x_fm=np.ascontiguousarray(x5[b]))
        m.update(_pref(moe_host_inputs(x5[b], z6, b, inputs), "m6_"))
        m.update(_pref(attn_b1_host_inputs(x5[b], inputs), "s7_"))
        m.update(_pref(moe_prep_host_inputs(x5[b], inputs), "zp_"))
        in2.append(m)
    r2 = run_bass_kernel_spmd(nc2, in2, core_ids=cores)
    x7 = [np.asarray(r2.results[b]["xo_fm"]) for b in range(NCORES)]
    z8 = np.concatenate([np.asarray(r2.results[b]["zp_z_my"]) for b in range(NCORES)], 0)

    nc3 = _prog("L3", build_L3_prog)
    in3 = []
    for b in range(NCORES):
        m = dict(x_fm=np.ascontiguousarray(x7[b]))
        m.update(_pref(moe_host_inputs(x7[b], z8, b, inputs), "m8_"))
        in3.append(m)
    r3 = run_bass_kernel_spmd(nc3, in3, core_ids=cores)
    return np.stack([from_fm(np.asarray(r3.results[b]["xo_fm"])) for b in range(NCORES)], 0)


# version 11 (final)
# speedup vs baseline: 1.1235x; 1.1235x over previous
import contextlib
import math

import numpy as np
import ml_dtypes

import concourse.bass as bass
import concourse.mybir as mybir
from concourse.bass import IndirectOffsetOnAxis
from concourse.bass_utils import run_bass_kernel_spmd

F32 = mybir.dt.float32
BF16 = mybir.dt.bfloat16
U32 = mybir.dt.uint32
AF = mybir.ActivationFunctionType
ALU = mybir.AluOpType
AX = mybir.AxisListType

NCORES = 8
B, T, D = 8, 2048, 1024
NH, HD = 16, 64
P = 128
DC = D // P

ENGINES = ("sync", "tensor", "vector", "scalar", "gpsimd")
DEBUG_ATTNB = False
DEBUG_MOE = False
PARANOID = True
PARANOID_ENGS = {"vector", "scalar", "gpsimd"}


class Buf:
    __slots__ = ("name", "writer", "readers")

    def __init__(self, name):
        self.name = name
        self.writer = None
        self.readers = {}


class Prog:

    def __init__(self, name="prog"):
        self.nc = bass.Bass("TRN2", target_bir_lowering=False, num_devices=8)
        self.name = name
        self.ops = {e: [] for e in ENGINES}
        self.tick = {e: 0 for e in ENGINES}
        self.pending = {e: False for e in ENGINES}
        for e in ENGINES:
            self.tick[e + "_dma"] = 0
        self.waited = {}
        self.sems = {}
        self.nbuf = 0

    def buf(self, name=None):
        self.nbuf += 1
        return Buf(name or f"b{self.nbuf}")

    def _force_tick(self, eng):
        if not self.pending[eng]:
            return
        self.tick[eng] += 1
        self.pending[eng] = False
        self.ops[eng].append(("inst", lambda s: s.nop(), eng, None))

    def _require(self, prod_key, value):
        if prod_key.endswith("_dma"):
            return
        if value > self.tick[prod_key]:
            self._force_tick(prod_key)
            assert value <= self.tick[prod_key], (prod_key, value, self.tick[prod_key])

    def _wait(self, eng, prod_key, value):
        if prod_key == eng:
            return
        if prod_key.endswith("_dma"):
            value = self.tick[prod_key]
        self._require(prod_key, value)
        key = (eng, prod_key)
        if self.waited.get(key, 0) >= value:
            return
        self.waited[key] = value
        self.ops[eng].append(("wait", prod_key, value))

    def emit(self, eng, f, reads=(), writes=(), tick=None, dma=False):
        if tick is None:
            tick = eng != "tensor"
        if (PARANOID or getattr(self, "paranoid_now", False)) and not dma \
                and eng in PARANOID_ENGS:
            tick = True
            if self.tick[eng] > 0:
                self.ops[eng].append(("wait", eng, self.tick[eng]))
        for b in reads:
            if b.writer is not None:
                self._wait(eng, *b.writer)
        for b in writes:
            if b.writer is not None:
                self._wait(eng, *b.writer)
            for pk, v in list(b.readers.items()):
                self._wait(eng, pk, v)
        if dma:
            prod = eng + "_dma"
            self.tick[prod] += 16
            myv = self.tick[prod]
            self.ops[eng].append(("dma", f, prod, myv))
            my_key = (prod, myv)
        elif tick:
            self.tick[eng] += 1
            self.pending[eng] = False
            myv = self.tick[eng]
            self.ops[eng].append(("inst", f, eng, None))
            my_key = (eng, myv)
        else:
            self.pending[eng] = True
            self.ops[eng].append(("inst", f, None, None))
            my_key = (eng, self.tick[eng] + 1)
        for b in reads:
            prev = b.readers.get(my_key[0], 0)
            b.readers[my_key[0]] = max(prev, my_key[1])
        for b in writes:
            b.writer = my_key
            b.readers = {}
        return my_key

    def barrier(self):
        for e in ENGINES:
            self._force_tick(e)
        for e in ENGINES:
            if e == "sync":
                continue
            if self.tick[e] > 0:
                self.ops["sync"].append(("wait", e, self.tick[e]))
                self.waited[("sync", e)] = self.tick[e]
            ed = e + "_dma"
            if self.tick[ed] > 0:
                self.ops["sync"].append(("wait", ed, self.tick[ed]))
                self.waited[("sync", ed)] = self.tick[ed]
        if self.tick["sync_dma"] > 0:
            self.ops["sync"].append(("wait", "sync_dma", self.tick["sync_dma"]))
            self.waited[("sync", "sync_dma")] = self.tick["sync_dma"]
        self.tick["sync"] += 1
        bar = self.tick["sync"]
        self.ops["sync"].append(("inst", lambda s: s.nop(), "sync", None))
        for e in ENGINES:
            if e == "sync":
                continue
            self.ops[e].append(("wait", "sync", bar))
            self.waited[(e, "sync")] = bar
        return bar

    def build(self, ctx):
        nc = self.nc
        for e in ENGINES:
            self.sems[e] = ctx.enter_context(nc.semaphore("sm_" + e))
            self.sems[e + "_dma"] = ctx.enter_context(nc.semaphore("smd_" + e))
        block = ctx.enter_context(nc.Block())
        prog = self

        def mk_section(eng):
            def section(seq):
                for op in prog.ops[eng]:
                    if op[0] == "wait":
                        seq.wait_ge(prog.sems[op[1]], op[2])
                    elif op[0] == "inst":
                        inst = op[1](seq)
                        if op[2] is not None:
                            inst.then_inc(prog.sems[op[2]], 1)
                    else:
                        op[1](seq).then_inc(prog.sems[op[2]], 16)
            return section

        sec = {e: mk_section(e) for e in ENGINES}

        @block.sync
        def _(sync):
            sec["sync"](sync)

        @block.tensor
        def _(tensor):
            sec["tensor"](tensor)

        @block.vector
        def _(vector):
            sec["vector"](vector)

        @block.scalar
        def _(scalar):
            sec["scalar"](scalar)

        @block.gpsimd
        def _(gpsimd):
            sec["gpsimd"](gpsimd)


class TB:
    __slots__ = ("t", "b")

    def __init__(self, t, b):
        self.t = t
        self.b = b


class Core:

    def __init__(self, p: Prog, ctx):
        self.p = p
        self.nc = p.nc
        self.ctx = ctx
        self.sb_bytes = 0
        self.psum_banks = 0

    prefix = ""

    def sbuf(self, name, shape, dtype):
        t = self.ctx.enter_context(self.nc.sbuf_tensor(self.prefix + name, shape, dtype))
        bpe = 2 if dtype in (BF16, mybir.dt.float16) else 4
        self.sb_bytes += int(np.prod(shape[1:])) * bpe
        return TB(t, self.p.buf(name))

    def psum(self, name, shape, dtype=F32):
        t = self.ctx.enter_context(self.nc.psum_tensor(self.prefix + name, shape, dtype))
        bpe = 2 if dtype == BF16 else 4
        self.psum_banks += math.ceil(int(np.prod(shape[1:])) * bpe / 2048)
        return TB(t, self.p.buf(name))

    def dram(self, name, shape, dtype, kind=None):
        if kind:
            t = self.nc.dram_tensor(name, shape, dtype, kind=kind)
        else:
            t = self.nc.dram_tensor(name, shape, dtype)
        return TB(t.ap(), self.p.buf(name))

    def dram_i(self, name, shape, dtype):
        t = self.nc.dram_tensor(self.prefix + name, shape, dtype)
        return TB(t.ap(), self.p.buf(self.prefix + name))



def emit_ln(c, sc, x_dram, xn, n_tok=T, eps=1e-5, gamma=None, beta=None,
            keep_rows=None):
    p = c.p
    xa, xa_b = sc["xchunk"].t, sc["xchunk"].b
    cast, sq = sc["cast"], sc["sq"]
    ones_bf, ones_f = sc["ones_bf"], sc["ones_f32"]
    stm_ps, sts_ps = sc["statm_ps"], sc["stats_ps"]
    bc_a, bc_b = sc["bc_a"], sc["bc_b"]
    rows, rtmp, tmp = sc["rows"], sc["rtmp"], sc["tmp32"]
    NU = n_tok // 512
    xbufs = sc["xchunk_bufs"]
    for u in range(NU):
        for ch in range(DC):
            par = ch % 2
            xs = xa[:, par * 512:(par + 1) * 512]
            xb = xbufs[par]
            p.emit("sync", lambda s, ch=ch, u=u, xs=xs: s.dma_start(
                out=xs, in_=x_dram.t[ch * P:(ch + 1) * P, u * 512:(u + 1) * 512]),
                reads=[x_dram.b], writes=[xb], dma=True)
            p.emit("vector", lambda s, xs=xs: s.tensor_copy(cast.t[:], xs),
                   reads=[xb], writes=[cast.b])
            p.emit("scalar", lambda s, xs=xs: s.activation(sq.t[:], xs, AF.Square),
                   reads=[xb], writes=[sq.b])
            p.emit("tensor", lambda s, ch=ch: s.matmul(
                stm_ps.t[0:1, 0:512], ones_bf.t[:], cast.t[:],
                start=(ch == 0), stop=(ch == DC - 1)),
                reads=[cast.b, ones_bf.b], writes=[stm_ps.b], tick=(ch == DC - 1))
            p.emit("tensor", lambda s, ch=ch: s.matmul(
                sts_ps.t[0:1, 0:512], ones_bf.t[:], sq.t[:],
                start=(ch == 0), stop=(ch == DC - 1)),
                reads=[sq.b, ones_bf.b], writes=[sts_ps.b], tick=(ch == DC - 1))
        sl = slice(0, 512)
        slb = slice(512, 1024)
        usl = slice(u * 512, (u + 1) * 512)
        p.emit("vector", lambda s, slb=slb: s.tensor_scalar_mul(
            rows.t[0:1, slb], stm_ps.t[0:1, 0:512], 1.0 / D),
            reads=[stm_ps.b], writes=[rows.b])
        p.emit("vector", lambda s, sl=sl: s.tensor_scalar_mul(
            rows.t[0:1, sl], sts_ps.t[0:1, 0:512], 1.0 / D),
            reads=[sts_ps.b], writes=[rows.b])
        p.emit("vector", lambda s, slb=slb: s.tensor_tensor(
            rtmp.t[0:1, 0:512], rows.t[0:1, slb], rows.t[0:1, slb], ALU.mult),
            reads=[rows.b], writes=[rtmp.b])
        p.emit("vector", lambda s, sl=sl: s.tensor_tensor(
            rows.t[0:1, sl], rows.t[0:1, sl], rtmp.t[0:1, 0:512], ALU.subtract),
            reads=[rows.b, rtmp.b], writes=[rows.b])
        p.emit("scalar", lambda s, sl=sl: s.activation(
            rows.t[0:1, sl], rows.t[0:1, sl], AF.Sqrt, bias=sc["eps"].t[0:1, 0:1]),
            reads=[rows.b, sc["eps"].b], writes=[rows.b])
        p.emit("vector", lambda s, sl=sl: s.reciprocal(rows.t[0:1, sl], rows.t[0:1, sl]),
               reads=[rows.b], writes=[rows.b])
        p.emit("vector", lambda s, sl=sl, slb=slb: s.tensor_tensor(
            rows.t[0:1, slb], rows.t[0:1, slb], rows.t[0:1, sl], ALU.mult),
            reads=[rows.b], writes=[rows.b])
        p.emit("vector", lambda s, slb=slb: s.tensor_scalar_mul(
            rows.t[0:1, slb], rows.t[0:1, slb], -1.0), reads=[rows.b], writes=[rows.b])
        if keep_rows is not None:
            p.emit("vector", lambda s, usl=usl: s.tensor_copy(
                keep_rows.t[0:1, usl], rows.t[0:1, 0:512]),
                reads=[rows.b], writes=[keep_rows.b])
            p.emit("vector", lambda s, usl=usl: s.tensor_copy(
                keep_rows.t[0:1, n_tok + usl.start:n_tok + usl.stop], rows.t[0:1, 512:1024]),
                reads=[rows.b], writes=[keep_rows.b])
        p.emit("tensor", lambda s, sl=sl: s.matmul(
            bc_a.t[:], ones_f.t[:], rows.t[0:1, sl], start=True, stop=True),
            reads=[rows.b, ones_f.b], writes=[bc_a.b])
        p.emit("tensor", lambda s, slb=slb: s.matmul(
            bc_b.t[:], ones_f.t[:], rows.t[0:1, slb], start=True, stop=True),
            reads=[rows.b, ones_f.b], writes=[bc_b.b], tick=True)
        for ch in range(DC):
            par = ch % 2
            xs = xa[:, par * 512:(par + 1) * 512]
            xb = xbufs[par]
            xo = xn.t[:, ch * n_tok + u * 512: ch * n_tok + u * 512 + 512]
            p.emit("sync", lambda s, ch=ch, u=u, xs=xs: s.dma_start(
                out=xs, in_=x_dram.t[ch * P:(ch + 1) * P, u * 512:(u + 1) * 512]),
                reads=[x_dram.b], writes=[xb], dma=True)
            p.emit("vector", lambda s, xs=xs: s.tensor_tensor(
                tmp.t[:, 0:512], xs, bc_a.t[:], ALU.mult),
                reads=[xb, bc_a.b], writes=[tmp.b])
            if gamma is None:
                p.emit("vector", lambda s, xo=xo: s.tensor_tensor(
                    xo, tmp.t[:, 0:512], bc_b.t[:], ALU.add),
                    reads=[tmp.b, bc_b.b], writes=[xn.b])
            else:
                p.emit("vector", lambda s: s.tensor_tensor(
                    tmp.t[:, 0:512], tmp.t[:, 0:512], bc_b.t[:], ALU.add),
                    reads=[tmp.b, bc_b.b], writes=[tmp.b])
                p.emit("vector", lambda s, xo=xo, ch=ch: s.tensor_scalar(
                    xo, tmp.t[:, 0:512], gamma.t[:, ch:ch + 1], beta.t[:, ch:ch + 1],
                    ALU.mult, ALU.add),
                    reads=[tmp.b, gamma.b, beta.b], writes=[xn.b])


def emit_ln_double(c, sc, x_dram, xn, gamma, beta, keep_rows, n_tok=T, eps=1e-5):
    p = c.p
    xa = sc["xchunk"].t
    xbufs = sc["xchunk_bufs"]
    cast, sq = sc["cast"], sc["sq"]
    ones_bf, ones_f = sc["ones_bf"], sc["ones_f32"]
    stm_ps, sts_ps = sc["statm_ps"], sc["stats_ps"]
    bc_a, bc_b = sc["bc_a"], sc["bc_b"]
    rows, rtmp, rtmp2, tmp = sc["rows"], sc["rtmp"], sc["rtmp2"], sc["tmp32"]
    NU = n_tok // 512
    for u in range(NU):
        for ch in range(DC):
            par = ch % 2
            xs = xa[:, par * 512:(par + 1) * 512]
            xb = xbufs[par]
            p.emit("sync", lambda s, ch=ch, u=u, xs=xs: s.dma_start(
                out=xs, in_=x_dram.t[ch * P:(ch + 1) * P, u * 512:(u + 1) * 512]),
                reads=[x_dram.b], writes=[xb], dma=True)
            p.emit("vector", lambda s, xs=xs: s.tensor_copy(cast.t[:], xs),
                   reads=[xb], writes=[cast.b])
            p.emit("scalar", lambda s, xs=xs: s.activation(sq.t[:], xs, AF.Square),
                   reads=[xb], writes=[sq.b])
            p.emit("tensor", lambda s, ch=ch: s.matmul(
                stm_ps.t[0:1, 0:512], ones_bf.t[:], cast.t[:],
                start=(ch == 0), stop=(ch == DC - 1)),
                reads=[cast.b, ones_bf.b], writes=[stm_ps.b], tick=(ch == DC - 1))
            p.emit("tensor", lambda s, ch=ch: s.matmul(
                sts_ps.t[0:1, 0:512], ones_bf.t[:], sq.t[:],
                start=(ch == 0), stop=(ch == DC - 1)),
                reads=[sq.b, ones_bf.b], writes=[sts_ps.b], tick=(ch == DC - 1))
        sl = slice(0, 512)
        slb = slice(512, 1024)
        usl = slice(u * 512, (u + 1) * 512)
        p.emit("vector", lambda s: s.tensor_scalar_mul(
            rows.t[0:1, slb], stm_ps.t[0:1, 0:512], 1.0 / D),
            reads=[stm_ps.b], writes=[rows.b])
        p.emit("vector", lambda s: s.tensor_scalar_mul(
            rows.t[0:1, sl], sts_ps.t[0:1, 0:512], 1.0 / D),
            reads=[sts_ps.b], writes=[rows.b])
        p.emit("vector", lambda s: s.tensor_tensor(
            rtmp.t[0:1, 0:512], rows.t[0:1, slb], rows.t[0:1, slb], ALU.mult),
            reads=[rows.b], writes=[rtmp.b])
        p.emit("vector", lambda s: s.tensor_tensor(
            rows.t[0:1, sl], rows.t[0:1, sl], rtmp.t[0:1, 0:512], ALU.subtract),
            reads=[rows.b, rtmp.b], writes=[rows.b])
        p.emit("vector", lambda s: s.tensor_copy(rtmp2.t[0:1, :], rows.t[0:1, sl]),
               reads=[rows.b], writes=[rtmp2.b])
        p.emit("scalar", lambda s: s.activation(
            rows.t[0:1, sl], rows.t[0:1, sl], AF.Sqrt, bias=sc["eps"].t[0:1, 0:1]),
            reads=[rows.b, sc["eps"].b], writes=[rows.b])
        p.emit("vector", lambda s: s.reciprocal(rows.t[0:1, sl], rows.t[0:1, sl]),
               reads=[rows.b], writes=[rows.b])
        p.emit("vector", lambda s: s.tensor_tensor(
            rows.t[0:1, slb], rows.t[0:1, slb], rows.t[0:1, sl], ALU.mult),
            reads=[rows.b], writes=[rows.b])
        p.emit("vector", lambda s: s.tensor_scalar_mul(
            rows.t[0:1, slb], rows.t[0:1, slb], -1.0), reads=[rows.b], writes=[rows.b])
        p.emit("vector", lambda s, usl=usl: s.tensor_copy(
            keep_rows.t[0:1, usl], rows.t[0:1, sl]),
            reads=[rows.b], writes=[keep_rows.b])
        p.emit("vector", lambda s, usl=usl: s.tensor_copy(
            keep_rows.t[0:1, n_tok + usl.start:n_tok + usl.stop], rows.t[0:1, slb]),
            reads=[rows.b], writes=[keep_rows.b])
        p.emit("vector", lambda s: s.tensor_tensor(
            rtmp.t[0:1, 0:512], rows.t[0:1, sl], rows.t[0:1, sl], ALU.mult),
            reads=[rows.b], writes=[rtmp.b])
        p.emit("vector", lambda s: s.tensor_tensor(
            rtmp2.t[0:1, :], rtmp2.t[0:1, :], rtmp.t[0:1, 0:512], ALU.mult),
            reads=[rtmp2.b, rtmp.b], writes=[rtmp2.b])
        p.emit("scalar", lambda s: s.activation(
            rtmp2.t[0:1, :], rtmp2.t[0:1, :], AF.Sqrt, bias=sc["eps"].t[0:1, 0:1]),
            reads=[rtmp2.b, sc["eps"].b], writes=[rtmp2.b])
        p.emit("vector", lambda s: s.reciprocal(rtmp2.t[0:1, :], rtmp2.t[0:1, :]),
               reads=[rtmp2.b], writes=[rtmp2.b])
        p.emit("vector", lambda s: s.tensor_tensor(
            rows.t[0:1, sl], rows.t[0:1, sl], rtmp2.t[0:1, :], ALU.mult),
            reads=[rows.b, rtmp2.b], writes=[rows.b])
        p.emit("vector", lambda s: s.tensor_tensor(
            rows.t[0:1, slb], rows.t[0:1, slb], rtmp2.t[0:1, :], ALU.mult),
            reads=[rows.b, rtmp2.b], writes=[rows.b])
        p.emit("tensor", lambda s: s.matmul(
            bc_a.t[:], ones_f.t[:], rows.t[0:1, sl], start=True, stop=True),
            reads=[rows.b, ones_f.b], writes=[bc_a.b])
        p.emit("tensor", lambda s: s.matmul(
            bc_b.t[:], ones_f.t[:], rows.t[0:1, slb], start=True, stop=True),
            reads=[rows.b, ones_f.b], writes=[bc_b.b], tick=True)
        for ch in range(DC):
            par = ch % 2
            xs = xa[:, par * 512:(par + 1) * 512]
            xb = xbufs[par]
            xo = xn.t[:, ch * n_tok + u * 512: ch * n_tok + u * 512 + 512]
            p.emit("sync", lambda s, ch=ch, u=u, xs=xs: s.dma_start(
                out=xs, in_=x_dram.t[ch * P:(ch + 1) * P, u * 512:(u + 1) * 512]),
                reads=[x_dram.b], writes=[xb], dma=True)
            p.emit("vector", lambda s, xs=xs, xb=xb: s.tensor_tensor(
                tmp.t[:, 0:512], xs, bc_a.t[:], ALU.mult),
                reads=[xb, bc_a.b], writes=[tmp.b])
            p.emit("vector", lambda s: s.tensor_tensor(
                tmp.t[:, 0:512], tmp.t[:, 0:512], bc_b.t[:], ALU.add),
                reads=[tmp.b, bc_b.b], writes=[tmp.b])
            p.emit("vector", lambda s, xo=xo, ch=ch: s.tensor_scalar(
                xo, tmp.t[:, 0:512], gamma.t[:, ch:ch + 1], beta.t[:, ch:ch + 1],
                ALU.mult, ALU.add),
                reads=[tmp.b, gamma.b, beta.b], writes=[xn.b])


def make_ln_scratch(c, n_tok=T, ps_share=None):
    sc = {}
    xch = c.sbuf("ln_xchunk", [P, 1024], F32)
    sc["xchunk"] = xch
    sc["xchunk_bufs"] = [c.p.buf("ln_xc0"), c.p.buf("ln_xc1")]
    sc["cast"] = c.sbuf("ln_cast", [P, 512], BF16)
    sc["sq"] = c.sbuf("ln_sq", [P, 512], BF16)
    sc["ones_bf"] = c.sbuf("ln_ones_bf", [P, 1], BF16)
    sc["ones_f32"] = c.sbuf("ln_ones_f", [1, P], F32)
    sc["rows"] = c.sbuf("ln_rows", [1, 1024], F32)
    sc["rtmp"] = c.sbuf("ln_rtmp", [1, 512], F32)
    sc["eps"] = c.sbuf("ln_eps", [1, 1], F32)
    sc["rtmp2"] = c.sbuf("ln_rtmp2", [1, 512], F32)
    sc["tmp32"] = c.sbuf("ln_tmp", [P, 512], F32)
    if ps_share is None:
        sc["statm_ps"] = c.psum("ln_statm_ps", [P, 512], F32)
        sc["stats_ps"] = c.psum("ln_stats_ps", [P, 512], F32)
        bc = c.psum("ln_bc_ps", [P, 1024], F32)
        sc["bc_a"] = TB(bc.t[:, 0:512], bc.b)
        sc["bc_b"] = TB(bc.t[:, 512:1024], bc.b)
    else:
        sc["statm_ps"] = ps_share["statm_ps"]
        sc["stats_ps"] = ps_share["stats_ps"]
        if "bcast_ps" in ps_share:
            bc = ps_share["bcast_ps"]
            sc["bc_a"] = TB(bc.t[:, 0:512], bc.b)
            sc["bc_b"] = TB(bc.t[:, 512:1024], bc.b)
        else:
            sc["bc_a"] = ps_share["bc_a"]
            sc["bc_b"] = ps_share["bc_b"]
    return sc


def init_ln_consts(c, sc):
    p = c.p
    p.emit("vector", lambda s: s.memset(sc["ones_bf"].t[:], 1.0),
           writes=[sc["ones_bf"].b])
    p.emit("vector", lambda s: s.memset(sc["ones_f32"].t[:], 1.0),
           writes=[sc["ones_f32"].b])
    p.emit("vector", lambda s: s.memset(sc["eps"].t[:], 1e-5),
           writes=[sc["eps"].b])



def to_fm(x):
    return np.ascontiguousarray(x.T)


def from_fm(y):
    return np.ascontiguousarray(y.T)


def decl_ffn0(c, npfx=""):
    return {
        "w1bf": c.dram(npfx + "w1bf", [32, P, DC * P], BF16),
        "w2bf": c.dram(npfx + "w2bf", [32, P, DC * P], BF16),
        "h1bf": c.dram(npfx + "h1bf", [32, P, 32 * P], BF16),
        "h2bf": c.dram(npfx + "h2bf", [32, P, 32 * P], BF16),
        "w3bf": c.dram(npfx + "w3bf", [DC, P, 32 * P], BF16),
        "w1_d": c.dram(npfx + "w1T", [D, 4096], F32, kind="ExternalInput"),
        "w2_d": c.dram(npfx + "w2T", [D, 4096], F32, kind="ExternalInput"),
        "h1_d": c.dram(npfx + "h1T", [4096, 4096], F32, kind="ExternalInput"),
        "h2_d": c.dram(npfx + "h2T", [4096, 4096], F32, kind="ExternalInput"),
        "w3_d": c.dram(npfx + "w3T", [4096, D], F32, kind="ExternalInput"),
        "s1_d": c.dram(npfx + "w1s", [P, 32], F32, kind="ExternalInput"),
        "s2_d": c.dram(npfx + "w2s", [P, 32], F32, kind="ExternalInput"),
        "sh1_d": c.dram(npfx + "h1s", [P, 32], F32, kind="ExternalInput"),
        "sh2_d": c.dram(npfx + "h2s", [P, 32], F32, kind="ExternalInput"),
        "s3_d": c.dram(npfx + "w3s", [P, DC], F32, kind="ExternalInput"),
    }


def emit_ffn0_stage(c, d, x_d, xo_d, pfx="f_", first_use=True):
    p = c.p
    w1_d = d["w1_d"]
    w2_d = d["w2_d"]
    h1_d = d["h1_d"]
    h2_d = d["h2_d"]
    w3_d = d["w3_d"]
    s1_d = d["s1_d"]
    s2_d = d["s2_d"]
    sh1_d = d["sh1_d"]
    sh2_d = d["sh2_d"]
    s3_d = d["s3_d"]
    w1bf, w2bf, h1bf, h2bf, w3bf = d["w1bf"], d["w2bf"], d["h1bf"], d["h2bf"], d["w3bf"]
    old_pfx, old_ctx = c.prefix, c.ctx
    c.prefix = pfx
    with contextlib.ExitStack() as sctx:
        c.ctx = sctx
        sc = make_ln_scratch(c)
        xn = c.sbuf("xn", [P, DC * T], BF16)
        g1 = c.sbuf("g1", [P, 32 * 1024], BF16)
        hh = c.sbuf("hh", [P, 32 * 1024], BF16)
        wab = [c.sbuf(f"wa{i}", [P, DC * P], BF16) for i in range(2)]
        wbb = [c.sbuf(f"wb{i}", [P, DC * P], BF16) for i in range(2)]
        whb = [c.sbuf(f"wh{i}", [P, 32 * P], BF16) for i in range(2)]
        scol = c.sbuf("scol", [P, 32 * 4 + DC], F32)
        t1 = c.sbuf("t1", [P, 512], F32)
        t2 = c.sbuf("t2", [P, 512], F32)
        xrt = [t1, t2]
        xout = c.sbuf("xout", [P, 512], F32)
        psA = c.psum("psA", [P, 512], F32)
        psB = c.psum("psB", [P, 512], F32)

        init_ln_consts(c, sc)
        for i, sd in enumerate([s1_d, s2_d, sh1_d, sh2_d]):
            p.emit("sync", lambda s, i=i, sd=sd: s.dma_start(
                out=scol.t[:, i * 32:(i + 1) * 32], in_=sd.t),
                reads=[sd.b], writes=[scol.b], dma=True)
        p.emit("sync", lambda s: s.dma_start(
            out=scol.t[:, 128:128 + DC], in_=s3_d.t),
            reads=[s3_d.b], writes=[scol.b], dma=True)

        emit_ln(c, sc, x_d, xn)

        def wload(wb, fp_d, bf_d, fo, nk, tt):
            if first_use and tt == 0:
                p.emit("gpsimd", lambda s, fo=fo, wb=wb, fp_d=fp_d, nk=nk: s.dma_start(
                    out=wb.t[:, 0:nk * P].rearrange("p (k f) -> p k f", f=P),
                    in_=fp_d.t.rearrange("(k p) f -> p k f", p=P)[:, :, fo * P:(fo + 1) * P]),
                    reads=[fp_d.b], writes=[wb.b], dma=True)
                p.emit("sync", lambda s, fo=fo, wb=wb, bf_d=bf_d, nk=nk: s.dma_start(
                    out=bf_d.t[fo], in_=wb.t[:, 0:nk * P]),
                    reads=[wb.b], writes=[bf_d.b], dma=True)
            else:
                p.emit("gpsimd", lambda s, fo=fo, wb=wb, bf_d=bf_d, nk=nk: s.dma_start(
                    out=wb.t[:, 0:nk * P], in_=bf_d.t[fo]),
                    reads=[bf_d.b], writes=[wb.b], dma=True)

        NT = 2
        for tt in range(NT):
            ts0 = tt * 1024
            for fo in range(32):
                par = fo % 2
                wa, wb_ = wab[par], wbb[par]
                wload(wa, w1_d, w1bf, fo, DC, tt)
                wload(wb_, w2_d, w2bf, fo, DC, tt)
                for tu in range(2):
                    toff = ts0 + tu * 512
                    for k in range(DC):
                        p.emit("tensor", lambda s, k=k, toff=toff, wa=wa: s.matmul(
                            psA.t[:], wa.t[:, k * P:(k + 1) * P],
                            xn.t[:, k * T + toff:k * T + toff + 512],
                            start=(k == 0), stop=(k == DC - 1)),
                            reads=[wa.b, xn.b], writes=[psA.b], tick=(k == DC - 1))
                        p.emit("tensor", lambda s, k=k, toff=toff, wb_=wb_: s.matmul(
                            psB.t[:], wb_.t[:, k * P:(k + 1) * P],
                            xn.t[:, k * T + toff:k * T + toff + 512],
                            start=(k == 0), stop=(k == DC - 1)),
                            reads=[wb_.b, xn.b], writes=[psB.b], tick=(k == DC - 1))
                    p.emit("scalar", lambda s, fo=fo: s.activation(
                        t1.t[:], psA.t[:], AF.Silu, scale=scol.t[:, fo:fo + 1]),
                        reads=[psA.b, scol.b], writes=[t1.b])
                    p.emit("vector", lambda s, fo=fo: s.tensor_scalar_mul(
                        t2.t[:], psB.t[:], scol.t[:, 32 + fo:32 + fo + 1]),
                        reads=[psB.b, scol.b], writes=[t2.b])
                    p.emit("vector", lambda s, fo=fo, tu=tu: s.tensor_tensor(
                        g1.t[:, fo * 1024 + tu * 512:fo * 1024 + tu * 512 + 512],
                        t1.t[:], t2.t[:], ALU.mult),
                        reads=[t1.b, t2.b], writes=[g1.b])
            for stage, (wd, bfd, src_t, dst_t, s_off) in enumerate([
                    (h1_d, h1bf, g1, hh, 64), (h2_d, h2bf, hh, g1, 96)]):
                for fo in range(32):
                    par = fo % 2
                    wh = whb[par]
                    wload(wh, wd, bfd, fo, 32, tt)
                    for tu in range(2):
                        toff_l = tu * 512
                        for k in range(32):
                            p.emit("tensor", lambda s, k=k, wh=wh, src_t=src_t, toff_l=toff_l: s.matmul(
                                psA.t[:], wh.t[:, k * P:(k + 1) * P],
                                src_t.t[:, k * 1024 + toff_l:k * 1024 + toff_l + 512],
                                start=(k == 0), stop=(k == 31)),
                                reads=[wh.b, src_t.b], writes=[psA.b], tick=(k == 31))
                        p.emit("scalar", lambda s, fo=fo, dst_t=dst_t, s_off=s_off, tu=tu: s.activation(
                            dst_t.t[:, fo * 1024 + tu * 512:fo * 1024 + tu * 512 + 512],
                            psA.t[:], AF.Gelu,
                            scale=scol.t[:, s_off + fo:s_off + fo + 1]),
                            reads=[psA.b, scol.b], writes=[dst_t.b])
            for fo in range(DC):
                par = fo % 2
                wh = whb[par]
                wload(wh, w3_d, w3bf, fo, 32, tt)
                for tu in range(2):
                    toff = ts0 + tu * 512
                    par2 = tu % 2
                    xs = xrt[par2].t[:, 0:512]
                    p.emit("sync", lambda s, fo=fo, toff=toff, xs=xs: s.dma_start(
                        out=xs, in_=x_d.t[fo * P:(fo + 1) * P, toff:toff + 512]),
                        reads=[x_d.b], writes=[xrt[par2].b], dma=True)
                    for k in range(32):
                        p.emit("tensor", lambda s, k=k, wh=wh, tu=tu: s.matmul(
                            psA.t[:], wh.t[:, k * P:(k + 1) * P],
                            g1.t[:, k * 1024 + tu * 512:k * 1024 + tu * 512 + 512],
                            start=(k == 0), stop=(k == 31)),
                            reads=[wh.b, g1.b], writes=[psA.b], tick=(k == 31))
                    p.emit("vector", lambda s, fo=fo, xs=xs, par2=par2: s.scalar_tensor_tensor(
                        xout.t[:], psA.t[:], scol.t[:, 128 + fo:128 + fo + 1], xs,
                        ALU.mult, ALU.add),
                        reads=[psA.b, scol.b, xrt[par2].b], writes=[xout.b])
                    p.emit("sync", lambda s, fo=fo, toff=toff: s.dma_start(
                        out=xo_d.t[fo * P:(fo + 1) * P, toff:toff + 512], in_=xout.t[:]),
                        reads=[xout.b], writes=[xo_d.b], dma=True)
        p.barrier()
    c.prefix, c.ctx = old_pfx, old_ctx

def build_ffn0_prog():
    p = Prog("ffn0")
    nc = p.nc
    ctx = contextlib.ExitStack()
    c = Core(p, ctx)
    x_d = c.dram("x_fm", [D, T], F32, kind="ExternalInput")
    xo_d = c.dram("xo_fm", [D, T], F32, kind="ExternalOutput")
    d = decl_ffn0(c)
    with ctx:
        emit_ffn0_stage(c, d, x_d, xo_d, pfx="")
        p.emit("sync", lambda s: s.nop(), reads=[xo_d.b])
        p.build(ctx)
    return nc

def ffn0_host_inputs(x_batch_fm, ins, li="L0"):
    g = lambda k: np.asarray(ins[k], np.float32)
    return dict(
        x_fm=np.ascontiguousarray(x_batch_fm),
        w1T=np.ascontiguousarray(g(f"{li}_ffn_w1_w").T),
        w2T=np.ascontiguousarray(g(f"{li}_ffn_w2_w").T),
        h1T=np.ascontiguousarray(g(f"{li}_ffn_h1_w").T),
        h2T=np.ascontiguousarray(g(f"{li}_ffn_h2_w").T),
        w3T=np.ascontiguousarray(g(f"{li}_ffn_w3_w").T),
        w1s=np.ascontiguousarray(g(f"{li}_ffn_w1_s").reshape(32, P).T),
        w2s=np.ascontiguousarray(g(f"{li}_ffn_w2_s").reshape(32, P).T),
        h1s=np.ascontiguousarray(g(f"{li}_ffn_h1_s").reshape(32, P).T),
        h2s=np.ascontiguousarray(g(f"{li}_ffn_h2_s").reshape(32, P).T),
        w3s=np.ascontiguousarray(g(f"{li}_ffn_w3_s").reshape(DC, P).T),
    )


def emit_wproj_fm(c, w_d, wbufs, xn, out_emit, n_fo=DC, n_k=DC, xn_stride=T,
                  psP=None, q="gpsimd"):
    p = c.p
    for fo in range(n_fo):
        wb = wbufs[fo % len(wbufs)]
        p.emit(q, lambda s, fo=fo, wb=wb: s.dma_start(
            out=wb.t[:].rearrange("p (k f) -> p k f", f=P),
            in_=w_d.t.rearrange("(k p) f -> p k f", p=P)[:, :, fo * P:(fo + 1) * P]),
            reads=[w_d.b], writes=[wb.b], dma=True)
        for u in range(T // 512):
            for k in range(n_k):
                p.emit("tensor", lambda s, fo=fo, u=u, k=k, wb=wb: s.matmul(
                    psP.t[:], wb.t[:, k * P:(k + 1) * P],
                    xn.t[:, k * xn_stride + u * 512: k * xn_stride + u * 512 + 512],
                    start=(k == 0), stop=(k == n_k - 1)),
                    reads=[wb.b, xn.b], writes=[psP.b], tick=(k == n_k - 1))
            out_emit(fo, u, psP)




def emit_attn_heads(c, sh, qf, kf, vt, causal, n_heads=NH, kaug_prebuilt=False,
                    kf_single=False):
    p = c.p
    qaug2, kaug2 = sh["qaug2"], sh["kaug2"]
    ptile, ptb2 = sh["ptile"], sh["ptb2"]
    mcols_2, rrow, bcs = sh["mcols_2"], sh["rrow"], sh["bcs"]
    mcols2 = sh["mcols2"]
    nid = sh["nid"]
    msk = sh.get("msk")
    ones_bf, ones_f = sh["ones_bf"], sh["ones_f"]
    psSA, psSB, psRow, psO, psD, psBC = (sh["psSA"], sh["psSB"], sh["psRow"],
                                         sh["psO"], sh["psD"], sh["psBC"])
    p1banks = sh["p1banks"]

    def pass1_aug(h):
        hp, hi = h // 2, h % 2
        base = hi * 64
        par = h % 2
        qaug, kaug, mcols = qaug2[par], kaug2[par % len(kaug2)], mcols_2[par]
        if kf_single:
            p.emit("sync", lambda s, base=base, hp=hp, qaug=qaug: s.dma_start(
                out=qaug.t[0:64, :], in_=qf.t[base:base + 64, hp * T:(hp + 1) * T]),
                reads=[qf.b], writes=[qaug.b], dma=True)
        for q_ in range(16):
            nu = (q_ // 4 + 1) if causal else 4
            for su in range(nu):
                ps_ = p1banks[(q_ * 4 + su) % len(p1banks)]
                if kf_single:
                    p.emit("tensor", lambda s, q_=q_, su=su, ps_=ps_, qaug=qaug: s.matmul(
                        ps_.t[:], qaug.t[0:64, q_ * P:(q_ + 1) * P],
                        kf.t[0:64, su * 512:(su + 1) * 512],
                        start=True, stop=True),
                        reads=[qaug.b, kf.b], writes=[ps_.b], tick=True)
                else:
                    p.emit("tensor", lambda s, q_=q_, su=su, base=base, hp=hp, ps_=ps_: s.matmul(
                        ps_.t[:], qf.t[base:base + 64, hp * T + q_ * P: hp * T + (q_ + 1) * P],
                        kf.t[base:base + 64, hp * T + su * 512: hp * T + (su + 1) * 512],
                        start=True, stop=True,
                        tile_position=None),
                        reads=[qf.b, kf.b], writes=[ps_.b], tick=True)
                p.emit("vector", lambda s, q_=q_, su=su, ps_=ps_: s.reduce_max(
                    mcols2.t[:, q_ * 4 + su:q_ * 4 + su + 1], ps_.t[:], axis=AX.X),
                    reads=[ps_.b], writes=[mcols2.b])
            p.emit("vector", lambda s, q_=q_, nu=nu, mcols=mcols: s.reduce_max(
                mcols.t[:, q_:q_ + 1], mcols2.t[:, q_ * 4:q_ * 4 + nu], axis=AX.X),
                reads=[mcols2.b], writes=[mcols.b])
        if not kf_single:
            p.emit("sync", lambda s, base=base, hp=hp, qaug=qaug: s.dma_start(
                out=qaug.t[0:64, :], in_=qf.t[base:base + 64, hp * T:(hp + 1) * T]),
                reads=[qf.b], writes=[qaug.b], dma=True)
        if not kaug_prebuilt:
            p.emit("sync", lambda s, base=base, hp=hp, kaug=kaug: s.dma_start(
                out=kaug.t[0:64, :], in_=kf.t[base:base + 64, hp * T:(hp + 1) * T]),
                reads=[kf.b], writes=[kaug.b], dma=True)
            p.emit("vector", lambda s, kaug=kaug: s.memset(kaug.t[64:65, :], 1.0),
                   writes=[kaug.b])
        for u in range(4):
            for qq in range(4):
                q_ = u * 4 + qq
                p.emit("tensor", lambda s, q_=q_, qq=qq, mcols=mcols: s.matmul(
                    psRow.t[0:1, qq * P:(qq + 1) * P], mcols.t[:, q_:q_ + 1],
                    nid.t[:], start=(qq == 0), stop=(qq == 3), skip_group_check=True),
                    reads=[mcols.b, nid.b], writes=[psRow.b], tick=(qq == 3))
            p.emit("vector", lambda s, u=u, qaug=qaug: s.tensor_copy(
                qaug.t[64:65, u * 512:(u + 1) * 512], psRow.t[0:1, 0:512]),
                reads=[psRow.b], writes=[qaug.b])

    def pass2(h):
        hp, hi = h // 2, h % 2
        base = hi * 64
        par = h % 2
        qaug = qaug2[par]
        kaug = kaug2[par] if not kaug_prebuilt else kaug2[0]
        for u in range(4):
            nst = (4 * u + 4) if causal else 16
            first_pv = True
            for st_i in range(nst):
                pvpar = st_i % 2
                pt = ptile.t[:, pvpar * 512:(pvpar + 1) * 512]
                ptb = ptb2[pvpar]
                psSC = psSA if pvpar == 0 else psSB
                p.emit("tensor", lambda s, st_i=st_i, u=u, psSC=psSC, kaug=kaug, qaug=qaug: s.matmul(
                    psSC.t[:], kaug.t[0:65, st_i * P:(st_i + 1) * P],
                    qaug.t[0:65, u * 512:(u + 1) * 512],
                    start=True, stop=True),
                    reads=[kaug.b, qaug.b], writes=[psSC.b], tick=True)
                diag = causal and st_i >= 4 * u
                if diag:
                    p.emit("scalar", lambda s, psSC=psSC: s.activation(
                        sh["t1"].t[:], psSC.t[:], AF.Exp),
                        reads=[psSC.b], writes=[sh["t1"].b])
                    p.emit("vector", lambda s, pt=pt, st_i=st_i, u=u: s.tensor_tensor(
                        pt, sh["t1"].t[:],
                        msk.t[:, (st_i - 4 * u) * 512:(st_i - 4 * u + 1) * 512],
                        ALU.mult),
                        reads=[sh["t1"].b, msk.b], writes=[ptb])
                else:
                    p.emit("scalar", lambda s, pt=pt, psSC=psSC: s.activation(
                        pt, psSC.t[:], AF.Exp),
                        reads=[psSC.b], writes=[ptb])
                p.emit("tensor", lambda s, pt=pt, st_i=st_i, h=h, first_pv=first_pv,
                       nst=nst: s.matmul(
                    psO.t[0:65, :], vt.t[:, st_i * 1040 + h * 65: st_i * 1040 + h * 65 + 65],
                    pt, start=first_pv, stop=(st_i == nst - 1)),
                    reads=[vt.b, ptb], writes=[psO.b], tick=(st_i == nst - 1))
                first_pv = False
            p.emit("vector", lambda s: s.reciprocal(rrow.t[0:1, :], psO.t[64:65, :]),
                   reads=[psO.b], writes=[rrow.b])
            p.emit("tensor", lambda s: s.matmul(
                psBC.t[0:64, :], ones_f.t[0:1, 0:64], rrow.t[0:1, :], start=True, stop=True),
                reads=[ones_f.b, rrow.b], writes=[psBC.b], tick=True)
            p.emit("vector", lambda s: s.tensor_copy(bcs.t[0:64, :], psBC.t[0:64, :]),
                   reads=[psBC.b], writes=[bcs.b])
            p.emit("vector", lambda s, base=base, hp=hp, u=u: s.tensor_tensor(
                qf.t[base:base + 64, hp * T + u * 512: hp * T + (u + 1) * 512],
                psO.t[0:64, :], bcs.t[0:64, :], ALU.mult),
                reads=[psO.b, bcs.b], writes=[qf.b])

    for h in range(n_heads + 1):
        if h >= 1:
            pass2(h - 1)
        if h < n_heads:
            pass1_aug(h)

def decl_attn_b(c, npfx=""):
    return {
        "wq_d": c.dram(npfx + "WqT", [D, D], F32, kind="ExternalInput"),
        "wqs_d": c.dram(npfx + "WqsT", [D, D], F32, kind="ExternalInput"),
        "wk_d": c.dram(npfx + "WkT", [D, D], F32, kind="ExternalInput"),
        "wks_d": c.dram(npfx + "WksT", [D, D], F32, kind="ExternalInput"),
        "wv_d": c.dram(npfx + "WvT", [D, D], F32, kind="ExternalInput"),
        "wo_d": c.dram(npfx + "WoT", [D, D], F32, kind="ExternalInput"),
        "cq_d": c.dram(npfx + "cos8q", [P, T], F32, kind="ExternalInput"),
        "sq_d": c.dram(npfx + "sin8qs", [P, T], F32, kind="ExternalInput"),
        "ck_d": c.dram(npfx + "cosk", [P, T], F32, kind="ExternalInput"),
        "sk_d": c.dram(npfx + "sinks", [P, T], F32, kind="ExternalInput"),
        "msk_d": c.dram(npfx + "masks", [P, 4 * 512], F32, kind="ExternalInput"),
        "nid_d": c.dram(npfx + "negid8", [P, P], F32, kind="ExternalInput"),
    }


def emit_attn_b_stage(c, d, x_d, xo_d, pfx="s3_"):
    p = c.p
    wq_d = d["wq_d"]
    wqs_d = d["wqs_d"]
    wk_d = d["wk_d"]
    wks_d = d["wks_d"]
    wv_d = d["wv_d"]
    wo_d = d["wo_d"]
    cq_d = d["cq_d"]
    sq_d = d["sq_d"]
    ck_d = d["ck_d"]
    sk_d = d["sk_d"]
    msk_d = d["msk_d"]
    nid_d = d["nid_d"]
    old_pfx, old_ctx = c.prefix, c.ctx
    c.prefix = pfx
    with contextlib.ExitStack() as sctx:
        c.ctx = sctx
        ps_a1 = c.psum("ps_a1", [P, 512], F32)
        ps_a2 = c.psum("ps_a2", [P, 512], F32)
        ps_row = c.psum("ps_row", [P, 512], F32)
        ps_bc = c.psum("ps_bc", [P, 512], F32)
        sc = make_ln_scratch(c, ps_share=dict(
            statm_ps=ps_row, stats_ps=ps_bc, bc_a=ps_a1, bc_b=ps_a2))
        xn = c.sbuf("xn", [P, DC * T], BF16)
        qf = c.sbuf("qf", [P, DC * T], BF16)
        kf = c.sbuf("kf", [P, DC * T], BF16)
        vt = c.sbuf("vt", [P, 16 * 1040], BF16)
        p.emit("vector", lambda s: s.memset(
            vt.t[:].rearrange("p (t h f) -> p t h f", h=16, f=65)[:, :, :, 64:65], 1.0),
            writes=[vt.b])
        qaug2 = [c.sbuf("qaugA", [P, T], BF16), c.sbuf("qaugB", [P, T], BF16)]
        kaug2 = [c.sbuf("kaugA", [P, T], BF16), c.sbuf("kaugB", [P, T], BF16)]
        ptile = c.sbuf("ptile", [P, 2 * 512], BF16)
        ptb2 = [p.buf("pt0"), p.buf("pt1")]
        mcols_2 = [c.sbuf("mcolsA", [P, 16], BF16), c.sbuf("mcolsB", [P, 16], BF16)]
        mcols2 = c.sbuf("mcols2", [P, 64], BF16)
        rrow = c.sbuf("rrow", [1, 512], F32)
        bcs = c.sbuf("bcs", [P, 512], F32)
        cq = c.sbuf("cq", [P, T], BF16)
        sq = c.sbuf("sq", [P, T], BF16)
        ck = c.sbuf("ck", [P, T], BF16)
        sk = c.sbuf("sk", [P, T], BF16)
        msk = c.sbuf("msk", [P, 4 * 512], BF16)
        nid = c.sbuf("nid", [P, P], BF16)
        wb2 = [c.sbuf(f"wpb{i}", [P, DC * P], BF16) for i in range(2)]
        wv_sb = c.sbuf("wv_sb", [P, DC * 512], BF16)
        t1 = c.sbuf("t1", [P, 512], F32)
        t2 = c.sbuf("t2", [P, 512], F32)
        xrt = [t1, t2]
        xout = c.sbuf("xout", [P, 512], F32)
        psP = c.psum("psP", [P, 512], F32)
        psQ = c.psum("psQ", [P, 512], F32)
        psSA = ps_a1
        psSB = ps_a2
        psRow = ps_row
        psO = c.psum("psO", [P, 512], F32)
        psD = c.psum("psD", [P, 512], F32)
        psBC = ps_bc

        init_ln_consts(c, sc)
        ones_bf = sc["ones_bf"]
        ones_f = sc["ones_f32"]

        for dst, src_d in [(cq, cq_d), (sq, sq_d), (ck, ck_d), (sk, sk_d),
                           (msk, msk_d), (nid, nid_d)]:
            p.emit("gpsimd", lambda s, dst=dst, src_d=src_d: s.dma_start(
                out=dst.t[:], in_=src_d.t), reads=[src_d.b], writes=[dst.b], dma=True)

        emit_ln(c, sc, x_d, xn)

        for (wd, wsd, ct, st_, dst) in [(wq_d, wqs_d, cq, sq, qf),
                                        (wk_d, wks_d, ck, sk, kf)]:
            for fo in range(DC):
                wbA, wbB = wb2[0], wb2[1]
                p.emit("gpsimd", lambda s, fo=fo, wbA=wbA, wd=wd: s.dma_start(
                    out=wbA.t[:].rearrange("p (k f) -> p k f", f=P),
                    in_=wd.t.rearrange("(k p) f -> p k f", p=P)[:, :, fo * P:(fo + 1) * P]),
                    reads=[wd.b], writes=[wbA.b], dma=True)
                p.emit("gpsimd", lambda s, fo=fo, wbB=wbB, wsd=wsd: s.dma_start(
                    out=wbB.t[:].rearrange("p (k f) -> p k f", f=P),
                    in_=wsd.t.rearrange("(k p) f -> p k f", p=P)[:, :, fo * P:(fo + 1) * P]),
                    reads=[wsd.b], writes=[wbB.b], dma=True)
                for u in range(4):
                    for k in range(DC):
                        p.emit("tensor", lambda s, fo=fo, u=u, k=k, wbA=wbA: s.matmul(
                            psQ.t[:], wbA.t[:, k * P:(k + 1) * P],
                            xn.t[:, k * T + u * 512: k * T + u * 512 + 512],
                            start=(k == 0), stop=(k == DC - 1)),
                            reads=[wbA.b, xn.b], writes=[psQ.b], tick=(k == DC - 1))
                        p.emit("tensor", lambda s, fo=fo, u=u, k=k, wbB=wbB: s.matmul(
                            psP.t[:], wbB.t[:, k * P:(k + 1) * P],
                            xn.t[:, k * T + u * 512: k * T + u * 512 + 512],
                            start=(k == 0), stop=(k == DC - 1)),
                            reads=[wbB.b, xn.b], writes=[psP.b], tick=(k == DC - 1))
                    us = slice(u * 512, (u + 1) * 512)
                    p.emit("vector", lambda s, us=us, ct=ct: s.tensor_tensor(
                        t1.t[:], psQ.t[:], ct.t[:, us], ALU.mult),
                        reads=[psQ.b, ct.b], writes=[t1.b])
                    p.emit("vector", lambda s, us=us, st_=st_: s.tensor_tensor(
                        t2.t[:], psP.t[:], st_.t[:, us], ALU.mult),
                        reads=[psP.b, st_.b], writes=[t2.b])
                    p.emit("vector", lambda s, fo=fo, us=us, dst=dst: s.tensor_tensor(
                        dst.t[:, fo * T + us.start: fo * T + us.stop],
                        t1.t[:], t2.t[:], ALU.add),
                        reads=[t1.b, t2.b], writes=[dst.b])
        for half in range(2):
            p.emit("gpsimd", lambda s, half=half: s.dma_start(
                out=wv_sb.t[:].rearrange("p (k f) -> p k f", f=512),
                in_=wv_d.t.rearrange("(k p) f -> p k f", p=P)[:, :, half * 512:(half + 1) * 512]),
                reads=[wv_d.b], writes=[wv_sb.b], dma=True)
            for tile in range(16):
                for k in range(DC):
                    p.emit("tensor", lambda s, tile=tile, k=k: s.matmul(
                        psP.t[:], xn.t[:, k * T + tile * P: k * T + (tile + 1) * P],
                        wv_sb.t[:, k * 512:(k + 1) * 512],
                        start=(k == 0), stop=(k == DC - 1)),
                        reads=[xn.b, wv_sb.b], writes=[psP.b], tick=(k == DC - 1))
                p.emit("vector", lambda s, tile=tile, half=half: s.tensor_copy(
                    vt.t[:].rearrange("p (t h f) -> p t h f", h=16, f=65)
                    [:, tile, half * 8:(half + 1) * 8, 0:64],
                    psP.t[:].rearrange("p (h f) -> p h f", f=64)),
                    reads=[psP.b], writes=[vt.b])

        sh = dict(qaug2=qaug2, kaug2=kaug2, ptile=ptile, ptb2=ptb2,
                  mcols_2=mcols_2, mcols2=mcols2, rrow=rrow, bcs=bcs, msk=msk, nid=nid,
                  ones_bf=ones_bf, ones_f=ones_f, t1=t1,
                  psSA=psSA, psSB=psSB, psRow=psRow, psO=psO, psD=psD, psBC=psBC,
                  p1banks=[psP, psQ, psD])
        emit_attn_heads(c, sh, qf, kf, vt, causal=True)

        for fo in range(DC):
            wb = wb2[fo % 2]
            p.emit("gpsimd", lambda s, fo=fo, wb=wb: s.dma_start(
                out=wb.t[:].rearrange("p (k f) -> p k f", f=P),
                in_=wo_d.t.rearrange("(k p) f -> p k f", p=P)[:, :, fo * P:(fo + 1) * P]),
                reads=[wo_d.b], writes=[wb.b], dma=True)
            for u in range(4):
                par = u % 2
                xs = xrt[par].t[:, 0:512]
                p.emit("sync", lambda s, fo=fo, u=u, xs=xs: s.dma_start(
                    out=xs, in_=x_d.t[fo * P:(fo + 1) * P, u * 512:(u + 1) * 512]),
                    reads=[x_d.b], writes=[xrt[par].b], dma=True)
                for k in range(DC):
                    p.emit("tensor", lambda s, fo=fo, u=u, k=k, wb=wb: s.matmul(
                        psP.t[:], wb.t[:, k * P:(k + 1) * P],
                        qf.t[:, k * T + u * 512: k * T + u * 512 + 512],
                        start=(k == 0), stop=(k == DC - 1)),
                        reads=[wb.b, qf.b], writes=[psP.b], tick=(k == DC - 1))
                p.emit("vector", lambda s, xs=xs, par=par: s.tensor_tensor(
                    xout.t[:], psP.t[:], xs, ALU.add),
                    reads=[psP.b, xrt[par].b], writes=[xout.b])
                p.emit("sync", lambda s, fo=fo, u=u: s.dma_start(
                    out=xo_d.t[fo * P:(fo + 1) * P, u * 512:(u + 1) * 512], in_=xout.t[:]),
                    reads=[xout.b], writes=[xo_d.b], dma=True)
        p.emit("sync", lambda s: s.nop(), reads=[xo_d.b])
        p.barrier()
    c.prefix, c.ctx = old_pfx, old_ctx


def build_attn_b_prog():
    p = Prog("attnb")
    nc = p.nc
    ctx = contextlib.ExitStack()
    c = Core(p, ctx)
    x_d = c.dram("x_fm", [D, T], F32, kind="ExternalInput")
    xo_d = c.dram("xo_fm", [D, T], F32, kind="ExternalOutput")
    d = decl_attn_b(c)
    with ctx:
        emit_attn_b_stage(c, d, x_d, xo_d, pfx="")
        p.emit("sync", lambda s: s.nop(), reads=[xo_d.b])
        p.build(ctx)
    return nc

def rope_tables_b():
    dk = HD
    theta = (1.0 / 10000.0 ** (np.arange(0, dk, 2, dtype=np.float32) / dk)).astype(np.float32)
    mth = np.outer(np.arange(T, dtype=np.float32), theta).astype(np.float32)
    ang = np.concatenate([mth, mth], axis=1)
    cos = np.cos(ang, dtype=np.float32)
    sin = np.sin(ang, dtype=np.float32)
    sgn = np.concatenate([-np.ones(32, np.float32), np.ones(32, np.float32)])
    sin_s = sin * sgn[None, :]
    cosf = np.ascontiguousarray(np.tile(cos.T, (2, 1)))
    sinf = np.ascontiguousarray(np.tile(sin_s.T, (2, 1)))
    return cosf, sinf


def attn_b_host_inputs(x_batch_fm, ins):
    g = lambda k: np.asarray(ins[k], np.float32)
    Wq, Wk, Wv, Wo = g("L0_attn_b_W_Q"), g("L0_attn_b_W_K"), g("L0_attn_b_W_V"), g("L0_attn_b_W_O")
    perm = np.arange(D).reshape(NH, HD)
    perm = np.concatenate([perm[:, 32:], perm[:, :32]], axis=1).reshape(-1)
    cosf, sinf = rope_tables_b()
    masks = np.zeros((P, 4 * 512), np.float32)
    for j in range(4):
        s_glob = np.arange(P)[:, None] + j * P + 0
        t_glob = np.arange(512)[None, :]
        masks[:, j * 512:(j + 1) * 512] = (np.arange(P)[:, None] + (3 - j) * -0 + j * 128
                                           <= t_glob)
    for j in range(4):
        masks[:, j * 512:(j + 1) * 512] = ((j * 128 + np.arange(P)[:, None]) <= np.arange(512)[None, :])
    return dict(
        x_fm=np.ascontiguousarray(x_batch_fm),
        WqT=np.ascontiguousarray(Wq.T), WqsT=np.ascontiguousarray(Wq.T[:, perm]),
        WkT=np.ascontiguousarray(Wk.T), WksT=np.ascontiguousarray(Wk.T[:, perm]),
        WvT=np.ascontiguousarray(Wv.T), WoT=np.ascontiguousarray(Wo.T),
        cos8q=np.ascontiguousarray(cosf / 8.0), sin8qs=np.ascontiguousarray(sinf / 8.0),
        cosk=cosf, sinks=sinf,
        masks=masks, negid8=np.ascontiguousarray(-np.eye(P, dtype=np.float32)),
    )


def decl_attn_a(c, npfx=""):
    return {
        "wq_d": c.dram(npfx + "WqT", [D, D], F32, kind="ExternalInput"),
        "wqs_d": c.dram(npfx + "WqsT", [D, D], F32, kind="ExternalInput"),
        "wk_d": c.dram(npfx + "WkT", [D, D], F32, kind="ExternalInput"),
        "wks_d": c.dram(npfx + "WksT", [D, D], F32, kind="ExternalInput"),
        "wv_d": c.dram(npfx + "WvT", [D, D], F32, kind="ExternalInput"),
        "wo_d": c.dram(npfx + "WoT", [D, D], F32, kind="ExternalInput"),
        "ck_d": c.dram(npfx + "cosk", [P, T], F32, kind="ExternalInput"),
        "sk_d": c.dram(npfx + "sinks", [P, T], F32, kind="ExternalInput"),
        "nid_d": c.dram(npfx + "negid", [P, P], F32, kind="ExternalInput"),
        "lnw_d": c.dram(npfx + "lnw", [P, DC], F32, kind="ExternalInput"),
        "lnb_d": c.dram(npfx + "lnb", [P, DC], F32, kind="ExternalInput"),
        "bq_d": c.dram(npfx + "bq", [P, DC], F32, kind="ExternalInput"),
        "bqs_d": c.dram(npfx + "bqs", [P, DC], F32, kind="ExternalInput"),
        "bk_d": c.dram(npfx + "bk", [P, DC], F32, kind="ExternalInput"),
        "bks_d": c.dram(npfx + "bks", [P, DC], F32, kind="ExternalInput"),
        "bv_d": c.dram(npfx + "bv_row", [1, D], F32, kind="ExternalInput"),
        "bo_d": c.dram(npfx + "bo", [P, DC], F32, kind="ExternalInput"),
    }


def emit_attn_a_stage(c, d, x_d, xo_d, pfx="s5_"):
    p = c.p
    wq_d = d["wq_d"]
    wqs_d = d["wqs_d"]
    wk_d = d["wk_d"]
    wks_d = d["wks_d"]
    wv_d = d["wv_d"]
    wo_d = d["wo_d"]
    ck_d = d["ck_d"]
    sk_d = d["sk_d"]
    nid_d = d["nid_d"]
    lnw_d = d["lnw_d"]
    lnb_d = d["lnb_d"]
    bq_d = d["bq_d"]
    bqs_d = d["bqs_d"]
    bk_d = d["bk_d"]
    bks_d = d["bks_d"]
    bv_d = d["bv_d"]
    bo_d = d["bo_d"]
    old_pfx, old_ctx = c.prefix, c.ctx
    c.prefix = pfx
    with contextlib.ExitStack() as sctx:
        c.ctx = sctx
        ps_a1 = c.psum("ps_a1", [P, 512], F32)
        ps_a2 = c.psum("ps_a2", [P, 512], F32)
        ps_row = c.psum("ps_row", [P, 512], F32)
        ps_bc = c.psum("ps_bc", [P, 512], F32)
        sc = make_ln_scratch(c, ps_share=dict(
            statm_ps=ps_row, stats_ps=ps_bc, bc_a=ps_a1, bc_b=ps_a2))
        xn = c.sbuf("xn", [P, DC * T], BF16)
        abrows = c.sbuf("abrows", [1, 2 * T], F32)
        qf = c.sbuf("qf", [P, DC * T], BF16)
        kf = c.sbuf("kf", [P, DC * T], BF16)
        vt = c.sbuf("vt", [P, 16 * 1040], BF16)
        p.emit("vector", lambda s: s.memset(
            vt.t[:].rearrange("p (t h f) -> p t h f", h=16, f=65)[:, :, :, 64:65], 1.0),
            writes=[vt.b])
        qaug2 = [c.sbuf("qaugA", [P, T], BF16), c.sbuf("qaugB", [P, T], BF16)]
        kaug2 = [c.sbuf("kaugA", [P, T], BF16), c.sbuf("kaugB", [P, T], BF16)]
        ptile = c.sbuf("ptile", [P, 2 * 512], BF16)
        ptb2 = [p.buf("pt0"), p.buf("pt1")]
        mcols_2 = [c.sbuf("mcolsA", [P, 16], BF16), c.sbuf("mcolsB", [P, 16], BF16)]
        mcols2 = c.sbuf("mcols2", [P, 64], BF16)
        rrow = c.sbuf("rrow", [1, 512], F32)
        bcs = c.sbuf("bcs", [P, 512], F32)
        ck = c.sbuf("ck", [P, T], BF16)
        sk = c.sbuf("sk", [P, T], BF16)
        nid = c.sbuf("nid", [P, P], BF16)
        cols = c.sbuf("cols", [P, 9 * DC], F32)
        bvbc = c.sbuf("bvbc", [P, D], BF16)
        wb2 = [c.sbuf(f"wpb{i}", [P, DC * P], BF16) for i in range(2)]
        wv_sb = c.sbuf("wv_sb", [P, DC * 256], BF16)
        t1 = c.sbuf("t1", [P, 512], F32)
        t2 = c.sbuf("t2", [P, 512], F32)
        xrt = [t1, t2]
        xout = c.sbuf("xout", [P, 512], F32)
        psP = c.psum("psP", [P, 512], F32)
        psQ = c.psum("psQ", [P, 512], F32)
        psSA = ps_a1
        psSB = ps_a2
        psRow = ps_row
        psO = c.psum("psO", [P, 512], F32)
        psD = c.psum("psD", [P, 512], F32)
        psBC = ps_bc

        init_ln_consts(c, sc)
        ones_bf = sc["ones_bf"]
        ones_f = sc["ones_f32"]

        for dst, src_d in [(ck, ck_d), (sk, sk_d), (nid, nid_d)]:
            p.emit("gpsimd", lambda s, dst=dst, src_d=src_d: s.dma_start(
                out=dst.t[:], in_=src_d.t), reads=[src_d.b], writes=[dst.b], dma=True)
        for i, src_d in enumerate([lnw_d, lnb_d, bq_d, bqs_d, bk_d, bks_d, bo_d]):
            p.emit("sync", lambda s, i=i, src_d=src_d: s.dma_start(
                out=cols.t[:, i * DC:(i + 1) * DC], in_=src_d.t),
                reads=[src_d.b], writes=[cols.b], dma=True)
        LNW, LNB, BQ, BQS, BK, BKS, BO = range(7)
        colsl = lambda i, ch: cols.t[:, i * DC + ch:i * DC + ch + 1]
        p.emit("gpsimd", lambda s: s.dma_start(
            out=bvbc.t[:], in_=bv_d.t.partition_broadcast(P).squeeze(1)),
            reads=[bv_d.b], writes=[bvbc.b], dma=True)

        emit_ln_double(c, sc, x_d, xn, gamma=TB(cols.t[:, LNW * DC:(LNW + 1) * DC], cols.b),
                       beta=TB(cols.t[:, LNB * DC:(LNB + 1) * DC], cols.b),
                       keep_rows=abrows)

        for (wd, wsd, ct, st_, dst, bi, bsi, qscale) in [
                (wq_d, wqs_d, ck, sk, qf, BQ, BQS, True),
                (wk_d, wks_d, ck, sk, kf, BK, BKS, False)]:
            for fo in range(DC):
                wbA, wbB = wb2[0], wb2[1]
                p.emit("gpsimd", lambda s, fo=fo, wbA=wbA, wd=wd: s.dma_start(
                    out=wbA.t[:].rearrange("p (k f) -> p k f", f=P),
                    in_=wd.t.rearrange("(k p) f -> p k f", p=P)[:, :, fo * P:(fo + 1) * P]),
                    reads=[wd.b], writes=[wbA.b], dma=True)
                p.emit("gpsimd", lambda s, fo=fo, wbB=wbB, wsd=wsd: s.dma_start(
                    out=wbB.t[:].rearrange("p (k f) -> p k f", f=P),
                    in_=wsd.t.rearrange("(k p) f -> p k f", p=P)[:, :, fo * P:(fo + 1) * P]),
                    reads=[wsd.b], writes=[wbB.b], dma=True)
                for u in range(4):
                    for k in range(DC):
                        p.emit("tensor", lambda s, fo=fo, u=u, k=k, wbA=wbA: s.matmul(
                            psQ.t[:], wbA.t[:, k * P:(k + 1) * P],
                            xn.t[:, k * T + u * 512: k * T + u * 512 + 512],
                            start=(k == 0), stop=(k == DC - 1)),
                            reads=[wbA.b, xn.b], writes=[psQ.b], tick=(k == DC - 1))
                        p.emit("tensor", lambda s, fo=fo, u=u, k=k, wbB=wbB: s.matmul(
                            psP.t[:], wbB.t[:, k * P:(k + 1) * P],
                            xn.t[:, k * T + u * 512: k * T + u * 512 + 512],
                            start=(k == 0), stop=(k == DC - 1)),
                            reads=[wbB.b, xn.b], writes=[psP.b], tick=(k == DC - 1))
                    us = slice(u * 512, (u + 1) * 512)
                    if qscale:
                        p.emit("vector", lambda s, fo=fo, bi=bi: s.tensor_scalar(
                            t1.t[:], psQ.t[:], colsl(bi, fo), 0.125, ALU.add, ALU.mult),
                            reads=[psQ.b, cols.b], writes=[t1.b])
                        p.emit("vector", lambda s, fo=fo, bsi=bsi: s.tensor_scalar(
                            t2.t[:], psP.t[:], colsl(bsi, fo), 0.125, ALU.add, ALU.mult),
                            reads=[psP.b, cols.b], writes=[t2.b])
                    else:
                        p.emit("vector", lambda s, fo=fo, bi=bi: s.tensor_scalar_add(
                            t1.t[:], psQ.t[:], colsl(bi, fo)),
                            reads=[psQ.b, cols.b], writes=[t1.b])
                        p.emit("vector", lambda s, fo=fo, bsi=bsi: s.tensor_scalar_add(
                            t2.t[:], psP.t[:], colsl(bsi, fo)),
                            reads=[psP.b, cols.b], writes=[t2.b])
                    p.emit("vector", lambda s, us=us, ct=ct: s.tensor_tensor(
                        t1.t[:], t1.t[:], ct.t[:, us], ALU.mult),
                        reads=[t1.b, ct.b], writes=[t1.b])
                    p.emit("vector", lambda s, us=us, st_=st_: s.tensor_tensor(
                        t2.t[:], t2.t[:], st_.t[:, us], ALU.mult),
                        reads=[t2.b, st_.b], writes=[t2.b])
                    p.emit("vector", lambda s, fo=fo, us=us, dst=dst: s.tensor_tensor(
                        dst.t[:, fo * T + us.start: fo * T + us.stop],
                        t1.t[:], t2.t[:], ALU.add),
                        reads=[t1.b, t2.b], writes=[dst.b])
        for quar in range(4):
            p.emit("gpsimd", lambda s, quar=quar: s.dma_start(
                out=wv_sb.t[:].rearrange("p (k f) -> p k f", f=256),
                in_=wv_d.t.rearrange("(k p) f -> p k f", p=P)[:, :, quar * 256:(quar + 1) * 256]),
                reads=[wv_d.b], writes=[wv_sb.b], dma=True)
            for tile in range(16):
                for k in range(DC):
                    p.emit("tensor", lambda s, tile=tile, k=k: s.matmul(
                        psP.t[:, 0:256], xn.t[:, k * T + tile * P: k * T + (tile + 1) * P],
                        wv_sb.t[:, k * 256:(k + 1) * 256],
                        start=(k == 0), stop=(k == DC - 1)),
                        reads=[xn.b, wv_sb.b], writes=[psP.b], tick=(k == DC - 1))
                p.emit("vector", lambda s, tile=tile, quar=quar: s.tensor_tensor(
                    vt.t[:].rearrange("p (t h f) -> p t h f", h=16, f=65)
                    [:, tile, quar * 4:(quar + 1) * 4, 0:64],
                    psP.t[:, 0:256].rearrange("p (h f) -> p h f", f=64),
                    bvbc.t[:, quar * 256:(quar + 1) * 256].rearrange("p (h f) -> p h f", f=64),
                    ALU.add),
                    reads=[psP.b, bvbc.b], writes=[vt.b])

        sh = dict(qaug2=qaug2, kaug2=kaug2, ptile=ptile, ptb2=ptb2,
                  mcols_2=mcols_2, mcols2=mcols2, rrow=rrow, bcs=bcs, nid=nid,
                  ones_bf=ones_bf, ones_f=ones_f, t1=t1,
                  psSA=psSA, psSB=psSB, psRow=psRow, psO=psO, psD=psD, psBC=psBC,
                  p1banks=[psP, psQ, psD])
        emit_attn_heads(c, sh, qf, kf, vt, causal=False)

        for u in range(4):
            p.emit("vector", lambda s, u=u: s.tensor_scalar_add(
                rrow.t[0:1, :], abrows.t[0:1, u * 512:(u + 1) * 512], 1.0),
                reads=[abrows.b], writes=[rrow.b])
            p.emit("tensor", lambda s, u=u: s.matmul(
                ps_a1.t[:], ones_f.t[:], rrow.t[0:1, :], start=True, stop=True),
                reads=[rrow.b, ones_f.b], writes=[ps_a1.b])
            p.emit("tensor", lambda s, u=u: s.matmul(
                ps_a2.t[:], ones_f.t[:],
                abrows.t[0:1, T + u * 512: T + (u + 1) * 512], start=True, stop=True),
                reads=[abrows.b, ones_f.b], writes=[ps_a2.b], tick=True)

            for fo in range(DC):
                wb = wb2[fo % 2]
                p.emit("gpsimd", lambda s, fo=fo, wb=wb: s.dma_start(
                    out=wb.t[:].rearrange("p (k f) -> p k f", f=P),
                    in_=wo_d.t.rearrange("(k p) f -> p k f", p=P)[:, :, fo * P:(fo + 1) * P]),
                    reads=[wo_d.b], writes=[wb.b], dma=True)
                par = fo % 2
                xs = xrt[par].t[:, 0:512]
                p.emit("sync", lambda s, fo=fo, u=u, xs=xs: s.dma_start(
                    out=xs, in_=x_d.t[fo * P:(fo + 1) * P, u * 512:(u + 1) * 512]),
                    reads=[x_d.b], writes=[xrt[par].b], dma=True)
                for k in range(DC):
                    p.emit("tensor", lambda s, fo=fo, u=u, k=k, wb=wb: s.matmul(
                        psP.t[:], wb.t[:, k * P:(k + 1) * P],
                        qf.t[:, k * T + u * 512: k * T + u * 512 + 512],
                        start=(k == 0), stop=(k == DC - 1)),
                        reads=[wb.b, qf.b], writes=[psP.b], tick=(k == DC - 1))
                p.emit("vector", lambda s, xs=xs, par=par: s.tensor_tensor(
                    xout.t[:], xs, ps_a1.t[:], ALU.mult),
                    reads=[xrt[par].b, ps_a1.b], writes=[xout.b])
                p.emit("vector", lambda s: s.tensor_tensor(
                    xout.t[:], xout.t[:], ps_a2.t[:], ALU.add),
                    reads=[xout.b, ps_a2.b], writes=[xout.b])
                p.emit("vector", lambda s, fo=fo: s.tensor_scalar_add(
                    t1.t[:], psP.t[:], colsl(BO, fo)),
                    reads=[psP.b, cols.b], writes=[t1.b])
                p.emit("vector", lambda s: s.tensor_tensor(
                    xout.t[:], xout.t[:], t1.t[:], ALU.add),
                    reads=[xout.b, t1.b], writes=[xout.b])
                p.emit("sync", lambda s, fo=fo, u=u: s.dma_start(
                    out=xo_d.t[fo * P:(fo + 1) * P, u * 512:(u + 1) * 512], in_=xout.t[:]),
                    reads=[xout.b], writes=[xo_d.b], dma=True)
        p.emit("sync", lambda s: s.nop(), reads=[xo_d.b])
        p.barrier()
    c.prefix, c.ctx = old_pfx, old_ctx


def build_attn_a_prog():
    p = Prog("attna")
    nc = p.nc
    ctx = contextlib.ExitStack()
    c = Core(p, ctx)
    x_d = c.dram("x_fm", [D, T], F32, kind="ExternalInput")
    xo_d = c.dram("xo_fm", [D, T], F32, kind="ExternalOutput")
    d = decl_attn_a(c)
    with ctx:
        emit_attn_a_stage(c, d, x_d, xo_d, pfx="")
        p.emit("sync", lambda s: s.nop(), reads=[xo_d.b])
        p.build(ctx)
    return nc

def rope_tables_a():
    hd = HD
    inv = (1.0 / 10000.0 ** (np.arange(0, hd, 2, dtype=np.float32) / hd)).astype(np.float32)
    fr = np.outer(np.arange(T, dtype=np.float32), inv).astype(np.float32)
    cos = np.cos(fr, dtype=np.float32)
    sin = np.sin(fr, dtype=np.float32)
    cos_il = np.repeat(cos, 2, axis=1)
    sin_il = np.repeat(sin, 2, axis=1)
    sgn = np.tile(np.array([-1.0, 1.0], np.float32), 32)
    sin_s = sin_il * sgn[None, :]
    cosf = np.ascontiguousarray(np.tile(cos_il.T, (2, 1)))
    sinf = np.ascontiguousarray(np.tile(sin_s.T, (2, 1)))
    return cosf, sinf


def attn_a_host_inputs(x_batch_fm, ins):
    g = lambda k: np.asarray(ins[k], np.float32)
    Wq, Wk, Wv, Wo = g("L1_attn_a_Wq"), g("L1_attn_a_Wk"), g("L1_attn_a_Wv"), g("L1_attn_a_Wo")
    bq, bk, bv, bo = g("L1_attn_a_bq"), g("L1_attn_a_bk"), g("L1_attn_a_bv"), g("L1_attn_a_bo")
    lnw, lnb = g("L1_attn_a_ln_w"), g("L1_attn_a_ln_b")
    perm = np.arange(D).reshape(NH * 32, 2)[:, ::-1].reshape(-1)
    cosf, sinf = rope_tables_a()
    colify = lambda v: np.ascontiguousarray(v.reshape(DC, P).T)
    return dict(
        x_fm=np.ascontiguousarray(x_batch_fm),
        WqT=np.ascontiguousarray(Wq.T), WqsT=np.ascontiguousarray(Wq.T[:, perm]),
        WkT=np.ascontiguousarray(Wk.T), WksT=np.ascontiguousarray(Wk.T[:, perm]),
        WvT=np.ascontiguousarray(Wv.T), WoT=np.ascontiguousarray(Wo.T),
        cosk=cosf, sinks=sinf,
        negid=np.ascontiguousarray(-np.eye(P, dtype=np.float32)),
        lnw=colify(lnw), lnb=colify(lnb),
        bq=colify(bq), bqs=colify(bq[perm]), bk=colify(bk), bks=colify(bk[perm]),
        bv_row=np.ascontiguousarray(bv.reshape(1, D)), bo=colify(bo),
    )


def decl_attn_b1(c, npfx=""):
    QL, KV = 512, 64
    return {
        "wqd_d": c.dram(npfx + "wqdT", [D, QL], F32, kind="ExternalInput"),
        "wqk_d": c.dram(npfx + "wqkT", [QL, D], F32, kind="ExternalInput"),
        "wkv_d": c.dram(npfx + "wkvT", [D, KV], F32, kind="ExternalInput"),
        "wvu_d": c.dram(npfx + "wvuT", [KV, D], F32, kind="ExternalInput"),
        "wpr_d": c.dram(npfx + "wprT", [D, D], F32, kind="ExternalInput"),
        "bqd_d": c.dram(npfx + "bqd", [P, 4], F32, kind="ExternalInput"),
        "bqk_d": c.dram(npfx + "bqk", [P, DC], F32, kind="ExternalInput"),
        "bkv_d": c.dram(npfx + "bkv", [64, 1], F32, kind="ExternalInput"),
        "bvu_d": c.dram(npfx + "bvu_row", [1, D], F32, kind="ExternalInput"),
        "bpr_d": c.dram(npfx + "bpr", [P, DC], F32, kind="ExternalInput"),
        "nid_d": c.dram(npfx + "negid", [P, P], F32, kind="ExternalInput"),
    }


def emit_attn_b1_stage(c, d, x_d, xo_d, pfx="s7_"):
    p = c.p
    QL, KV = 512, 64
    wqd_d = d["wqd_d"]
    wqk_d = d["wqk_d"]
    wkv_d = d["wkv_d"]
    wvu_d = d["wvu_d"]
    wpr_d = d["wpr_d"]
    bqd_d = d["bqd_d"]
    bqk_d = d["bqk_d"]
    bkv_d = d["bkv_d"]
    bvu_d = d["bvu_d"]
    bpr_d = d["bpr_d"]
    nid_d = d["nid_d"]
    old_pfx, old_ctx = c.prefix, c.ctx
    c.prefix = pfx
    with contextlib.ExitStack() as sctx:
        c.ctx = sctx
        ps_a1 = c.psum("ps_a1", [P, 512], F32)
        ps_a2 = c.psum("ps_a2", [P, 512], F32)
        ps_row = c.psum("ps_row", [P, 512], F32)
        ps_bc = c.psum("ps_bc", [P, 512], F32)
        sc = make_ln_scratch(c, ps_share=dict(
            statm_ps=ps_row, stats_ps=ps_bc, bc_a=ps_a1, bc_b=ps_a2))
        xn = c.sbuf("xn", [P, DC * T], BF16)
        qf = c.sbuf("qf", [P, DC * T], BF16)
        cqf = c.sbuf("cqf", [P, 4 * T], BF16)
        vt = c.sbuf("vt", [P, 16 * 1040], BF16)
        p.emit("vector", lambda s: s.memset(
            vt.t[:].rearrange("p (t h f) -> p t h f", h=16, f=65)[:, :, :, 64:65], 1.0),
            writes=[vt.b])
        qaug2 = [c.sbuf("qaugA", [P, T], BF16), c.sbuf("qaugB", [P, T], BF16)]
        kaug2 = [c.sbuf("kaugA", [P, T], BF16)]
        ptile = c.sbuf("ptile", [P, 2 * 512], BF16)
        ptb2 = [p.buf("pt0"), p.buf("pt1")]
        mcols_2 = [c.sbuf("mcolsA", [P, 16], BF16), c.sbuf("mcolsB", [P, 16], BF16)]
        mcols2 = c.sbuf("mcols2", [P, 64], BF16)
        rrow = c.sbuf("rrow", [1, 512], F32)
        bcs = c.sbuf("bcs", [P, 512], F32)
        nid = c.sbuf("nid", [P, P], BF16)
        cols = c.sbuf("cols", [P, 4 + DC + DC], F32)
        bkvc = c.sbuf("bkvc", [64, 1], F32)
        bvbc = c.sbuf("bvbc", [P, D], BF16)
        wb2 = [c.sbuf(f"wpb{i}", [P, DC * P], BF16) for i in range(2)]
        wqk_sb = c.sbuf("wqk_sb", [P, 4 * P], BF16)
        wkv_sb = c.sbuf("wkv_sb", [P, DC * KV], BF16)
        wvu_sb = c.sbuf("wvu_sb", [64, D], BF16)
        t1 = c.sbuf("t1", [P, 512], F32)
        t2x = c.sbuf("t2x", [P, 512], F32)
        t3x = c.sbuf("t3x", [P, 512], F32)
        xrt = [t2x, t3x]
        xout = c.sbuf("xout", [P, 512], F32)
        psP = c.psum("psP", [P, 512], F32)
        psQ = c.psum("psQ", [P, 512], F32)
        psSA = ps_a1
        psSB = ps_a2
        psRow = ps_row
        psO = c.psum("psO", [P, 512], F32)
        psD = c.psum("psD", [P, 512], F32)
        psBC = ps_bc

        init_ln_consts(c, sc)
        ones_bf = sc["ones_bf"]
        ones_f = sc["ones_f32"]

        p.emit("gpsimd", lambda s: s.dma_start(out=nid.t[:], in_=nid_d.t),
               reads=[nid_d.b], writes=[nid.b], dma=True)
        for i, (src_d, w) in enumerate([(bqd_d, 4), (bqk_d, DC), (bpr_d, DC)]):
            off = [0, 4, 4 + DC][i]
            p.emit("sync", lambda s, src_d=src_d, off=off, w=w: s.dma_start(
                out=cols.t[:, off:off + w], in_=src_d.t),
                reads=[src_d.b], writes=[cols.b], dma=True)
        p.emit("sync", lambda s: s.dma_start(out=bkvc.t[:], in_=bkv_d.t),
               reads=[bkv_d.b], writes=[bkvc.b], dma=True)
        p.emit("gpsimd", lambda s: s.dma_start(
            out=bvbc.t[:], in_=bvu_d.t.partition_broadcast(P).squeeze(1)),
            reads=[bvu_d.b], writes=[bvbc.b], dma=True)
        p.emit("gpsimd", lambda s: s.dma_start(
            out=wkv_sb.t[:].rearrange("p (k f) -> p k f", f=KV),
            in_=wkv_d.t.rearrange("(k p) f -> p k f", p=P)),
            reads=[wkv_d.b], writes=[wkv_sb.b], dma=True)
        p.emit("gpsimd", lambda s: s.dma_start(
            out=wvu_sb.t[0:64, :], in_=wvu_d.t),
            reads=[wvu_d.b], writes=[wvu_sb.b], dma=True)

        emit_ln(c, sc, x_d, xn)

        for fo in range(4):
            wb = wb2[fo % 2]
            p.emit("gpsimd", lambda s, fo=fo, wb=wb: s.dma_start(
                out=wb.t[:].rearrange("p (k f) -> p k f", f=P),
                in_=wqd_d.t.rearrange("(k p) f -> p k f", p=P)[:, :, fo * P:(fo + 1) * P]),
                reads=[wqd_d.b], writes=[wb.b], dma=True)
            for u in range(4):
                for k in range(DC):
                    p.emit("tensor", lambda s, fo=fo, u=u, k=k, wb=wb: s.matmul(
                        psP.t[:], wb.t[:, k * P:(k + 1) * P],
                        xn.t[:, k * T + u * 512: k * T + u * 512 + 512],
                        start=(k == 0), stop=(k == DC - 1)),
                        reads=[wb.b, xn.b], writes=[psP.b], tick=(k == DC - 1))
                p.emit("vector", lambda s, fo=fo, u=u: s.tensor_scalar_add(
                    cqf.t[:, fo * T + u * 512: fo * T + u * 512 + 512],
                    psP.t[:], cols.t[:, fo:fo + 1]),
                    reads=[psP.b, cols.b], writes=[cqf.b])
        for u in range(4):
            for k in range(DC):
                p.emit("tensor", lambda s, u=u, k=k: s.matmul(
                    psQ.t[0:64, :], wkv_sb.t[:, k * KV:(k + 1) * KV],
                    xn.t[:, k * T + u * 512: k * T + u * 512 + 512],
                    start=(k == 0), stop=(k == DC - 1)),
                    reads=[wkv_sb.b, xn.b], writes=[psQ.b], tick=(k == DC - 1))
            p.emit("vector", lambda s, u=u: s.tensor_scalar_add(
                kaug2[0].t[0:64, u * 512:(u + 1) * 512], psQ.t[0:64, :], bkvc.t[0:64, 0:1]),
                reads=[psQ.b, bkvc.b], writes=[kaug2[0].b])
        p.emit("vector", lambda s: s.memset(kaug2[0].t[64:65, :], 1.0), writes=[kaug2[0].b])
        for fo in range(DC):
            p.emit("gpsimd", lambda s, fo=fo: s.dma_start(
                out=wqk_sb.t[:].rearrange("p (k f) -> p k f", f=P),
                in_=wqk_d.t.rearrange("(k p) f -> p k f", p=P)[:, :, fo * P:(fo + 1) * P]),
                reads=[wqk_d.b], writes=[wqk_sb.b], dma=True)
            for u in range(4):
                for k in range(4):
                    p.emit("tensor", lambda s, fo=fo, u=u, k=k: s.matmul(
                        psP.t[:], wqk_sb.t[:, k * P:(k + 1) * P],
                        cqf.t[:, k * T + u * 512: k * T + u * 512 + 512],
                        start=(k == 0), stop=(k == 3)),
                        reads=[wqk_sb.b, cqf.b], writes=[psP.b], tick=(k == 3))
                p.emit("vector", lambda s, fo=fo, u=u: s.tensor_scalar(
                    qf.t[:, fo * T + u * 512: fo * T + u * 512 + 512],
                    psP.t[:], cols.t[:, 4 + fo:4 + fo + 1], 0.125, ALU.add, ALU.mult),
                    reads=[psP.b, cols.b], writes=[qf.b])
        for tile in range(16):
            for half in range(2):
                p.emit("tensor", lambda s, tile=tile, half=half: s.matmul(
                    psP.t[:], kaug2[0].t[0:64, tile * P:(tile + 1) * P],
                    wvu_sb.t[0:64, half * 512:(half + 1) * 512],
                    start=True, stop=True),
                    reads=[kaug2[0].b, wvu_sb.b], writes=[psP.b], tick=True)
                p.emit("vector", lambda s, tile=tile, half=half: s.tensor_tensor(
                    vt.t[:].rearrange("p (t h f) -> p t h f", h=16, f=65)
                    [:, tile, half * 8:(half + 1) * 8, 0:64],
                    psP.t[:].rearrange("p (h f) -> p h f", f=64),
                    bvbc.t[:, half * 512:(half + 1) * 512].rearrange("p (h f) -> p h f", f=64),
                    ALU.add),
                    reads=[psP.b, bvbc.b], writes=[vt.b])

        sh = dict(qaug2=qaug2, kaug2=kaug2, ptile=ptile, ptb2=ptb2,
                  mcols_2=mcols_2, mcols2=mcols2, rrow=rrow, bcs=bcs, nid=nid,
                  ones_bf=ones_bf, ones_f=ones_f, t1=t1,
                  psSA=psSA, psSB=psSB, psRow=psRow, psO=psO, psD=psD, psBC=psBC,
                  p1banks=[psP, psQ, psD])
        emit_attn_heads(c, sh, qf, kaug2[0], vt, causal=False,
                        kaug_prebuilt=True, kf_single=True)

        for fo in range(DC):
            wb = wb2[fo % 2]
            p.emit("gpsimd", lambda s, fo=fo, wb=wb: s.dma_start(
                out=wb.t[:].rearrange("p (k f) -> p k f", f=P),
                in_=wpr_d.t.rearrange("(k p) f -> p k f", p=P)[:, :, fo * P:(fo + 1) * P]),
                reads=[wpr_d.b], writes=[wb.b], dma=True)
            for u in range(4):
                par = u % 2
                xs = xrt[par].t[:, 0:512]
                p.emit("sync", lambda s, fo=fo, u=u, xs=xs: s.dma_start(
                    out=xs, in_=x_d.t[fo * P:(fo + 1) * P, u * 512:(u + 1) * 512]),
                    reads=[x_d.b], writes=[xrt[par].b], dma=True)
                for k in range(DC):
                    p.emit("tensor", lambda s, fo=fo, u=u, k=k, wb=wb: s.matmul(
                        psP.t[:], wb.t[:, k * P:(k + 1) * P],
                        qf.t[:, k * T + u * 512: k * T + u * 512 + 512],
                        start=(k == 0), stop=(k == DC - 1)),
                        reads=[wb.b, qf.b], writes=[psP.b], tick=(k == DC - 1))
                p.emit("vector", lambda s, fo=fo: s.tensor_scalar_add(
                    t1.t[:], psP.t[:], cols.t[:, 4 + DC + fo:4 + DC + fo + 1]),
                    reads=[psP.b, cols.b], writes=[t1.b])
                p.emit("vector", lambda s, xs=xs, par=par: s.tensor_tensor(
                    xout.t[:], t1.t[:], xs, ALU.add),
                    reads=[t1.b, xrt[par].b], writes=[xout.b])
                p.emit("sync", lambda s, fo=fo, u=u: s.dma_start(
                    out=xo_d.t[fo * P:(fo + 1) * P, u * 512:(u + 1) * 512], in_=xout.t[:]),
                    reads=[xout.b], writes=[xo_d.b], dma=True)
        p.emit("sync", lambda s: s.nop(), reads=[xo_d.b])
        p.barrier()
    c.prefix, c.ctx = old_pfx, old_ctx


def build_attn_b1_prog():
    p = Prog("attnb1")
    nc = p.nc
    ctx = contextlib.ExitStack()
    c = Core(p, ctx)
    x_d = c.dram("x_fm", [D, T], F32, kind="ExternalInput")
    xo_d = c.dram("xo_fm", [D, T], F32, kind="ExternalOutput")
    d = decl_attn_b1(c)
    with ctx:
        emit_attn_b1_stage(c, d, x_d, xo_d, pfx="")
        p.emit("sync", lambda s: s.nop(), reads=[xo_d.b])
        p.build(ctx)
    return nc

def attn_b1_host_inputs(x_batch_fm, ins):
    g = lambda k: np.asarray(ins[k], np.float32)
    colify = lambda v, n: np.ascontiguousarray(v.reshape(n, P).T)
    return dict(
        x_fm=np.ascontiguousarray(x_batch_fm),
        wqdT=np.ascontiguousarray(g("L1_attn_b_wq_d").T),
        wqkT=np.ascontiguousarray(g("L1_attn_b_w_qk").T),
        wkvT=np.ascontiguousarray(g("L1_attn_b_wkv_d").T),
        wvuT=np.ascontiguousarray(g("L1_attn_b_wv_u").T),
        wprT=np.ascontiguousarray(g("L1_attn_b_w_proj").T),
        bqd=colify(g("L1_attn_b_bq_d"), 4),
        bqk=colify(g("L1_attn_b_b_qk"), DC),
        bkv=np.ascontiguousarray(g("L1_attn_b_bkv_d").reshape(64, 1)),
        bvu_row=np.ascontiguousarray(g("L1_attn_b_bv_u").reshape(1, D)),
        bpr=colify(g("L1_attn_b_b_proj"), DC),
        negid=np.ascontiguousarray(-np.eye(P, dtype=np.float32)),
    )


def decl_cosformer(c, npfx=""):
    return {
        "wq_d": c.dram(npfx + "Wq", [D, D], F32, kind="ExternalInput"),
        "wk_d": c.dram(npfx + "Wk", [D, D], F32, kind="ExternalInput"),
        "wv_d": c.dram(npfx + "Wv", [D, D], F32, kind="ExternalInput"),
        "wo_d": c.dram(npfx + "Wo", [D, D], F32, kind="ExternalInput"),
        "bq_d": c.dram(npfx + "bq", [P, DC], F32, kind="ExternalInput"),
        "bk_d": c.dram(npfx + "bk", [P, DC], F32, kind="ExternalInput"),
        "bv_d": c.dram(npfx + "bv_row", [1, D], F32, kind="ExternalInput"),
        "bo_d": c.dram(npfx + "bo", [P, DC], F32, kind="ExternalInput"),
        "cosb_d": c.dram(npfx + "cosb", [P, T], F32, kind="ExternalInput"),
        "sinb_d": c.dram(npfx + "sinb", [P, T], F32, kind="ExternalInput"),
        "tri_d": c.dram(npfx + "tri", [P, P], F32, kind="ExternalInput"),
        "id64_d": c.dram(npfx + "id64", [P, 64], F32, kind="ExternalInput"),
    }


def emit_cosformer_stage(c, d, x_d, xo_d, pfx="s1_"):
    p = c.p
    wq_d = d["wq_d"]
    wk_d = d["wk_d"]
    wv_d = d["wv_d"]
    wo_d = d["wo_d"]
    bq_d = d["bq_d"]
    bk_d = d["bk_d"]
    bv_d = d["bv_d"]
    bo_d = d["bo_d"]
    cosb_d = d["cosb_d"]
    sinb_d = d["sinb_d"]
    tri_d = d["tri_d"]
    id64_d = d["id64_d"]
    old_pfx, old_ctx = c.prefix, c.ctx
    c.prefix = pfx
    with contextlib.ExitStack() as sctx:
        c.ctx = sctx
        ps_a1 = c.psum("ps_a1", [P, 512], F32)
        ps_a2 = c.psum("ps_a2", [P, 512], F32)
        ps_row = c.psum("ps_row", [P, 512], F32)
        ps_bc = c.psum("ps_bc", [P, 512], F32)
        sc = make_ln_scratch(c, ps_share=dict(
            statm_ps=ps_row, stats_ps=ps_bc, bc_a=ps_a1, bc_b=ps_a2))
        xn = c.sbuf("xn", [P, DC * T], BF16)
        qf = c.sbuf("qf", [P, DC * T], BF16)
        kf = c.sbuf("kf", [P, DC * T], BF16)
        vt = c.sbuf("vt", [P, 16 * D], BF16)
        qc0 = c.sbuf("qc0", [64, T], BF16)
        qs0 = c.sbuf("qs0", [64, T], BF16)
        kc0 = c.sbuf("kc0", [64, T], BF16)
        ks0 = c.sbuf("ks0", [64, T], BF16)
        h0 = c.sbuf("h0", [64, T], BF16)
        at_bf = c.sbuf("at_bf", [P, P], BF16)
        kct = c.sbuf("kct", [P, 2 * 64], BF16)
        scos = c.sbuf("scos", [64, 130], BF16)
        cosb = c.sbuf("cosb_sb", [P, T], BF16)
        sinb = c.sbuf("sinb_sb", [P, T], BF16)
        tri = c.sbuf("tri_sb", [P, P], BF16)
        id64 = c.sbuf("id64_sb", [P, 64], BF16)
        cols = c.sbuf("cols", [P, 3 * DC], F32)
        bvbc = c.sbuf("bvbc", [P, D], BF16)
        wb2 = [c.sbuf(f"wpb{i}", [P, DC * P], BF16) for i in range(2)]
        wv_sb = c.sbuf("wv_sb", [P, DC * 512], BF16)
        rrow = c.sbuf("rrow", [1, P], F32)
        bcs = c.sbuf("bcs", [64, P], F32)
        t1 = c.sbuf("t1", [P, 512], F32)
        t2x = c.sbuf("t2x", [P, 512], F32)
        t3x = c.sbuf("t3x", [P, 512], F32)
        xrt = [t2x, t3x]
        xout = c.sbuf("xout", [P, 512], F32)
        psP = ps_row
        psS = ps_a1
        psT = TB(ps_a2.t[:].bitcast(BF16), ps_a2.b)
        psO = c.psum("psO", [P, 512], F32)
        psD = c.psum("psD", [P, 512], F32)
        psSt = c.psum("psSt", [64, 130], F32)
        psBC = ps_bc

        init_ln_consts(c, sc)
        ones_bf = sc["ones_bf"]
        ones_f = sc["ones_f32"]

        for dst, src_d in [(cosb, cosb_d), (sinb, sinb_d), (tri, tri_d), (id64, id64_d)]:
            p.emit("gpsimd", lambda s, dst=dst, src_d=src_d: s.dma_start(
                out=dst.t[:], in_=src_d.t), reads=[src_d.b], writes=[dst.b], dma=True)
        for i, src_d in enumerate([bq_d, bk_d, bo_d]):
            p.emit("sync", lambda s, i=i, src_d=src_d: s.dma_start(
                out=cols.t[:, i * DC:(i + 1) * DC], in_=src_d.t),
                reads=[src_d.b], writes=[cols.b], dma=True)
        p.emit("gpsimd", lambda s: s.dma_start(
            out=bvbc.t[:], in_=bv_d.t.partition_broadcast(P).squeeze(1)),
            reads=[bv_d.b], writes=[bvbc.b], dma=True)

        emit_ln(c, sc, x_d, xn)

        for (wd, dst, bi) in [(wq_d, qf, 0), (wk_d, kf, 1)]:
            for fo in range(DC):
                wb = wb2[fo % 2]
                p.emit("gpsimd", lambda s, fo=fo, wb=wb, wd=wd: s.dma_start(
                    out=wb.t[:].rearrange("p (k f) -> p k f", f=P),
                    in_=wd.t.rearrange("(k p) f -> p k f", p=P)[:, :, fo * P:(fo + 1) * P]),
                    reads=[wd.b], writes=[wb.b], dma=True)
                for u in range(4):
                    for k in range(DC):
                        p.emit("tensor", lambda s, fo=fo, u=u, k=k, wb=wb: s.matmul(
                            psP.t[:], wb.t[:, k * P:(k + 1) * P],
                            xn.t[:, k * T + u * 512: k * T + u * 512 + 512],
                            start=(k == 0), stop=(k == DC - 1)),
                            reads=[wb.b, xn.b], writes=[psP.b], tick=(k == DC - 1))
                    p.emit("vector", lambda s, fo=fo, u=u, dst=dst, bi=bi: s.tensor_scalar(
                        dst.t[:, fo * T + u * 512: fo * T + u * 512 + 512],
                        psP.t[:], cols.t[:, bi * DC + fo:bi * DC + fo + 1], 0.0,
                        ALU.add, ALU.max),
                        reads=[psP.b, cols.b], writes=[dst.b])
        for half in range(2):
            p.emit("gpsimd", lambda s, half=half: s.dma_start(
                out=wv_sb.t[:].rearrange("p (k f) -> p k f", f=512),
                in_=wv_d.t.rearrange("(k p) f -> p k f", p=P)[:, :, half * 512:(half + 1) * 512]),
                reads=[wv_d.b], writes=[wv_sb.b], dma=True)
            for tile in range(16):
                for k in range(DC):
                    p.emit("tensor", lambda s, tile=tile, k=k: s.matmul(
                        psP.t[:], xn.t[:, k * T + tile * P: k * T + (tile + 1) * P],
                        wv_sb.t[:, k * 512:(k + 1) * 512],
                        start=(k == 0), stop=(k == DC - 1)),
                        reads=[xn.b, wv_sb.b], writes=[psP.b], tick=(k == DC - 1))
                p.emit("vector", lambda s, tile=tile, half=half: s.tensor_tensor(
                    vt.t[:, tile * D + half * 512: tile * D + half * 512 + 512],
                    psP.t[:], bvbc.t[:, half * 512:(half + 1) * 512], ALU.add),
                    reads=[psP.b, bvbc.b], writes=[vt.b])

        for h in range(NH):
            hp, hi = h // 2, h % 2
            base = hi * 64
            for (srct, cdst, sdst) in [(qf, qc0, qs0), (kf, kc0, ks0)]:
                p.emit("sync", lambda s, base=base, hp=hp, srct=srct: s.dma_start(
                    out=h0.t[0:64, :], in_=srct.t[base:base + 64, hp * T:(hp + 1) * T]),
                    reads=[srct.b], writes=[h0.b], dma=True)
                p.emit("vector", lambda s, cdst=cdst: s.tensor_tensor(
                    cdst.t[0:64, :], h0.t[0:64, :], cosb.t[0:64, :], ALU.mult),
                    reads=[h0.b, cosb.b], writes=[cdst.b])
                p.emit("vector", lambda s, sdst=sdst: s.tensor_tensor(
                    sdst.t[0:64, :], h0.t[0:64, :], sinb.t[0:64, :], ALU.mult),
                    reads=[h0.b, sinb.b], writes=[sdst.b])
            p.emit("vector", lambda s: s.memset(scos.t[:], 0.0), writes=[scos.b])
            for ci in range(16):
                cs = slice(ci * P, (ci + 1) * P)
                p.emit("tensor", lambda s, cs=cs: s.matmul(
                    psS.t[:, 0:P], kc0.t[0:64, cs], qc0.t[0:64, cs],
                    start=True, stop=False),
                    reads=[kc0.b, qc0.b], writes=[psS.b])
                p.emit("tensor", lambda s, cs=cs: s.matmul(
                    psS.t[:, 0:P], ks0.t[0:64, cs], qs0.t[0:64, cs],
                    start=False, stop=True),
                    reads=[ks0.b, qs0.b], writes=[psS.b], tick=True)
                p.emit("vector", lambda s: s.tensor_tensor(
                    at_bf.t[:], psS.t[:, 0:P], tri.t[:], ALU.mult),
                    reads=[psS.b, tri.b], writes=[at_bf.b])
                p.emit("tensor", lambda s, ci=ci, h=h: s.matmul(
                    psO.t[0:64, 0:P], vt.t[:, ci * D + h * 64: ci * D + h * 64 + 64],
                    at_bf.t[:], start=True, stop=False),
                    reads=[vt.b, at_bf.b], writes=[psO.b])
                p.emit("tensor", lambda s, cs=cs: s.matmul(
                    psO.t[0:64, 0:P], scos.t[0:64, 0:64], qc0.t[0:64, cs],
                    start=False, stop=False),
                    reads=[scos.b, qc0.b], writes=[psO.b])
                p.emit("tensor", lambda s, cs=cs: s.matmul(
                    psO.t[0:64, 0:P], scos.t[0:64, 65:129], qs0.t[0:64, cs],
                    start=False, stop=True),
                    reads=[scos.b, qs0.b], writes=[psO.b], tick=True)
                p.emit("tensor", lambda s: s.matmul(
                    psD.t[64:65, 0:P], ones_bf.t[:], at_bf.t[:],
                    start=True, stop=False, tile_position=(0, 64)),
                    reads=[ones_bf.b, at_bf.b], writes=[psD.b])
                p.emit("tensor", lambda s, cs=cs: s.matmul(
                    psD.t[64:65, 0:P], scos.t[0:64, 64:65], qc0.t[0:64, cs],
                    start=False, stop=False, tile_position=(0, 64)),
                    reads=[scos.b, qc0.b], writes=[psD.b])
                p.emit("tensor", lambda s, cs=cs: s.matmul(
                    psD.t[64:65, 0:P], scos.t[0:64, 129:130], qs0.t[0:64, cs],
                    start=False, stop=True, tile_position=(0, 64)),
                    reads=[scos.b, qs0.b], writes=[psD.b], tick=True)
                p.emit("vector", lambda s: s.tensor_scalar_max(
                    rrow.t[0:1, :], psD.t[64:65, 0:P], 1e-6),
                    reads=[psD.b], writes=[rrow.b])
                p.emit("vector", lambda s: s.reciprocal(rrow.t[0:1, :], rrow.t[0:1, :]),
                       reads=[rrow.b], writes=[rrow.b])
                p.emit("tensor", lambda s: s.matmul(
                    psBC.t[0:64, 0:P], ones_f.t[0:1, 0:64], rrow.t[0:1, :],
                    start=True, stop=True),
                    reads=[ones_f.b, rrow.b], writes=[psBC.b], tick=True)
                p.emit("vector", lambda s: s.tensor_copy(bcs.t[0:64, :], psBC.t[0:64, 0:P]),
                       reads=[psBC.b], writes=[bcs.b])
                p.emit("vector", lambda s, base=base, hp=hp, cs=cs: s.tensor_tensor(
                    qf.t[base:base + 64, hp * T + cs.start: hp * T + cs.stop],
                    psO.t[0:64, 0:P], bcs.t[0:64, :], ALU.mult),
                    reads=[psO.b, bcs.b], writes=[qf.b])
                p.emit("tensor", lambda s, cs=cs: s.transpose(
                    psT.t[:, 0:64], kc0.t[0:64, cs], id64.t[0:64, :]),
                    reads=[kc0.b, id64.b], writes=[psT.b], tick=True)
                p.emit("vector", lambda s: s.tensor_copy(kct.t[:, 0:64], psT.t[:, 0:64]),
                       reads=[psT.b], writes=[kct.b])
                p.emit("tensor", lambda s, cs=cs: s.transpose(
                    psT.t[:, 0:64], ks0.t[0:64, cs], id64.t[0:64, :]),
                    reads=[ks0.b, id64.b], writes=[psT.b], tick=True)
                p.emit("vector", lambda s: s.tensor_copy(kct.t[:, 64:128], psT.t[:, 0:64]),
                       reads=[psT.b], writes=[kct.b])
                st0 = (ci == 0)
                p.emit("tensor", lambda s, ci=ci, h=h, st0=st0: s.matmul(
                    psSt.t[:, 0:64], kct.t[:, 0:64],
                    vt.t[:, ci * D + h * 64: ci * D + h * 64 + 64],
                    start=st0, stop=False, skip_group_check=True),
                    reads=[kct.b, vt.b], writes=[psSt.b])
                p.emit("tensor", lambda s, st0=st0: s.matmul(
                    psSt.t[:, 64:65], kct.t[:, 0:64], ones_bf.t[:],
                    start=False, stop=False, skip_group_check=True),
                    reads=[kct.b, ones_bf.b], writes=[psSt.b])
                p.emit("tensor", lambda s, ci=ci, h=h: s.matmul(
                    psSt.t[:, 65:129], kct.t[:, 64:128],
                    vt.t[:, ci * D + h * 64: ci * D + h * 64 + 64],
                    start=False, stop=False, skip_group_check=True),
                    reads=[kct.b, vt.b], writes=[psSt.b])
                p.emit("tensor", lambda s, ci=ci: s.matmul(
                    psSt.t[:, 129:130], kct.t[:, 64:128], ones_bf.t[:],
                    start=False, stop=(ci == 15), skip_group_check=True),
                    reads=[kct.b, ones_bf.b], writes=[psSt.b], tick=True)
                if ci < 15:
                    p.emit("vector", lambda s: s.tensor_copy(scos.t[:], psSt.t[:]),
                           reads=[psSt.b], writes=[scos.b])

        for fo in range(DC):
            wb = wb2[fo % 2]
            p.emit("gpsimd", lambda s, fo=fo, wb=wb: s.dma_start(
                out=wb.t[:].rearrange("p (k f) -> p k f", f=P),
                in_=wo_d.t.rearrange("(k p) f -> p k f", p=P)[:, :, fo * P:(fo + 1) * P]),
                reads=[wo_d.b], writes=[wb.b], dma=True)
            for u in range(4):
                par = u % 2
                xs = xrt[par].t[:, 0:512]
                p.emit("sync", lambda s, fo=fo, u=u, xs=xs: s.dma_start(
                    out=xs, in_=x_d.t[fo * P:(fo + 1) * P, u * 512:(u + 1) * 512]),
                    reads=[x_d.b], writes=[xrt[par].b], dma=True)
                for k in range(DC):
                    p.emit("tensor", lambda s, fo=fo, u=u, k=k, wb=wb: s.matmul(
                        psP.t[:], wb.t[:, k * P:(k + 1) * P],
                        qf.t[:, k * T + u * 512: k * T + u * 512 + 512],
                        start=(k == 0), stop=(k == DC - 1)),
                        reads=[wb.b, qf.b], writes=[psP.b], tick=(k == DC - 1))
                p.emit("vector", lambda s, fo=fo: s.tensor_scalar_add(
                    t1.t[:], psP.t[:], cols.t[:, 2 * DC + fo:2 * DC + fo + 1]),
                    reads=[psP.b, cols.b], writes=[t1.b])
                p.emit("vector", lambda s, xs=xs, par=par: s.tensor_tensor(
                    xout.t[:], t1.t[:], xs, ALU.add),
                    reads=[t1.b, xrt[par].b], writes=[xout.b])
                p.emit("sync", lambda s, fo=fo, u=u: s.dma_start(
                    out=xo_d.t[fo * P:(fo + 1) * P, u * 512:(u + 1) * 512], in_=xout.t[:]),
                    reads=[xout.b], writes=[xo_d.b], dma=True)
        p.emit("sync", lambda s: s.nop(), reads=[xo_d.b])
        p.barrier()
    c.prefix, c.ctx = old_pfx, old_ctx


def build_cosformer_prog():
    p = Prog("cosf")
    nc = p.nc
    ctx = contextlib.ExitStack()
    c = Core(p, ctx)
    x_d = c.dram("x_fm", [D, T], F32, kind="ExternalInput")
    xo_d = c.dram("xo_fm", [D, T], F32, kind="ExternalOutput")
    d = decl_cosformer(c)
    with ctx:
        emit_cosformer_stage(c, d, x_d, xo_d, pfx="")
        p.emit("sync", lambda s: s.nop(), reads=[xo_d.b])
        p.build(ctx)
    return nc

def cosformer_host_inputs(x_batch_fm, ins):
    g = lambda k: np.asarray(ins[k], np.float32)
    colify = lambda v: np.ascontiguousarray(v.reshape(DC, P).T)
    ang = (np.pi / (2.0 * T) * np.arange(T, dtype=np.float32)).astype(np.float32)
    cosr = np.cos(ang, dtype=np.float32)
    sinr = np.sin(ang, dtype=np.float32)
    tri = (np.arange(P)[:, None] <= np.arange(P)[None, :]).astype(np.float32)
    id64 = np.tile(np.eye(64, dtype=np.float32), (2, 1))
    return dict(
        x_fm=np.ascontiguousarray(x_batch_fm),
        Wq=np.ascontiguousarray(g("L0_attn_a_Wq")), Wk=np.ascontiguousarray(g("L0_attn_a_Wk")),
        Wv=np.ascontiguousarray(g("L0_attn_a_Wv")), Wo=np.ascontiguousarray(g("L0_attn_a_Wo")),
        bq=colify(g("L0_attn_a_bq")), bk=colify(g("L0_attn_a_bk")),
        bv_row=np.ascontiguousarray(g("L0_attn_a_bv").reshape(1, D)),
        bo=colify(g("L0_attn_a_bo")),
        cosb=np.ascontiguousarray(np.tile(cosr, (P, 1))),
        sinb=np.ascontiguousarray(np.tile(sinr, (P, 1))),
        tri=tri, id64=np.ascontiguousarray(id64),
    )


def decl_moe_prep(c, npfx=""):
    return {
        "pw_d": c.dram(npfx + "pwT", [D, 8], F32, kind="ExternalInput"),
        "pb_d": c.dram(npfx + "pb_row", [1, 8], F32, kind="ExternalInput"),
        "z_d": c.dram(npfx + "z_my", [T, 8], F32, kind="ExternalOutput"),
    }


def emit_moe_prep_stage(c, d, x_d, pfx="zp_"):
    p = c.p
    pw_d = d["pw_d"]
    pb_d = d["pb_d"]
    z_d = d["z_d"]
    old_pfx, old_ctx = c.prefix, c.ctx
    c.prefix = pfx
    with contextlib.ExitStack() as sctx:
        c.ctx = sctx
        sc = make_ln_scratch(c)
        xn = c.sbuf("xn", [P, DC * T], BF16)
        pw_sb = c.sbuf("pw_sb", [P, DC * 8], BF16)
        pbbc = c.sbuf("pbbc", [P, 8], F32)
        zt = c.sbuf("zt", [P, 8], F32)
        psZ = c.psum("psZ", [P, 8], F32)
        init_ln_consts(c, sc)
        p.emit("gpsimd", lambda s: s.dma_start(
            out=pw_sb.t[:].rearrange("p (k f) -> p k f", f=8),
            in_=pw_d.t.rearrange("(k p) f -> p k f", p=P)),
            reads=[pw_d.b], writes=[pw_sb.b], dma=True)
        p.emit("sync", lambda s: s.dma_start(
            out=pbbc.t[:], in_=pb_d.t.partition_broadcast(P).squeeze(1)),
            reads=[pb_d.b], writes=[pbbc.b], dma=True)
        emit_ln(c, sc, x_d, xn)
        for tile in range(16):
            for k in range(DC):
                p.emit("tensor", lambda s, tile=tile, k=k: s.matmul(
                    psZ.t[:], xn.t[:, k * T + tile * P: k * T + (tile + 1) * P],
                    pw_sb.t[:, k * 8:(k + 1) * 8],
                    start=(k == 0), stop=(k == DC - 1)),
                    reads=[xn.b, pw_sb.b], writes=[psZ.b], tick=(k == DC - 1))
            p.emit("vector", lambda s, tile=tile: s.tensor_tensor(
                zt.t[:], psZ.t[:], pbbc.t[:], ALU.add),
                reads=[psZ.b, pbbc.b], writes=[zt.b])
            p.emit("sync", lambda s, tile=tile: s.dma_start(
                out=z_d.t[tile * P:(tile + 1) * P, :], in_=zt.t[:]),
                reads=[zt.b], writes=[z_d.b], dma=True)
        p.emit("sync", lambda s: s.nop(), reads=[z_d.b])
        p.barrier()
    c.prefix, c.ctx = old_pfx, old_ctx


def build_moe_prep_prog():
    p = Prog("moeprep")
    nc = p.nc
    ctx = contextlib.ExitStack()
    c = Core(p, ctx)
    x_d = c.dram("x_fm", [D, T], F32, kind="ExternalInput")
    d = decl_moe_prep(c)
    with ctx:
        emit_moe_prep_stage(c, d, x_d, pfx="")
        p.emit("sync", lambda s: s.nop(), reads=[d["z_d"].b])
        p.build(ctx)
    return nc

def emit_moe_routing(c, z_sb, keepw, sc, consts):
    p = c.p
    E = 8
    CAP = 5120.0
    u1, u2, u3 = consts["u1"], consts["u2"], consts["u3"]
    rowT = consts["rowT"]
    psA = consts["psA"]
    psB = consts["psB"]
    ebias_bc, erow_bc = consts["ebias_bc"], consts["erow_bc"]
    ltmask, ones_col = consts["ltmask"], consts["ones_col"]
    idf = consts["idf"]
    p.emit("scalar", lambda s: s.activation(u1.t[:], z_sb.t[:], AF.Sigmoid),
           reads=[z_sb.b], writes=[u1.b])
    p.emit("vector", lambda s: s.tensor_tensor(u1.t[:], u1.t[:], ebias_bc.t[:], ALU.add),
           reads=[u1.b, ebias_bc.b], writes=[u1.b])
    p.emit("vector", lambda s: s.tensor_scalar_max(u1.t[:], u1.t[:], 1e-12),
           reads=[u1.b], writes=[u1.b])
    p.emit("vector", lambda s: s.tensor_copy(u2.t[:], erow_bc.t[:]),
           reads=[erow_bc.b], writes=[u2.b])
    for a in range(E):
        for b_ in range(a + 1, E):
            p.emit("vector", lambda s, a=a, b_=b_: s.tensor_tensor(
                u3.t[:, 0:1024:8], u1.t[:, b_:1024:8], u1.t[:, a:1024:8], ALU.is_gt),
                reads=[u1.b], writes=[u3.b])
            p.emit("vector", lambda s, a=a: s.tensor_tensor(
                u2.t[:, a:1024:8], u2.t[:, a:1024:8], u3.t[:, 0:1024:8], ALU.add),
                reads=[u2.b, u3.b], writes=[u2.b])
            p.emit("vector", lambda s, b_=b_: s.tensor_tensor(
                u2.t[:, b_:1024:8], u2.t[:, b_:1024:8], u3.t[:, 0:1024:8], ALU.subtract),
                reads=[u2.b, u3.b], writes=[u2.b])
    p.emit("vector", lambda s: s.tensor_single_scalar(
        u2.t[:], u2.t[:], 1.5, ALU.is_le), reads=[u2.b], writes=[u2.b])
    p.emit("vector", lambda s: s.tensor_tensor(u3.t[:], u2.t[:], u1.t[:], ALU.mult),
           reads=[u2.b, u1.b], writes=[u3.b])
    p.emit("vector", lambda s: s.tensor_tensor(
        psB.t[:, 0:P], u3.t[:, 0:1024:8], u3.t[:, 1:1024:8], ALU.add),
        reads=[u3.b], writes=[psB.b])
    for e in range(2, E):
        p.emit("vector", lambda s, e=e: s.tensor_tensor(
            psB.t[:, 0:P], psB.t[:, 0:P], u3.t[:, e:1024:8], ALU.add),
            reads=[psB.b, u3.b], writes=[psB.b])
    p.emit("vector", lambda s: s.tensor_scalar_max(psB.t[:, 0:P], psB.t[:, 0:P], 1e-12),
           reads=[psB.b], writes=[psB.b])
    p.emit("vector", lambda s: s.reciprocal(psB.t[:, P:2 * P], psB.t[:, 0:P]),
           reads=[psB.b], writes=[psB.b])
    for e in range(E):
        p.emit("vector", lambda s, e=e: s.tensor_tensor(
            u3.t[:, e:1024:8], u3.t[:, e:1024:8], psB.t[:, P:2 * P], ALU.mult),
            reads=[u3.b, psB.b], writes=[u3.b])
    p.emit("vector", lambda s: s.tensor_tensor(u1.t[:], u3.t[:], u2.t[:], ALU.add),
           reads=[u3.b, u2.b], writes=[u1.b])
    p.emit("vector", lambda s: s.tensor_scalar_add(u1.t[:], u1.t[:], -1.0),
           reads=[u1.b], writes=[u1.b])
    cand_sb = consts["cand_sb"]
    quota_sb = consts["quota_sb"]
    off_sb = consts["off_sb"]

    def count_ge(cand_cols_off):
        for e in range(E):
            p.emit("vector", lambda s, e=e: s.tensor_scalar(
                u3.t[:, e:1024:8], u1.t[:, e:1024:8],
                cand_sb.t[:, e:e + 1], None, ALU.is_ge),
                reads=[u1.b, cand_sb.b], writes=[u3.b])
        for e in range(E):
            p.emit("vector", lambda s, e=e: s.tensor_reduce(
                u2.t[:, e:e + 1], u3.t[:, e:1024:8], AX.X, ALU.add),
                reads=[u3.b], writes=[u2.b])
        p.emit("tensor", lambda s: s.matmul(
            psB.t[0:1, 256:264], ones_col.t[:], u2.t[:, 0:8], start=True, stop=True),
            reads=[ones_col.b, u2.b], writes=[psB.b], tick=True)
        p.emit("vector", lambda s: s.tensor_copy(rowT.t[0:1, 32:40], psB.t[0:1, 256:264]),
               reads=[psB.b], writes=[rowT.b])

    def bcast_cand():
        p.emit("tensor", lambda s: s.matmul(
            psA.t[:, 0:8], consts["ones_f"].t[:], rowT.t[0:1, 16:24],
            start=True, stop=True),
            reads=[rowT.b, consts["ones_f"].b], writes=[psA.b], tick=True)
        p.emit("vector", lambda s: s.tensor_copy(cand_sb.t[:], psA.t[:, 0:8]),
               reads=[psA.b], writes=[cand_sb.b])

    dbg = consts.get("dbg")
    rnd_i = [0]

    def dbg_snap():
        if dbg is None:
            return
        bisbuf, _ = dbg
        i = rnd_i[0]
        rnd_i[0] += 1
        p.emit("vector", lambda s, i=i: s.tensor_copy(
            bisbuf.t[0:1, i * 48:(i + 1) * 48], rowT.t[0:1, 0:48]),
            reads=[rowT.b], writes=[bisbuf.b])

    p.emit("vector", lambda s: s.memset(rowT.t[0:1, 0:8], 2.0), writes=[rowT.b])
    for bbit in [32, 16, 8, 4, 2, 1]:
        p.emit("vector", lambda s, f=float(2.0 ** (-bbit)): s.tensor_scalar_mul(
            rowT.t[0:1, 16:24], rowT.t[0:1, 0:8], f), reads=[rowT.b], writes=[rowT.b])
        bcast_cand()
        count_ge(512)
        p.emit("vector", lambda s: s.tensor_single_scalar(
            rowT.t[0:1, 24:32], rowT.t[0:1, 32:40], CAP, ALU.is_lt),
            reads=[rowT.b], writes=[rowT.b])
        p.emit("vector", lambda s: s.tensor_tensor(
            rowT.t[0:1, 8:16], rowT.t[0:1, 16:24], rowT.t[0:1, 0:8], ALU.subtract),
            reads=[rowT.b], writes=[rowT.b])
        p.emit("vector", lambda s: s.tensor_tensor(
            rowT.t[0:1, 8:16], rowT.t[0:1, 8:16], rowT.t[0:1, 24:32], ALU.mult),
            reads=[rowT.b], writes=[rowT.b])
        p.emit("vector", lambda s: s.tensor_tensor(
            rowT.t[0:1, 0:8], rowT.t[0:1, 0:8], rowT.t[0:1, 8:16], ALU.add),
            reads=[rowT.b], writes=[rowT.b])
        dbg_snap()
    p.emit("vector", lambda s: s.tensor_scalar_mul(
        rowT.t[0:1, 0:8], rowT.t[0:1, 0:8], 0.5), reads=[rowT.b], writes=[rowT.b])
    p.emit("vector", lambda s: s.tensor_scalar_mul(
        rowT.t[0:1, 8:16], rowT.t[0:1, 0:8], 0.5), reads=[rowT.b], writes=[rowT.b])
    for kbit in range(23):
        p.emit("vector", lambda s: s.tensor_tensor(
            rowT.t[0:1, 16:24], rowT.t[0:1, 0:8], rowT.t[0:1, 8:16], ALU.add),
            reads=[rowT.b], writes=[rowT.b])
        bcast_cand()
        count_ge(512)
        p.emit("vector", lambda s: s.tensor_single_scalar(
            rowT.t[0:1, 24:32], rowT.t[0:1, 32:40], CAP, ALU.is_ge),
            reads=[rowT.b], writes=[rowT.b])
        p.emit("vector", lambda s: s.tensor_tensor(
            rowT.t[0:1, 48:56], rowT.t[0:1, 16:24], rowT.t[0:1, 0:8], ALU.subtract),
            reads=[rowT.b], writes=[rowT.b])
        p.emit("vector", lambda s: s.tensor_tensor(
            rowT.t[0:1, 48:56], rowT.t[0:1, 48:56], rowT.t[0:1, 24:32], ALU.mult),
            reads=[rowT.b], writes=[rowT.b])
        p.emit("vector", lambda s: s.tensor_tensor(
            rowT.t[0:1, 0:8], rowT.t[0:1, 0:8], rowT.t[0:1, 48:56], ALU.add),
            reads=[rowT.b], writes=[rowT.b])
        p.emit("vector", lambda s: s.tensor_scalar_mul(
            rowT.t[0:1, 8:16], rowT.t[0:1, 8:16], 0.5), reads=[rowT.b], writes=[rowT.b])
        dbg_snap()
    p.emit("vector", lambda s: s.tensor_copy(rowT.t[0:1, 16:24], rowT.t[0:1, 0:8]),
           reads=[rowT.b], writes=[rowT.b])
    bcast_cand()
    for e in range(E):
        p.emit("vector", lambda s, e=e: s.tensor_scalar(
            u2.t[:, e:1024:8], u1.t[:, e:1024:8], cand_sb.t[:, e:e + 1],
            None, ALU.is_gt),
            reads=[u1.b, cand_sb.b], writes=[u2.b])
        p.emit("vector", lambda s, e=e: s.tensor_scalar(
            u3.t[:, e:1024:8], u1.t[:, e:1024:8], cand_sb.t[:, e:e + 1],
            None, ALU.is_equal),
            reads=[u1.b, cand_sb.b], writes=[u3.b])
    for e in range(E):
        p.emit("vector", lambda s, e=e: s.tensor_reduce(
            consts["tot"].t[:, e:e + 1], u2.t[:, e:1024:8], AX.X, ALU.add),
            reads=[u2.b], writes=[consts["tot"].b])
    p.emit("tensor", lambda s: s.matmul(
        psB.t[0:1, 256:264], ones_col.t[:], consts["tot"].t[:, 0:8], start=True, stop=True),
        reads=[ones_col.b, consts["tot"].b], writes=[psB.b], tick=True)
    p.emit("vector", lambda s: s.tensor_scalar(
        rowT.t[0:1, 40:48], psB.t[0:1, 256:264], -1.0, CAP, ALU.mult, ALU.add),
        reads=[psB.b], writes=[rowT.b])
    p.emit("tensor", lambda s: s.matmul(
        psA.t[:, 8:16], consts["ones_f"].t[:], rowT.t[0:1, 40:48],
        start=True, stop=True),
        reads=[rowT.b, consts["ones_f"].b], writes=[psA.b], tick=True)
    p.emit("vector", lambda s: s.tensor_copy(quota_sb.t[:], psA.t[:, 8:16]),
           reads=[psA.b], writes=[quota_sb.b])
    for e in range(E):
        p.emit("vector", lambda s, e=e: s.tensor_tensor_scan(
            consts["pfx"].t[:, e:1024:8], u3.t[:, e:1024:8], consts["zeros"].t[:, 0:P],
            0.0, ALU.add, ALU.add),
            reads=[u3.b, consts["zeros"].b], writes=[consts["pfx"].b])
    p.emit("vector", lambda s: s.tensor_copy(
        consts["tot"].t[:, 0:8], consts["pfx"].t[:, 127 * 8:128 * 8]),
        reads=[consts["pfx"].b], writes=[consts["tot"].b])
    p.emit("tensor", lambda s: s.matmul(
        psB.t[:, 384:392], ltmask.t[:], consts["tot"].t[:, 0:8], start=True, stop=True),
        reads=[ltmask.b, consts["tot"].b], writes=[psB.b], tick=True)
    p.emit("vector", lambda s: s.tensor_copy(off_sb.t[:], psB.t[:, 384:392]),
           reads=[psB.b], writes=[off_sb.b])
    p.emit("vector", lambda s: s.tensor_tensor(
        consts["pfx"].t[:], consts["pfx"].t[:], u3.t[:], ALU.subtract),
        reads=[consts["pfx"].b, u3.b], writes=[consts["pfx"].b])
    for e in range(E):
        p.emit("vector", lambda s, e=e: s.tensor_scalar(
            consts["pfx"].t[:, e:1024:8], consts["pfx"].t[:, e:1024:8],
            off_sb.t[:, e:e + 1], None, ALU.add),
            reads=[consts["pfx"].b, off_sb.b], writes=[consts["pfx"].b])
        p.emit("vector", lambda s, e=e: s.tensor_scalar(
            consts["pfx"].t[:, e:1024:8], consts["pfx"].t[:, e:1024:8],
            quota_sb.t[:, e:e + 1], None, ALU.is_lt),
            reads=[consts["pfx"].b, quota_sb.b], writes=[consts["pfx"].b])
    p.emit("vector", lambda s: s.tensor_tensor(
        u3.t[:], u3.t[:], consts["pfx"].t[:], ALU.mult),
        reads=[u3.b, consts["pfx"].b], writes=[u3.b])
    p.emit("vector", lambda s: s.tensor_tensor(u2.t[:], u2.t[:], u3.t[:], ALU.add),
           reads=[u2.b, u3.b], writes=[u2.b])
    p.emit("vector", lambda s: s.tensor_tensor(keepw.t[:], u1.t[:], u2.t[:], ALU.mult),
           reads=[u1.b, u2.b], writes=[keepw.b])

def decl_moe(c, npfx=""):
    EH = 2048
    return {
        "fc1bf": c.dram(npfx + "fc1bf", [8, 16, P, DC * P], BF16),
        "fc2bf": c.dram(npfx + "fc2bf", [8, DC, P, 16 * P], BF16),
        "sh1bf": c.dram(npfx + "sh1bf", [32, P, DC * P], BF16),
        "sh2bf": c.dram(npfx + "sh2bf", [DC, P, 32 * P], BF16),
        "z_d": c.dram(npfx + "z_rot", [P, P * 8], F32, kind="ExternalInput"),
        "eb_d": c.dram(npfx + "ebias_row", [1, 1024], F32, kind="ExternalInput"),
        "er_d": c.dram(npfx + "erow_row", [1, 1024], F32, kind="ExternalInput"),
        "lt_d": c.dram(npfx + "ltmask", [P, P], F32, kind="ExternalInput"),
        "fc1_d": c.dram(npfx + "fc1T", [8, D, EH], F32, kind="ExternalInput"),
        "fc1b_d": c.dram(npfx + "fc1b", [P, 8 * 16], F32, kind="ExternalInput"),
        "fc2_d": c.dram(npfx + "fc2T", [8, EH, D], F32, kind="ExternalInput"),
        "fc2b_d": c.dram(npfx + "fc2b_rows", [1, 8 * D], F32, kind="ExternalInput"),
        "sh1_d": c.dram(npfx + "sh1T", [D, 4096], F32, kind="ExternalInput"),
        "sh1b_d": c.dram(npfx + "sh1b", [P, 32], F32, kind="ExternalInput"),
        "sh2_d": c.dram(npfx + "sh2T", [4096, D], F32, kind="ExternalInput"),
        "sh2b_d": c.dram(npfx + "sh2b", [P, DC], F32, kind="ExternalInput"),
        "idf_d": c.dram(npfx + "idf", [P, P], F32, kind="ExternalInput"),
        "kw_i": c.dram(npfx + "kw_scratch", [16, P, 8], F32),
    }


def emit_moe_stage(c, d, x_d, xo_d, pfx="m_"):
    p = c.p
    EH = 2048
    z_d = d["z_d"]
    eb_d = d["eb_d"]
    er_d = d["er_d"]
    lt_d = d["lt_d"]
    fc1_d = d["fc1_d"]
    fc1b_d = d["fc1b_d"]
    fc2_d = d["fc2_d"]
    fc2b_d = d["fc2b_d"]
    sh1_d = d["sh1_d"]
    sh1b_d = d["sh1b_d"]
    sh2_d = d["sh2_d"]
    sh2b_d = d["sh2b_d"]
    idf_d = d["idf_d"]
    kw_i = d["kw_i"]
    fc1bf, fc2bf = d["fc1bf"], d["fc2bf"]
    sh1bf, sh2bf = d["sh1bf"], d["sh2bf"]
    old_pfx, old_ctx = c.prefix, c.ctx
    c.prefix = pfx
    with contextlib.ExitStack() as sctx:
        c.ctx = sctx
        ps_ab = c.psum("ps_ab", [P, 1024], F32)
        ps_row = c.psum("ps_row", [P, 512], F32)
        ps_bc = c.psum("ps_bc", [P, 512], F32)
        psRtA = c.psum("psRtA", [P, 512], F32)
        psRtB = c.psum("psRtB", [P, 512], F32)
        sc = make_ln_scratch(c, ps_share=dict(
            statm_ps=ps_row, stats_ps=ps_bc,
            bc_a=TB(ps_ab.t[:, 0:512], ps_ab.b), bc_b=TB(ps_ab.t[:, 512:1024], ps_ab.b)))
        xn = c.sbuf("xn", [P, DC * T], BF16)
        z_sb = c.sbuf("z_sb", [P, P * 8], F32)
        u1 = c.sbuf("u1", [P, 1024], F32)
        u2 = c.sbuf("u2", [P, 1024], F32)
        u3 = c.sbuf("u3", [P, 1024], F32)
        pfx = c.sbuf("pfx", [P, 1024], F32)
        keepw = c.sbuf("keepw", [P, 1024], F32)
        tot = c.sbuf("tot", [P, 8], F32)
        rowT = c.sbuf("rowT", [1, 64], F32)
        zeros = c.sbuf("zeros", [P, P], F32)
        ebias_bc = c.sbuf("ebias_bc", [P, 1024], F32)
        erow_bc = c.sbuf("erow_bc", [P, 1024], F32)
        ltm = c.sbuf("ltm", [P, P], F32)
        idf = c.sbuf("idf_sb", [P, P], F32)
        ones_col = c.sbuf("ones_col", [P, 1], F32)
        w_tm = c.sbuf("w_tm", [P, 16 * 8], F32)
        cand_sb = c.sbuf("cand_sb", [P, 8], F32)
        quota_sb = c.sbuf("quota_sb", [P, 8], F32)
        off_sb = c.sbuf("off_sb", [P, 8], F32)
        gbuf = c.sbuf("gbuf", [P, 32 * 512], BF16)
        y_fm = c.sbuf("y_fm", [P, DC * 512], F32)
        b2row = c.sbuf("b2row", [1, 8 * D], BF16)
        cols = c.sbuf("colsb", [P, 32 + DC + 8 * 16], F32)
        wb2 = [c.sbuf(f"wmb{i}", [P, 32 * P], BF16) for i in range(2)]
        t1 = c.sbuf("t1", [P, 512], F32)
        wrow = c.sbuf("wrow", [1, 512], F32)
        wrow_bf = c.sbuf("wrow_bf", [1, 512], BF16)
        xrt = [c.sbuf("xra", [P, 512], F32), c.sbuf("xrbb", [P, 512], F32)]
        xout = c.sbuf("xout", [P, 512], F32)
        psF = c.psum("psF", [P, 512], F32)
        psW = ps_bc
        psH = ps_row

        init_ln_consts(c, sc)
        ones_f = sc["ones_f32"]

        p.emit("sync", lambda s: s.dma_start(out=z_sb.t[:], in_=z_d.t),
               reads=[z_d.b], writes=[z_sb.b], dma=True)
        for dst, src_d in [(ltm, lt_d), (idf, idf_d)]:
            p.emit("sync", lambda s, dst=dst, src_d=src_d: s.dma_start(
                out=dst.t[:], in_=src_d.t), reads=[src_d.b], writes=[dst.b], dma=True)
        for dst, src_d in [(ebias_bc, eb_d), (erow_bc, er_d)]:
            p.emit("sync", lambda s, dst=dst, src_d=src_d: s.dma_start(
                out=dst.t[:], in_=src_d.t.partition_broadcast(P).squeeze(1)),
                reads=[src_d.b], writes=[dst.b], dma=True)
        p.emit("gpsimd", lambda s: s.dma_start(out=b2row.t[:], in_=fc2b_d.t),
               reads=[fc2b_d.b], writes=[b2row.b], dma=True)
        p.emit("sync", lambda s: s.dma_start(out=cols.t[:, 0:32], in_=sh1b_d.t),
               reads=[sh1b_d.b], writes=[cols.b], dma=True)
        p.emit("sync", lambda s: s.dma_start(out=cols.t[:, 32:32 + DC], in_=sh2b_d.t),
               reads=[sh2b_d.b], writes=[cols.b], dma=True)
        p.emit("sync", lambda s: s.dma_start(out=cols.t[:, 40:40 + 128], in_=fc1b_d.t),
               reads=[fc1b_d.b], writes=[cols.b], dma=True)
        p.emit("vector", lambda s: s.memset(zeros.t[:], 0.0), writes=[zeros.b])
        p.emit("vector", lambda s: s.memset(ones_col.t[:], 1.0), writes=[ones_col.b])

        emit_ln(c, sc, x_d, xn)

        def emit_shared(u):
                for fo in range(32):
                    wb = wb2[fo % 2]
                    if u == 0:
                        p.emit("gpsimd", lambda s, fo=fo, wb=wb: s.dma_start(
                            out=wb.t[:, 0:DC * P].rearrange("p (k f) -> p k f", f=P),
                            in_=sh1_d.t.rearrange("(k p) f -> p k f", p=P)[:, :, fo * P:(fo + 1) * P]),
                            reads=[sh1_d.b], writes=[wb.b], dma=True)
                        p.emit("sync", lambda s, fo=fo, wb=wb: s.dma_start(
                            out=sh1bf.t[fo], in_=wb.t[:, 0:DC * P]),
                            reads=[wb.b], writes=[sh1bf.b], dma=True)
                    else:
                        p.emit("gpsimd", lambda s, fo=fo, wb=wb: s.dma_start(
                            out=wb.t[:, 0:DC * P], in_=sh1bf.t[fo]),
                            reads=[sh1bf.b], writes=[wb.b], dma=True)
                    for k in range(DC):
                        p.emit("tensor", lambda s, fo=fo, u=u, k=k, wb=wb: s.matmul(
                            psH.t[:], wb.t[:, k * P:(k + 1) * P],
                            xn.t[:, k * T + u * 512: k * T + u * 512 + 512],
                            start=(k == 0), stop=(k == DC - 1)),
                            reads=[wb.b, xn.b], writes=[psH.b], tick=(k == DC - 1))
                    p.emit("scalar", lambda s, fo=fo: s.activation(
                        gbuf.t[:, fo * 512:(fo + 1) * 512], psH.t[:], AF.Gelu,
                        bias=cols.t[:, fo:fo + 1]),
                        reads=[psH.b, cols.b], writes=[gbuf.b])
                for fo in range(DC):
                    wb = wb2[fo % 2]
                    if u == 0:
                        p.emit("gpsimd", lambda s, fo=fo, wb=wb: s.dma_start(
                            out=wb.t[:].rearrange("p (k f) -> p k f", f=P),
                            in_=sh2_d.t.rearrange("(k p) f -> p k f", p=P)[:, :, fo * P:(fo + 1) * P]),
                            reads=[sh2_d.b], writes=[wb.b], dma=True)
                        p.emit("sync", lambda s, fo=fo, wb=wb: s.dma_start(
                            out=sh2bf.t[fo], in_=wb.t[:]),
                            reads=[wb.b], writes=[sh2bf.b], dma=True)
                    else:
                        p.emit("gpsimd", lambda s, fo=fo, wb=wb: s.dma_start(
                            out=wb.t[:], in_=sh2bf.t[fo]),
                            reads=[sh2bf.b], writes=[wb.b], dma=True)
                    for k in range(32):
                        p.emit("tensor", lambda s, fo=fo, k=k, wb=wb: s.matmul(
                            psF.t[:], wb.t[:, k * P:(k + 1) * P],
                            gbuf.t[:, k * 512:(k + 1) * 512],
                            start=(k == 0), stop=(k == 31)),
                            reads=[wb.b, gbuf.b], writes=[psF.b], tick=(k == 31))
                    p.emit("vector", lambda s, fo=fo: s.tensor_scalar_add(
                        y_fm.t[:, fo * 512:(fo + 1) * 512], psF.t[:],
                        cols.t[:, 32 + fo:32 + fo + 1]),
                        reads=[psF.b, cols.b], writes=[y_fm.b])

        emit_shared(0)

        consts = dict(dbg=None, u1=u1, u2=u2, u3=u3, rowT=rowT, psA=psRtA, psB=psRtB,
                      ebias_bc=ebias_bc, erow_bc=erow_bc, ltmask=ltm,
                      ones_col=ones_col, idf=idf, pfx=pfx, tot=tot, zeros=zeros,
                      ones_f=ones_f, cand_sb=cand_sb, quota_sb=quota_sb,
                      off_sb=off_sb)
        p.paranoid_now = True
        emit_moe_routing(c, TB(z_sb.t[:], z_sb.b), keepw, sc, consts)
        p.paranoid_now = False

        p.emit("sync", lambda s: s.dma_start(
            out=kw_i.t.rearrange("tile r e -> tile (r e)"), in_=keepw.t[0:16, :]),
            reads=[keepw.b], writes=[kw_i.b], dma=True)
        p.emit("sync", lambda s: s.dma_start(
            out=w_tm.t[:].rearrange("r (tile e) -> r tile e", e=8),
            in_=kw_i.t.rearrange("tile r e -> r tile e")),
            reads=[kw_i.b], writes=[w_tm.b], dma=True)

        for u in range(4):
            if u > 0:
                emit_shared(u)
            for e in range(8):
                for qq in range(4):
                    tile = u * 4 + qq
                    p.emit("tensor", lambda s, tile=tile, qq=qq, e=e: s.matmul(
                        psW.t[0:1, qq * P:(qq + 1) * P],
                        w_tm.t[:, tile * 8 + e:tile * 8 + e + 1], idf.t[:],
                        start=(qq == 0), stop=(qq == 3), skip_group_check=True),
                        reads=[w_tm.b, idf.b], writes=[psW.b], tick=(qq == 3))
                p.emit("vector", lambda s: s.tensor_copy(wrow.t[0:1, :], psW.t[0:1, 0:512]),
                       reads=[psW.b], writes=[wrow.b])
                p.emit("vector", lambda s: s.tensor_copy(wrow_bf.t[0:1, :], wrow.t[0:1, :]),
                       reads=[wrow.b], writes=[wrow_bf.b])
                p.emit("tensor", lambda s: s.matmul(
                    psW.t[:, 0:512], ones_f.t[:], wrow.t[0:1, :], start=True, stop=True),
                    reads=[ones_f.b, wrow.b], writes=[psW.b], tick=True)
                for fo in range(16):
                    wb = wb2[fo % 2]
                    if u == 0:
                        p.emit("gpsimd", lambda s, fo=fo, wb=wb, e=e: s.dma_start(
                            out=wb.t[:, 0:DC * P].rearrange("p (k f) -> p k f", f=P),
                            in_=fc1_d.t[e].rearrange("(k p) f -> p k f", p=P)[:, :, fo * P:(fo + 1) * P]),
                            reads=[fc1_d.b], writes=[wb.b], dma=True)
                        p.emit("sync", lambda s, fo=fo, wb=wb, e=e: s.dma_start(
                            out=fc1bf.t[e, fo], in_=wb.t[:, 0:DC * P]),
                            reads=[wb.b], writes=[fc1bf.b], dma=True)
                    else:
                        p.emit("gpsimd", lambda s, fo=fo, wb=wb, e=e: s.dma_start(
                            out=wb.t[:, 0:DC * P], in_=fc1bf.t[e, fo]),
                            reads=[fc1bf.b], writes=[wb.b], dma=True)
                    for k in range(DC):
                        p.emit("tensor", lambda s, fo=fo, u=u, k=k, wb=wb: s.matmul(
                            psH.t[:], wb.t[:, k * P:(k + 1) * P],
                            xn.t[:, k * T + u * 512: k * T + u * 512 + 512],
                            start=(k == 0), stop=(k == DC - 1)),
                            reads=[wb.b, xn.b], writes=[psH.b], tick=(k == DC - 1))
                    p.emit("scalar", lambda s, fo=fo, e=e: s.activation(
                        t1.t[:], psH.t[:], AF.Gelu,
                        bias=cols.t[:, 40 + e * 16 + fo:40 + e * 16 + fo + 1]),
                        reads=[psH.b, cols.b], writes=[t1.b])
                    p.emit("vector", lambda s, fo=fo: s.tensor_tensor(
                        gbuf.t[:, fo * 512:(fo + 1) * 512], t1.t[:], psW.t[:, 0:512],
                        ALU.mult),
                        reads=[t1.b, psW.b], writes=[gbuf.b])
                for fo in range(DC):
                    wb = wb2[fo % 2]
                    if u == 0:
                        p.emit("gpsimd", lambda s, fo=fo, wb=wb, e=e: s.dma_start(
                            out=wb.t[:, 0:16 * P].rearrange("p (k f) -> p k f", f=P),
                            in_=fc2_d.t[e].rearrange("(k p) f -> p k f", p=P)[:, :, fo * P:(fo + 1) * P]),
                            reads=[fc2_d.b], writes=[wb.b], dma=True)
                        p.emit("sync", lambda s, fo=fo, wb=wb, e=e: s.dma_start(
                            out=fc2bf.t[e, fo], in_=wb.t[:, 0:16 * P]),
                            reads=[wb.b], writes=[fc2bf.b], dma=True)
                    else:
                        p.emit("gpsimd", lambda s, fo=fo, wb=wb, e=e: s.dma_start(
                            out=wb.t[:, 0:16 * P], in_=fc2bf.t[e, fo]),
                            reads=[fc2bf.b], writes=[wb.b], dma=True)
                    for k in range(16):
                        p.emit("tensor", lambda s, fo=fo, k=k, wb=wb: s.matmul(
                            psF.t[:], wb.t[:, k * P:(k + 1) * P],
                            gbuf.t[:, k * 512:(k + 1) * 512],
                            start=(k == 0), stop=False),
                            reads=[wb.b, gbuf.b], writes=[psF.b])
                    p.emit("tensor", lambda s, fo=fo, e=e: s.matmul(
                        psF.t[:], b2row.t[0:1, e * D + fo * P: e * D + (fo + 1) * P],
                        wrow_bf.t[0:1, :],
                        start=False, stop=True),
                        reads=[b2row.b, wrow_bf.b], writes=[psF.b], tick=True)
                    p.emit("vector", lambda s, fo=fo: s.tensor_tensor(
                        y_fm.t[:, fo * 512:(fo + 1) * 512],
                        y_fm.t[:, fo * 512:(fo + 1) * 512], psF.t[:], ALU.add),
                        reads=[y_fm.b, psF.b], writes=[y_fm.b])
            for fo in range(DC):
                par = fo % 2
                xs = xrt[par].t[:, 0:512]
                p.emit("sync", lambda s, fo=fo, u=u, xs=xs: s.dma_start(
                    out=xs, in_=x_d.t[fo * P:(fo + 1) * P, u * 512:(u + 1) * 512]),
                    reads=[x_d.b], writes=[xrt[par].b], dma=True)
                p.emit("vector", lambda s, fo=fo, xs=xs, par=par: s.tensor_tensor(
                    xout.t[:], y_fm.t[:, fo * 512:(fo + 1) * 512], xs, ALU.add),
                    reads=[y_fm.b, xrt[par].b], writes=[xout.b])
                p.emit("sync", lambda s, fo=fo, u=u: s.dma_start(
                    out=xo_d.t[fo * P:(fo + 1) * P, u * 512:(u + 1) * 512], in_=xout.t[:]),
                    reads=[xout.b], writes=[xo_d.b], dma=True)
        p.emit("sync", lambda s: s.nop(), reads=[xo_d.b])
        p.barrier()
    c.prefix, c.ctx = old_pfx, old_ctx


def build_moe_prog():
    p = Prog("moe")
    nc = p.nc
    ctx = contextlib.ExitStack()
    c = Core(p, ctx)
    x_d = c.dram("x_fm", [D, T], F32, kind="ExternalInput")
    xo_d = c.dram("xo_fm", [D, T], F32, kind="ExternalOutput")
    d = decl_moe(c)
    with ctx:
        emit_moe_stage(c, d, x_d, xo_d, pfx="")
        p.emit("sync", lambda s: s.nop(), reads=[xo_d.b])
        p.build(ctx)
    return nc

def moe_prep_host_inputs(x_batch_fm, ins):
    g = lambda k: np.asarray(ins[k], np.float32)
    return dict(
        x_fm=np.ascontiguousarray(x_batch_fm),
        pwT=np.ascontiguousarray(g("L1_ffn_proj_w").T),
        pb_row=np.ascontiguousarray(g("L1_ffn_proj_b").reshape(1, 8)),
    )


def moe_host_inputs(x_batch_fm, z_full, core_id, ins):
    g = lambda k: np.asarray(ins[k], np.float32)
    zr = np.roll(z_full.reshape(P, P, 8), -16 * core_id, axis=0)
    p_true = (np.arange(P) + 16 * core_id) % P
    ltm = (p_true[:, None] < p_true[None, :]).astype(np.float32)
    fc1 = g("L1_ffn_fc1_w")
    fc2 = g("L1_ffn_fc2_w")
    fc1b = g("L1_ffn_fc1_b")
    fc2b = g("L1_ffn_fc2_b")
    return dict(
        x_fm=np.ascontiguousarray(x_batch_fm),
        z_rot=np.ascontiguousarray(zr.reshape(P, P * 8)),
        ebias_row=np.ascontiguousarray(np.tile(g("L1_ffn_expert_bias"), P).reshape(1, 1024)),
        erow_row=np.ascontiguousarray(np.tile(np.arange(8, dtype=np.float32), P).reshape(1, 1024)),
        ltmask=ltm,
        fc1T=np.ascontiguousarray(fc1.transpose(0, 2, 1)),
        fc1b=np.ascontiguousarray(fc1b.reshape(8, 16, P).transpose(2, 0, 1).reshape(P, 128)),
        fc2T=np.ascontiguousarray(fc2.transpose(0, 2, 1)),
        fc2b_rows=np.ascontiguousarray(fc2b.reshape(1, 8 * D)),
        sh1T=np.ascontiguousarray(g("L1_ffn_sh1_w").T),
        sh1b=np.ascontiguousarray(g("L1_ffn_sh1_b").reshape(32, P).T),
        sh2T=np.ascontiguousarray(g("L1_ffn_sh2_w").T),
        sh2b=np.ascontiguousarray(g("L1_ffn_sh2_b").reshape(DC, P).T),
        idf=np.eye(P, dtype=np.float32),
    )


_PROG_CACHE = {}


def _prog(name, builder):
    if name not in _PROG_CACHE:
        _PROG_CACHE[name] = builder()
    return _PROG_CACHE[name]


def kernel_multilaunch(**inputs):
    cores = list(range(NCORES))
    x = np.asarray(inputs["L0_attn_a_x"], np.float32)
    x_fm = [to_fm(x[b]) for b in range(NCORES)]

    def run(name, builder, host_fn):
        nc = _prog(name, builder)
        in_maps = [host_fn(b) for b in range(NCORES)]
        res = run_bass_kernel_spmd(nc, in_maps, core_ids=cores)
        return ([np.asarray(res.results[b]["xo_fm"]) for b in range(NCORES)],)

    x_fm = run("cosf", build_cosformer_prog,
               lambda b: cosformer_host_inputs(x_fm[b], inputs))[0]
    x_fm = run("ffn0", build_ffn0_prog,
               lambda b: ffn0_host_inputs(x_fm[b], inputs, "L0"))[0]
    x_fm = run("attnb", build_attn_b_prog,
               lambda b: attn_b_host_inputs(x_fm[b], inputs))[0]
    x_fm = run("ffn0", build_ffn0_prog,
               lambda b: ffn0_host_inputs(x_fm[b], inputs, "L0"))[0]
    x_fm = run("attna", build_attn_a_prog,
               lambda b: attn_a_host_inputs(x_fm[b], inputs))[0]

    def moe_pass(x_fm):
        nc1 = _prog("moeprep", build_moe_prep_prog)
        in1 = [moe_prep_host_inputs(x_fm[b], inputs) for b in range(NCORES)]
        r1 = run_bass_kernel_spmd(nc1, in1, core_ids=cores)
        z_full = np.concatenate([np.asarray(r1.results[b]["z_my"]) for b in range(NCORES)], axis=0)
        nc2 = _prog("moe", build_moe_prog)
        in2 = [moe_host_inputs(x_fm[b], z_full, b, inputs) for b in range(NCORES)]
        r2 = run_bass_kernel_spmd(nc2, in2, core_ids=cores)
        return [np.asarray(r2.results[b]["xo_fm"]) for b in range(NCORES)]

    x_fm = moe_pass(x_fm)
    x_fm = run("attnb1", build_attn_b1_prog,
               lambda b: attn_b1_host_inputs(x_fm[b], inputs))[0]
    x_fm = moe_pass(x_fm)

    return np.stack([from_fm(f) for f in x_fm], axis=0)


def build_L1_prog():
    p = Prog("L1")
    nc = p.nc
    ctx = contextlib.ExitStack()
    c = Core(p, ctx)
    x_d = c.dram("x_fm", [D, T], F32, kind="ExternalInput")
    xo_d = c.dram("xo_fm", [D, T], F32, kind="ExternalOutput")
    d1 = decl_cosformer(c, "s1_")
    d2 = decl_ffn0(c, "ff_")
    d3 = decl_attn_b(c, "s3_")
    d5 = decl_attn_a(c, "s5_")
    dz = decl_moe_prep(c, "zp_")
    xa = c.dram("xa_i", [D, T], F32)
    xb = c.dram("xb_i", [D, T], F32)
    with ctx:
        emit_cosformer_stage(c, d1, x_d, xa, pfx="s1_")
        emit_ffn0_stage(c, d2, xa, xb, pfx="s2_", first_use=True)
        emit_attn_b_stage(c, d3, xb, xa, pfx="s3_")
        emit_ffn0_stage(c, d2, xa, xb, pfx="s4_", first_use=False)
        emit_attn_a_stage(c, d5, xb, xo_d, pfx="s5_")
        emit_moe_prep_stage(c, dz, xo_d, pfx="zp_")
        p.emit("sync", lambda s: s.nop(), reads=[xo_d.b, dz["z_d"].b])
        p.build(ctx)
    return nc


def build_L2_prog():
    p = Prog("L2")
    nc = p.nc
    ctx = contextlib.ExitStack()
    c = Core(p, ctx)
    x_d = c.dram("x_fm", [D, T], F32, kind="ExternalInput")
    xo_d = c.dram("xo_fm", [D, T], F32, kind="ExternalOutput")
    d6 = decl_moe(c, "m6_")
    d7 = decl_attn_b1(c, "s7_")
    dz = decl_moe_prep(c, "zp_")
    xa = c.dram("xa_i", [D, T], F32)
    with ctx:
        emit_moe_stage(c, d6, x_d, xa, pfx="s6_")
        emit_attn_b1_stage(c, d7, xa, xo_d, pfx="s7_")
        emit_moe_prep_stage(c, dz, xo_d, pfx="zp_")
        p.emit("sync", lambda s: s.nop(), reads=[xo_d.b, dz["z_d"].b])
        p.build(ctx)
    return nc


def build_L3_prog():
    p = Prog("L3")
    nc = p.nc
    ctx = contextlib.ExitStack()
    c = Core(p, ctx)
    x_d = c.dram("x_fm", [D, T], F32, kind="ExternalInput")
    xo_d = c.dram("xo_fm", [D, T], F32, kind="ExternalOutput")
    d8 = decl_moe(c, "m8_")
    with ctx:
        emit_moe_stage(c, d8, x_d, xo_d, pfx="s8_")
        p.emit("sync", lambda s: s.nop(), reads=[xo_d.b])
        p.build(ctx)
    return nc


def _pref(dic, npfx):
    return {npfx + k: v for k, v in dic.items() if k != "x_fm"}


def kernel(**inputs):
    cores = list(range(NCORES))
    x = np.asarray(inputs["L0_attn_a_x"], np.float32)
    x_fm = [to_fm(x[b]) for b in range(NCORES)]

    nc1 = _prog("L1", build_L1_prog)
    in1 = []
    for b in range(NCORES):
        m = dict(x_fm=np.ascontiguousarray(x_fm[b]))
        m.update(_pref(cosformer_host_inputs(x_fm[b], inputs), "s1_"))
        m.update(_pref(ffn0_host_inputs(x_fm[b], inputs, "L0"), "ff_"))
        m.update(_pref(attn_b_host_inputs(x_fm[b], inputs), "s3_"))
        m.update(_pref(attn_a_host_inputs(x_fm[b], inputs), "s5_"))
        m.update(_pref(moe_prep_host_inputs(x_fm[b], inputs), "zp_"))
        in1.append(m)
    r1 = run_bass_kernel_spmd(nc1, in1, core_ids=cores)
    x5 = [np.asarray(r1.results[b]["xo_fm"]) for b in range(NCORES)]
    z6 = np.concatenate([np.asarray(r1.results[b]["zp_z_my"]) for b in range(NCORES)], 0)

    nc2 = _prog("L2", build_L2_prog)
    in2 = []
    for b in range(NCORES):
        m = dict(x_fm=np.ascontiguousarray(x5[b]))
        m.update(_pref(moe_host_inputs(x5[b], z6, b, inputs), "m6_"))
        m.update(_pref(attn_b1_host_inputs(x5[b], inputs), "s7_"))
        m.update(_pref(moe_prep_host_inputs(x5[b], inputs), "zp_"))
        in2.append(m)
    r2 = run_bass_kernel_spmd(nc2, in2, core_ids=cores)
    x7 = [np.asarray(r2.results[b]["xo_fm"]) for b in range(NCORES)]
    z8 = np.concatenate([np.asarray(r2.results[b]["zp_z_my"]) for b in range(NCORES)], 0)

    nc3 = _prog("L3", build_L3_prog)
    in3 = []
    for b in range(NCORES):
        m = dict(x_fm=np.ascontiguousarray(x7[b]))
        m.update(_pref(moe_host_inputs(x7[b], z8, b, inputs), "m8_"))
        in3.append(m)
    r3 = run_bass_kernel_spmd(nc3, in3, core_ids=cores)
    return np.stack([from_fm(np.asarray(r3.results[b]["xo_fm"])) for b in range(NCORES)], 0)
